# version 14
# baseline (speedup 1.0000x reference)
from contextlib import ExitStack

import numpy as np
import ml_dtypes

import concourse.bass as bass
import concourse.mybir as mybir
from concourse.bass_utils import run_bass_kernel_spmd

F32 = mybir.dt.float32
BF16 = mybir.dt.bfloat16
AF = mybir.ActivationFunctionType
OP = mybir.AluOpType
AX = mybir.AxisListType

S, D, H, DN, DR, DQ, DKV, HD = 2048, 2048, 16, 128, 64, 512, 512, 5632
NT, KC, HC = S // 128, D // 128, HD // 128
SG, NSG = 512, 4
NEG_MASK = -30000.0
N_CORES = 8

GAM = ([(0, r, h * 512, 512) for r in range(44) for h in range(4)]
       + [(1, r, h * 512, 512) for r in range(44) for h in range(4)]
       + [(2, r, h * 512, 512) for r in range(16) for h in range(11)])
NGAM = len(GAM)
GAM_OFFS = [0, 176, 352, NGAM]

DEBUG_TAPS = ()
_CACHE = {}


def _sched_c():
    gam = 0
    for g in range(4):
        yield ("wload", g)
        for nc2 in range(4):
            yield ("pack", g, nc2)
        for hl in range(4):
            idx = 4 * g + hl
            yield ("qasm", idx)
            yield ("proj", idx)
            for qi in range(NT):
                yield ("attn", idx, qi)
                for _ in range(2):
                    if gam < NGAM:
                        yield ("gam", gam)
                        gam += 1
            for _ in range(1):
                if gam < NGAM:
                    yield ("gam", gam)
                    gam += 1
    while gam < NGAM:
        yield ("gam", gam)
        gam += 1


def _tp_batches(qi):
    kts = qi + 1
    out, b = [], 0
    while b < kts:
        out.append((b, min(8, kts - b)))
        b += 8
    return out


class SemC:

    def __init__(self, h):
        self.h = h
        self.base = 0

    def w(self, e, v):
        e.wait_ge(self.h, self.base + v)

    def i(self, ins, n=1):
        return ins.then_inc(self.h, n)

    def adv(self, total):
        self.base += total


class DSem:

    def __init__(self, nc, name, nway=2):
        self.n = nway
        self.h = [nc.alloc_semaphore(name=name + str(i)) for i in range(nway)]
        self.base = [0] * nway

    def i(self, ins, k):
        return ins.then_inc(self.h[k % self.n], 16)

    def w(self, e, k):
        e.wait_ge(self.h[k % self.n],
                  self.base[k % self.n] + 16 * (k // self.n + 1))

    def adv(self, n_total):
        for p in range(self.n):
            self.base[p] += 16 * ((n_total + self.n - 1 - p) // self.n)


def _rope(e, Ein, Oin, Eout, Oout, tabs, t0, t1, t2, t3):
    ch, cl, sh, sl = tabs
    e.tensor_tensor(t0, Ein, ch, op=OP.mult)
    e.tensor_tensor(t1, Oin, sh, op=OP.mult)
    e.tensor_tensor(t2, Ein, cl, op=OP.mult)
    e.tensor_tensor(t3, Oin, sl, op=OP.mult)
    e.drain()
    e.tensor_tensor(t0, t0, t1, op=OP.subtract)
    e.tensor_tensor(t2, t2, t3, op=OP.subtract)
    e.drain()
    e.tensor_tensor(Eout, t0, t2, op=OP.add)
    e.drain()
    e.tensor_tensor(t0, Oin, ch, op=OP.mult)
    e.tensor_tensor(t1, Ein, sh, op=OP.mult)
    e.tensor_tensor(t2, Oin, cl, op=OP.mult)
    e.tensor_tensor(t3, Ein, sl, op=OP.mult)
    e.drain()
    e.tensor_tensor(t0, t0, t1, op=OP.add)
    e.tensor_tensor(t2, t2, t3, op=OP.add)
    e.drain()
    e.tensor_tensor(Oout, t0, t2, op=OP.add)


def build(debug_taps=()):
    nc = bass.Bass("TRN2", target_bir_lowering=False, debug=False)

    x_hbm = nc.dram_tensor("x", [S, D], F32, kind="ExternalInput")
    wdq_hbm = nc.dram_tensor("w_dq", [D, DQ], F32, kind="ExternalInput")
    gq_hbm = nc.dram_tensor("g_q", [1, DQ], F32, kind="ExternalInput")
    wuq_hbm = nc.dram_tensor("w_uq_p", [DQ, H * 192], F32, kind="ExternalInput")
    wdkv_hbm = nc.dram_tensor("w_dkv", [D, DKV], F32, kind="ExternalInput")
    gkv_hbm = nc.dram_tensor("g_kv", [1, DKV], F32, kind="ExternalInput")
    wukr_hbm = nc.dram_tensor("w_ukr_p", [D, DR], F32, kind="ExternalInput")
    wuv_hbm = nc.dram_tensor("w_uv", [DKV, H * DN], F32, kind="ExternalInput")
    wukn_hbm = nc.dram_tensor("w_ukn", [DKV, H * DN], F32, kind="ExternalInput")
    wo_hbm = nc.dram_tensor("w_o", [D, D], F32, kind="ExternalInput")
    w_hbm = [nc.dram_tensor("w1", [HD, D], F32, kind="ExternalInput"),
             nc.dram_tensor("w2", [HD, D], F32, kind="ExternalInput"),
             nc.dram_tensor("w3", [D, HD], F32, kind="ExternalInput")]
    idn_hbm = nc.dram_tensor("idn", [128, 128], BF16, kind="ExternalInput")
    mask_hbm = nc.dram_tensor("maskd", [128, 128], BF16, kind="ExternalInput")
    qtab_hbm = [nc.dram_tensor(n, [128, S], F32, kind="ExternalInput")
                for n in ("q_ch", "q_cl", "q_sh", "q_sl")]

    y_hbm = nc.dram_tensor("y", [S, D], F32, kind="ExternalOutput")
    xnew_hbm = nc.dram_tensor("x_new", [S, D], F32, kind="Internal")
    wt_hbm = [nc.dram_tensor(n, [HC, 128, D], BF16, kind="Internal")
              for n in ("w1qT", "w2qT", "w3qT")]

    taps = {}
    for name, shape, dt in (("ln1", [128, NT * 2048], BF16),
                            ("qb", [128, 4 * 2048], BF16),
                            ("kv", [128, 4 * 2048], BF16),
                            ("kr", [64, 2048], BF16),
                            ("otm", [128, NT * 2048], BF16),
                            ("ln2", [128, NT * 2048], BF16),
                            ("h0", [128, HC * 512], BF16)):
        if name in debug_taps:
            taps[name] = nc.dram_tensor("tap_" + name, shape, dt,
                                        kind="ExternalOutput")

    def sem(name):
        return nc.alloc_semaphore(name=name)

    def tap_block(name, src_ap):
        if name not in taps:
            return
        st = sem("s_tap_" + name)
        with nc.Block("tap_" + name) as blk:
            @blk.sync
            def _(e):
                e.dma_start(taps[name].ap(), src_ap).then_inc(st, 16)
                e.wait_ge(st, 16)

    BB = nc.alloc_sbuf_tensor("BB", [128, NT * 2048], BF16)
    BB3 = BB[:, :].rearrange("p (a b) -> p a b", b=2048)
    idn_sb = nc.alloc_sbuf_tensor("idn_sb", [128, 128], BF16)
    mask_sb = nc.alloc_sbuf_tensor("mask_sb", [128, 128], BF16)
    dump = nc.alloc_sbuf_tensor("dump", [128, 2048], F32)
    stat = nc.alloc_sbuf_tensor("stat", [128, 48], F32)
    onesr = nc.alloc_sbuf_tensor("onesr", [1, 128], F32)
    onesc = nc.alloc_sbuf_tensor("onesc", [128, 1], F32)
    scc = nc.alloc_sbuf_tensor("scc", [128, 4], F32)
    tcol = nc.alloc_sbuf_tensor("tcol", [128, 8], F32)
    tsc = nc.alloc_sbuf_tensor("tsc", [1, 16], F32)
    gpart = nc.alloc_sbuf_tensor("gpart", [128, NGAM], F32)
    lcol = nc.alloc_sbuf_tensor("lcol", [128, 8], F32)
    cst = nc.alloc_sbuf_tensor("cst", [128, 4], F32)

    s_c, s_i = sem("s_c"), sem("s_i")
    with nc.Block("init") as blk:
        @blk.sync
        def _(e):
            e.dma_start(idn_sb[:, :], idn_hbm.ap()).then_inc(s_c, 16)
            e.dma_start(mask_sb[:, :], mask_hbm.ap()).then_inc(s_c, 16)
            e.wait_ge(s_c, 32)

        @blk.vector
        def _(e):
            e.memset(onesr[:, :], 1.0)
            e.memset(onesc[:, :], 1.0)
            e.memset(cst[:, 0:1], 1e-5)
            e.memset(cst[:, 1:2], 1e-6)
            e.memset(scc[:, 1:2], 192.0).then_inc(s_i, 1)
            e.wait_ge(s_i, 2)
            e.reciprocal(scc[:, 0:1], scc[:, 2:3])

        @blk.scalar
        def _(e):
            e.wait_ge(s_i, 1)
            e.activation(scc[:, 2:3], scc[:, 1:2], AF.Sqrt).then_inc(s_i, 1)

    def emit_ln(tag, src_fn, extra_gpsimd=None):
        s_x = DSem(nc, tag + "_x")
        s_sq = sem(tag + "_sq")
        s_va = sem(tag + "_va")
        s_sa = sem(tag + "_sa")
        s_vb = sem(tag + "_vb")
        s_ap = sem(tag + "_ap")
        s_ta = sem(tag + "_ta")
        s_ea = sem(tag + "_ea")
        with ExitStack() as _lnctx:
            xs = [_lnctx.enter_context(nc.sbuf_tensor(f"{tag}_xs{i}", [128, 2048], F32)) for i in range(2)]
            lt = [_lnctx.enter_context(nc.sbuf_tensor(f"{tag}_lt{i}", [128, 2048], BF16)) for i in range(2)]
            pt = [_lnctx.enter_context(nc.psum_tensor(f"{tag}_pt{i}", [128, 1024], BF16)) for i in range(2)]
            with nc.Block(tag) as blk:
                @blk.sync
                def _(e):
                    for ti in range(NT):
                        if ti >= 2:
                            e.wait_ge(s_ap, ti - 1)
                        s_x.i(e.dma_start(xs[ti % 2][:, :], src_fn(ti)), ti)

                @blk.vector
                def _(e):
                    def evac(tj):
                        for hf in range(2):
                            e.wait_ge(s_ta, 2 * tj + hf + 1)
                            dst = BB3[:, hf * 8:(hf + 1) * 8,
                                      tj * 128:(tj + 1) * 128]
                            src = pt[hf][:, :].rearrange(
                                "p (a b) -> p a b", b=128)
                            e.tensor_copy(dst, src).then_inc(s_ea, 1)
                    for ti in range(NT):
                        xb = xs[ti % 2]
                        s_x.w(e, ti)
                        e.reduce_sum(stat[:, 1:2], xb[:, :], axis=AX.X)
                        e.wait_ge(s_sq, ti + 1)
                        e.drain()
                        e.tensor_scalar(stat[:, 2:3], stat[:, 1:2],
                                        1.0 / 2048.0, None, op0=OP.mult)
                        e.tensor_scalar(stat[:, 3:4], stat[:, 0:1],
                                        1.0 / 2048.0, None, op0=OP.mult)
                        e.drain()
                        e.scalar_tensor_tensor(stat[:, 5:6], stat[:, 2:3],
                                               stat[:, 2:3], stat[:, 3:4],
                                               op0=OP.mult,
                                               op1=OP.subtract
                                               ).then_inc(s_va, 1)
                        e.wait_ge(s_sa, ti + 1)
                        e.reciprocal(stat[:, 7:8], stat[:, 6:7])
                        e.drain()
                        e.tensor_scalar(stat[:, 8:9], stat[:, 2:3],
                                        stat[:, 7:8], -1.0, op0=OP.mult,
                                        op1=OP.mult).then_inc(s_vb, 1)
                        if ti > 0:
                            evac(ti - 1)
                    evac(NT - 1)

                @blk.scalar
                def _(e):
                    for ti in range(NT):
                        s_x.w(e, ti)
                        e.drain()
                        e.activation(dump[:, :], xs[ti % 2][:, :], AF.Square,
                                     accum_out=stat[:, 0:1]).then_inc(s_sq, 1)
                        e.wait_ge(s_va, ti + 1)
                        e.activation(stat[:, 6:7], stat[:, 5:6], AF.Sqrt,
                                     scale=-1.0,
                                     bias=cst[:, 0:1]).then_inc(s_sa, 1)
                        e.wait_ge(s_vb, ti + 1)
                        if ti >= 2:
                            e.wait_ge(s_ta, 2 * (ti - 2) + 2)
                        e.activation(lt[ti % 2][:, :], xs[ti % 2][:, :],
                                     AF.Identity, bias=stat[:, 8:9],
                                     scale=stat[:, 7:8]).then_inc(s_ap, 1)

                @blk.tensor
                def _(e):
                    for ti in range(NT):
                        e.wait_ge(s_ap, ti + 1)
                        for hf in range(2):
                            if ti > 0:
                                e.wait_ge(s_ea, 2 * (ti - 1) + hf + 1)
                            for k in range(8):
                                kc = hf * 8 + k
                                ins = e.transpose(
                                    pt[hf][:, k * 128:(k + 1) * 128],
                                    lt[ti % 2][:, kc * 128:(kc + 1) * 128],
                                    idn_sb[:, :])
                                if k == 7:
                                    ins.then_inc(s_ta, 1)

                if extra_gpsimd is not None:
                    blk.gpsimd(extra_gpsimd)

    with ExitStack() as _bcctx:
        QBt = _bcctx.enter_context(nc.sbuf_tensor("QB", [128, 4 * 2048], BF16))
        KVt = _bcctx.enter_context(nc.sbuf_tensor("KVt", [128, 4 * 2048], BF16))
        KR = _bcctx.enter_context(nc.sbuf_tensor("KR", [64, 2048], BF16))
        qtab = [_bcctx.enter_context(nc.sbuf_tensor(f"qt{i}", [128, S], F32)) for i in range(4)]
        QB3 = QBt[:, :].rearrange("p (a b) -> p a b", b=2048)
        KV3 = KVt[:, :].rearrange("p (a b) -> p a b", b=2048)

        s_wB = sem("s_wB")
        with ExitStack() as _wdctx:
            wdq_sb = _wdctx.enter_context(nc.sbuf_tensor("wdq_sb", [128, KC * 512], BF16))
            wdkv_sb = _wdctx.enter_context(nc.sbuf_tensor("wdkv_sb", [128, KC * 512], BF16))
            wukr_sb = _wdctx.enter_context(nc.sbuf_tensor("wukr_sb", [128, KC * 64], BF16))

            def pa_gpsimd(e):
                e.dma_start(
                    wdq_sb[:, :].rearrange("p (k j) -> p k j", j=512),
                    wdq_hbm.ap().rearrange("(k p) j -> p k j", p=128)
                ).then_inc(s_wB, 16)
                e.dma_start(
                    wdkv_sb[:, :].rearrange("p (k j) -> p k j", j=512),
                    wdkv_hbm.ap().rearrange("(k p) j -> p k j", p=128)
                ).then_inc(s_wB, 16)
                e.dma_start(
                    wukr_sb[:, :].rearrange("p (k j) -> p k j", j=64),
                    wukr_hbm.ap().rearrange("(k p) j -> p k j", p=128)
                ).then_inc(s_wB, 16)

            emit_ln("ln1", lambda ti: x_hbm.ap()[ti * 128:(ti + 1) * 128, :],
                    extra_gpsimd=pa_gpsimd)
            tap_block("ln1", BB[:, :])

            s_gd = sem("b_gd"); s_gr = sem("b_gr"); s_gv = sem("b_gv")
            s_mq = sem("b_mq"); s_mk = sem("b_mk")
            s_q1 = sem("b_q1"); s_k1 = sem("b_k1")
            s_vq = sem("b_vq"); s_vk = sem("b_vk")
            s_tq = sem("b_tq"); s_tk = sem("b_tk")
            s_eb = sem("b_eb")
            s_mr = sem("b_mr"); s_er = sem("b_er"); s_kr = sem("b_kr")

            with ExitStack() as _pbctx:
                def _sbA(n, sh, dt):
                    return _pbctx.enter_context(nc.sbuf_tensor(n, sh, dt))
                def _psA(n, sh, dt=F32):
                    return _pbctx.enter_context(nc.psum_tensor(n, sh, dt))
                ktab = [t[0:32, :] for t in qtab]
                grow = _sbA("grow", [1, DQ + DKV], F32)
                gq_rep = _sbA("gq_rep", [128, DQ], F32)
                gkv_rep = _sbA("gkv_rep", [128, DKV], F32)
                tmq = _sbA("tmq", [128, 512], F32)
                qtm0 = _sbA("qtm0", [128, 512], BF16)
                qtm1 = _sbA("qtm1", [128, 512], BF16)
                ktm0 = _sbA("ktm0", [128, 512], BF16)
                ktm1 = _sbA("ktm1", [128, 512], BF16)
                ekr = _sbA("ekr", [32, 2048], BF16)
                okr = _sbA("okr", [32, 2048], BF16)
                rt0 = _sbA("rt0", [32, 512], F32)
                rt1 = _sbA("rt1", [32, 512], F32)
                rt2 = _sbA("rt2", [32, 512], F32)
                rt3 = _sbA("rt3", [32, 512], F32)
                pcq0 = _psA("ps_cq0", [128, 512])
                pcq1 = _psA("ps_cq1", [128, 512])
                pck0 = _psA("ps_ck0", [128, 512])
                pck1 = _psA("ps_ck1", [128, 512])
                ptb0 = _psA("ps_tb0", [128, 512], BF16)
                ptb1 = _psA("ps_tb1", [128, 512], BF16)
                pkE = _psA("ps_kE", [32, 512])
                pkO = _psA("ps_kO", [32, 512])
                pcq, pck = [pcq0, pcq1], [pck0, pck1]
                ptb = [ptb0, ptb1]
                qtm, ktm = [qtm0, qtm1], [ktm0, ktm1]

                with nc.Block("pb") as blk:
                    @blk.sync
                    def _(e):
                        e.dma_start(grow[0:1, 0:DQ], gq_hbm.ap()).then_inc(
                            s_gd, 16)
                        e.dma_start(grow[0:1, DQ:DQ + DKV],
                                    gkv_hbm.ap()).then_inc(s_gd, 16)
                        for t, hbm in zip(qtab, qtab_hbm):
                            e.dma_start(t[:, :], hbm.ap()).then_inc(s_gd, 16)

                    @blk.tensor
                    def _(e):
                        def pb_transposes(tj):
                            for kind in range(2):
                                src = (qtm if kind == 0 else ktm)[tj % 2]
                                sv = s_vq if kind == 0 else s_vk
                                st = s_tq if kind == 0 else s_tk
                                e.wait_ge(sv, tj + 1)
                                if tj > 0:
                                    e.wait_ge(s_eb, 2 * (tj - 1) + kind + 1)
                                for c in range(4):
                                    ins = e.transpose(
                                        ptb[kind][:, c * 128:(c + 1) * 128],
                                        src[:, c * 128:(c + 1) * 128],
                                        idn_sb[:, :])
                                ins.then_inc(st, 1)

                        e.wait_ge(s_gd, 96)
                        e.matmul(pcq[0][:, 0:512], onesr[:, :],
                                 grow[0:1, 0:512], start=True, stop=True)
                        e.matmul(pck[0][:, 0:512], onesr[:, :],
                                 grow[0:1, 512:1024], start=True,
                                 stop=True).then_inc(s_gr, 1)
                        e.wait_ge(s_wB, 48)
                        e.wait_ge(s_gv, 1)
                        for ti in range(NT):
                            for kind in range(2):
                                ps = (pcq if kind == 0 else pck)[ti % 2]
                                w = wdq_sb if kind == 0 else wdkv_sb
                                sfree = s_vq if kind == 0 else s_vk
                                sdone = s_mq if kind == 0 else s_mk
                                if ti >= 2:
                                    e.wait_ge(sfree, ti - 1)
                                for kc in range(KC):
                                    ins = e.matmul(
                                        ps[:, 0:512],
                                        BB[:, kc * 2048 + ti * 128:
                                           kc * 2048 + (ti + 1) * 128],
                                        w[:, kc * 512:(kc + 1) * 512],
                                        start=(kc == 0), stop=(kc == KC - 1))
                                ins.then_inc(sdone, 1)
                            if ti > 0:
                                pb_transposes(ti - 1)
                        pb_transposes(NT - 1)
                        for n2 in range(4):
                            if n2 >= 1:
                                e.wait_ge(s_er, n2)
                            for kc in range(KC):
                                e.matmul(pkE[:, 0:512],
                                         wukr_sb[:, kc * 64:kc * 64 + 32],
                                         BB[:, kc * 2048 + n2 * 512:
                                            kc * 2048 + (n2 + 1) * 512],
                                         start=(kc == 0), stop=(kc == KC - 1))
                            for kc in range(KC):
                                ins = e.matmul(
                                    pkO[:, 0:512],
                                    wukr_sb[:, kc * 64 + 32:kc * 64 + 64],
                                    BB[:, kc * 2048 + n2 * 512:
                                       kc * 2048 + (n2 + 1) * 512],
                                    start=(kc == 0), stop=(kc == KC - 1))
                            ins.then_inc(s_mr, 1)

                    @blk.scalar
                    def _(e):
                        for ti in range(NT):
                            for kind in range(2):
                                ps = (pcq if kind == 0 else pck)[ti % 2]
                                sm = s_mq if kind == 0 else s_mk
                                so = s_q1 if kind == 0 else s_k1
                                acc = 16 + kind * 3 + ti % 2
                                sq = 22 + kind * 2 + ti % 2
                                e.wait_ge(sm, ti + 1)
                                e.activation(dump[:, 0:512], ps[:, 0:512],
                                             AF.Square,
                                             accum_out=stat[:, acc:acc + 1])
                                e.drain()
                                e.activation(stat[:, sq:sq + 1],
                                             stat[:, acc:acc + 1], AF.Sqrt,
                                             scale=1.0 / 512.0,
                                             bias=cst[:, 1:2]).then_inc(so, 1)

                    @blk.vector
                    def _(e):
                        def pb_evacs(tj):
                            for kind in range(2):
                                st = s_tq if kind == 0 else s_tk
                                dst3 = QB3 if kind == 0 else KV3
                                e.wait_ge(st, tj + 1)
                                e.tensor_copy(
                                    dst3[:, 0:4, tj * 128:(tj + 1) * 128],
                                    ptb[kind][:, :].rearrange(
                                        "p (a b) -> p a b", b=128)
                                ).then_inc(s_eb, 1)

                        e.wait_ge(s_gr, 1)
                        e.tensor_copy(gq_rep[:, :], pcq[0][:, 0:512])
                        e.tensor_copy(gkv_rep[:, :], pck[0][:, 0:512])
                        e.drain().then_inc(s_gv, 1)
                        for ti in range(NT):
                            for kind in range(2):
                                ps = (pcq if kind == 0 else pck)[ti % 2]
                                so = s_q1 if kind == 0 else s_k1
                                sv = s_vq if kind == 0 else s_vk
                                grep = gq_rep if kind == 0 else gkv_rep
                                dst = (qtm if kind == 0 else ktm)[ti % 2]
                                sq = 22 + kind * 2 + ti % 2
                                rc_ = 26 + kind * 2 + ti % 2
                                e.wait_ge(so, ti + 1)
                                e.reciprocal(stat[:, rc_:rc_ + 1],
                                             stat[:, sq:sq + 1])
                                e.drain()
                                e.tensor_scalar(tmq[:, :], ps[:, 0:512],
                                                stat[:, rc_:rc_ + 1], None,
                                                op0=OP.mult)
                                e.drain()
                                e.tensor_tensor(dst[:, :], tmq[:, :],
                                                grep[:, :],
                                                op=OP.mult).then_inc(sv, 1)
                            if ti > 0:
                                pb_evacs(ti - 1)
                        pb_evacs(NT - 1)
                        e.wait_ge(s_gd, 96)
                        for n2 in range(4):
                            e.wait_ge(s_mr, n2 + 1)
                            cs = slice(n2 * 512, (n2 + 1) * 512)
                            _rope(e, pkE[:, 0:512], pkO[:, 0:512],
                                  ekr[:, cs], okr[:, cs],
                                  [t[0:32, cs] for t in qtab],
                                  rt0[:, :], rt1[:, :], rt2[:, :], rt3[:, :])
                            e.drain()
                            e.tensor_scalar(ekr[:, cs], ekr[:, cs],
                                            scc[0:32, 2:3], None, op0=OP.mult)
                            e.tensor_scalar(okr[:, cs], okr[:, cs],
                                            scc[0:32, 2:3], None, op0=OP.mult)
                            e.drain().then_inc(s_er, 1)

                    @blk.gpsimd
                    def _(e):
                        e.wait_ge(s_er, 4)
                        e.dma_start(KR[0:32, :], ekr[:, :]).then_inc(s_kr, 16)
                        e.dma_start(KR[32:64, :], okr[:, :]).then_inc(s_kr, 16)
                        e.wait_ge(s_kr, 32)

            tap_block("qb", QBt[:, :])
            tap_block("kv", KVt[:, :])
            tap_block("kr", KR[:, :])

        s_gw = sem("c_gw"); s_qa = sem("c_qa")
        s_pk = sem("c_pk"); s_pr = sem("c_pr")
        s_pj = sem("c_pj"); s_pe = sem("c_pe")
        s_sc = sem("c_sc"); s_hd = sem("c_hd")
        s_mx = sem("c_mx"); s_ex = sem("c_ex")
        s_tp = sem("c_tp"); s_pte = sem("c_pte")
        s_pv = sem("c_pv"); s_oe = sem("c_oe")
        s_gdd = DSem(nc, "c_gdd", nway=4)
        s_gcc2 = [sem("c_gcc0"), sem("c_gcc1")]
        sched = list(_sched_c())

        def counters():
            return dict(pk=0, pr=0, pj=0, pe=0, sc=0, hd=0, mx=0, ex=0, tp=0,
                        pte=0, pv=0, oe=0, qa=0, gd=0, gc=0)

        with ExitStack() as _pcctx:
            def _sbC(n, sh, dt):
                return _pcctx.enter_context(nc.sbuf_tensor(n, sh, dt))
            def _psC(n, sh, dt=F32):
                return _pcctx.enter_context(nc.psum_tensor(n, sh, dt))
            wuq_g = _sbC("wuq_g", [128, 4 * 768], BF16)
            wuk_g = _sbC("wuk_g", [128, 4 * 512], BF16)
            wuv_g = _sbC("wuv_g", [128, 4 * 512], BF16)
            qn_sb = _sbC("qn_sb", [128, 2048], BF16)
            qr_sb = _sbC("qr_sb", [64, 2048], BF16)
            kn_sb = _sbC("kn_sb", [128, 2048], BF16)
            v_sb = _sbC("v_sb", [128, 2048], BF16)
            epr = _sbC("epr", [128, 2048], BF16)
            opr = _sbC("opr", [128, 2048], BF16)
            P_sb2 = [_sbC(f"P_sb{i}", [128, 2048], BF16)
                     for i in range(2)]
            PT_sb = _sbC("PT_sb", [128, 2048], BF16)
            rc = [_sbC(f"rc{i}", [128, 512], F32) for i in range(4)]
            gb = [_sbC(f"gb{i}", [128, 512], F32) for i in range(2)]
            gb = gb + [dump[:, 0:512], dump[:, 512:1024]]
            ps_pa = _psC("ps_pa", [128, 512])
            ps_pb = _psC("ps_pb", [128, 512])
            ps_sc = _psC("ps_sc", [128, 2048])
            ps_pt = _psC("ps_pt", [128, 1024], BF16)
            ps_o = _psC("ps_o", [128, 512])

            with nc.Block("pc") as blk:
                @blk.gpsimd
                def _(e):
                    n = counters()
                    for step in sched:
                        kind = step[0]
                        if kind == "wload":
                            g = step[1]
                            if g > 0:
                                e.wait_ge(s_sc, 32 * 4 * g)
                            e.dma_start(
                                wuq_g[:, :].rearrange("p (k j) -> p k j",
                                                      j=768),
                                wuq_hbm.ap()[:, g * 768:(g + 1) * 768]
                                .rearrange("(k p) j -> p k j", p=128)
                            ).then_inc(s_gw, 16)
                            e.dma_start(
                                wuk_g[:, :].rearrange("p (k j) -> p k j",
                                                      j=512),
                                wukn_hbm.ap()[:, g * 512:(g + 1) * 512]
                                .rearrange("(k p) j -> p k j", p=128)
                            ).then_inc(s_gw, 16)
                            e.dma_start(
                                wuv_g[:, :].rearrange("p (k j) -> p k j",
                                                      j=512),
                                wuv_hbm.ap()[:, g * 512:(g + 1) * 512]
                                .rearrange("(k p) j -> p k j", p=128)
                            ).then_inc(s_gw, 16)
                        elif kind == "pack":
                            n["pk"] += 1; n["pr"] += 1
                        elif kind == "qasm":
                            idx = step[1]
                            g, hl = divmod(idx, 4)
                            e.wait_ge(s_pr, 4 * g + 4)
                            if idx > 0:
                                e.wait_ge(s_sc, 32 * idx)
                            e.dma_start(qr_sb[0:32, :],
                                        epr[32 * hl:32 * hl + 32, :]
                                        ).then_inc(s_qa, 16)
                            e.dma_start(qr_sb[32:64, :],
                                        opr[32 * hl:32 * hl + 32, :]
                                        ).then_inc(s_qa, 16)
                            n["qa"] += 32
                        elif kind == "proj":
                            n["pj"] += 24; n["pe"] += 24
                        elif kind == "gam":
                            n["gc"] += 1
                        else:
                            idx, qi = step[1], step[2]
                            n["sc"] += 2; n["mx"] += 3; n["ex"] += 2
                            nb = len(_tp_batches(qi))
                            n["tp"] += nb; n["pte"] += nb
                            n["pv"] += 1; n["oe"] += 1

                @blk.sync
                def _(e):
                    n = counters()
                    for step in sched:
                        if step[0] == "gam":
                            c = step[1]
                            if c >= 4:
                                e.wait_ge(s_gcc2[c % 2], c // 2 - 1)
                            m, rb, co, wd = GAM[c]
                            s_gdd.i(e.dma_start(
                                gb[c % 4][:, 0:wd],
                                w_hbm[m].ap()[rb * 128:(rb + 1) * 128,
                                              co:co + wd]), c)
                            n["gc"] += 1

                @blk.tensor
                def _(e):
                    n = counters()
                    tpend = [None]

                    def flush_pend(e):
                        kp, qp = tpend[0]
                        tpend[0] = None
                        e.wait_ge(s_ex, 2 * (kp + 1))
                        for (b0, bl) in _tp_batches(qp):
                            if n["tp"] >= 1:
                                e.wait_ge(s_pte, n["tp"])
                            for k in range(bl):
                                ins = e.transpose(
                                    ps_pt[:, k * 128:(k + 1) * 128],
                                    P_sb2[kp % 2][:, (b0 + k) * 128:
                                                  (b0 + k + 1) * 128],
                                    idn_sb[:, :])
                            ins.then_inc(s_tp, 1)
                            n["tp"] += 1; n["pte"] += 1
                        e.wait_ge(s_pte, n["pte"])
                        if n["pv"] >= 1:
                            e.wait_ge(s_oe, n["pv"])
                        for kt in range(qp + 1):
                            ins = e.matmul(
                                ps_o[:, 0:128],
                                PT_sb[:, kt * 128:(kt + 1) * 128],
                                v_sb[:, kt * 128:(kt + 1) * 128],
                                start=(kt == 0), stop=(kt == qp))
                        ins.then_inc(s_pv, 1)
                        n["pv"] += 1; n["oe"] += 1

                    for step in sched:
                        kind = step[0]
                        if kind == "wload":
                            pass
                        elif kind == "pack":
                            g, n2 = step[1], step[2]
                            if tpend[0] is not None:
                                flush_pend(e)
                            if n2 == 0:
                                e.wait_ge(s_gw, 48 * (g + 1))
                                if g > 0:
                                    e.wait_ge(s_ex, 128 * g)
                            if n["pk"] >= 2:
                                e.wait_ge(s_pr, n["pk"] - 1)
                            eo = n2 % 2
                            for which in range(2):
                                dst = ps_sc[:, which * 1024 + eo * 512:
                                            which * 1024 + eo * 512 + 512]
                                for kc in range(4):
                                    ins = e.matmul(
                                        dst,
                                        wuq_g[:, kc * 768 + 512 + which * 128:
                                              kc * 768 + 640 + which * 128],
                                        QB3[:, kc, n2 * 512:(n2 + 1) * 512],
                                        start=(kc == 0), stop=(kc == 3))
                            ins.then_inc(s_pk, 1)
                            n["pk"] += 1; n["pr"] += 1
                        elif kind == "qasm":
                            n["qa"] += 32
                        elif kind == "proj":
                            idx = step[1]
                            g, hl = divmod(idx, 4)
                            if tpend[0] is not None:
                                flush_pend(e)
                            ch = 0
                            for part in range(3):
                                nchunks = 4 if part < 2 else 16
                                for j in range(nchunks):
                                    ps = (ps_pa, ps_pb)[ch % 2]
                                    if n["pj"] + ch >= 2:
                                        e.wait_ge(s_pe, n["pj"] + ch - 1)
                                    if part == 0:
                                        for kc in range(4):
                                            ins = e.matmul(
                                                ps[:, 0:512],
                                                wuq_g[:, kc * 768 + hl * 128:
                                                      kc * 768 + (hl + 1) * 128],
                                                QB3[:, kc,
                                                    j * 512:(j + 1) * 512],
                                                start=(kc == 0),
                                                stop=(kc == 3))
                                    elif part == 1:
                                        for kc in range(4):
                                            ins = e.matmul(
                                                ps[:, 0:512],
                                                wuk_g[:, kc * 512 + hl * 128:
                                                      kc * 512 + (hl + 1) * 128],
                                                KV3[:, kc,
                                                    j * 512:(j + 1) * 512],
                                                start=(kc == 0),
                                                stop=(kc == 3))
                                    else:
                                        for kc in range(4):
                                            ins = e.matmul(
                                                ps[:, 0:128],
                                                KV3[:, kc,
                                                    j * 128:(j + 1) * 128],
                                                wuv_g[:, kc * 512 + hl * 128:
                                                      kc * 512 + (hl + 1) * 128],
                                                start=(kc == 0),
                                                stop=(kc == 3))
                                    ins.then_inc(s_pj, 1)
                                    ch += 1
                            n["pj"] += 24; n["pe"] += 24
                        elif kind == "gam":
                            n["gc"] += 1
                        else:
                            idx, qi = step[1], step[2]
                            g = idx // 4
                            klen = 128 * (qi + 1)
                            k_g = 16 * idx + qi
                            off = 1024 * (qi % 2) if qi < 8 else 0
                            if qi == 0 and tpend[0] is not None:
                                flush_pend(e)
                            if qi == 0:
                                e.wait_ge(s_pe, 24 * (idx + 1))
                                e.wait_ge(s_qa, 32 * (idx + 1))
                                e.wait_ge(s_pr, 4 * (g + 1))
                            nck = (klen + 511) // 512
                            if qi >= 8:
                                e.wait_ge(s_ex, 2 * k_g - 1)
                                parts = [(0, 2), (2, nck)]
                            else:
                                if 1 <= qi <= 7:
                                    if k_g >= 2:
                                        e.wait_ge(s_ex, 2 * (k_g - 1))
                                elif k_g >= 1:
                                    e.wait_ge(s_ex, 2 * k_g)
                                parts = [(0, nck)]
                            for pi, (c0, c1) in enumerate(parts):
                                if pi == 1:
                                    e.wait_ge(s_ex, 2 * k_g)
                                for ck in range(c0, c1):
                                    nn = min(512, klen - ck * 512)
                                    e.matmul(ps_sc[:, off + ck * 512:
                                                   off + ck * 512 + nn],
                                             qn_sb[:, qi * 128:
                                                   (qi + 1) * 128],
                                             kn_sb[:, ck * 512:
                                                   ck * 512 + nn],
                                             start=True, stop=False)
                                for ck in range(c0, c1):
                                    nn = min(512, klen - ck * 512)
                                    diag = (ck == nck - 1)
                                    ins = e.matmul(
                                        ps_sc[:, off + ck * 512:
                                              off + ck * 512 + nn],
                                        qr_sb[:, qi * 128:(qi + 1) * 128],
                                        KR[:, ck * 512:ck * 512 + nn],
                                        start=False, stop=not diag)
                                if pi == 0 and len(parts) == 2:
                                    ins.then_inc(s_sc, 1)
                            ins = e.matmul(
                                ps_sc[:, off + klen - 128:off + klen],
                                idn_sb[:, :], mask_sb[:, :],
                                start=False, stop=True)
                            ins.then_inc(s_sc, 1 if qi >= 8 else 2)
                            n["sc"] += 2; n["mx"] += 3; n["ex"] += 2
                            if tpend[0] is not None:
                                flush_pend(e)
                            tpend[0] = (k_g, qi)
                    if tpend[0] is not None:
                        flush_pend(e)

                @blk.vector
                def _(e):
                    n = counters()
                    vpend = [None]

                    def vflush(e):
                        kp, qp, ip = vpend[0]
                        vpend[0] = None
                        e.wait_ge(s_ex, 2 * (kp + 1))
                        pp = kp % 2
                        if qp >= 8:
                            e.tensor_tensor(lcol[:, 6 + pp:7 + pp],
                                            lcol[:, pp:pp + 1],
                                            lcol[:, 4 + pp:5 + pp],
                                            op=OP.add)
                            e.drain()
                            e.reciprocal(lcol[:, 2 + pp:3 + pp],
                                         lcol[:, 6 + pp:7 + pp])
                        else:
                            e.reciprocal(lcol[:, 2 + pp:3 + pp],
                                         lcol[:, pp:pp + 1])
                        e.drain()
                        n["pv"] += 1
                        e.wait_ge(s_pv, n["pv"])
                        e.tensor_scalar(
                            BB[:, qp * 2048 + ip * 128:
                               qp * 2048 + (ip + 1) * 128],
                            ps_o[:, 0:128],
                            lcol[:, 2 + kp % 2:3 + kp % 2], None,
                            op0=OP.mult).then_inc(s_oe, 1)
                        n["oe"] += 1

                    for step in sched:
                        kind = step[0]
                        if kind == "wload":
                            pass
                        elif kind == "pack":
                            g, n2 = step[1], step[2]
                            if vpend[0] is not None:
                                vflush(e)
                            n["pk"] += 1
                            e.wait_ge(s_pk, n["pk"])
                            eo = n2 % 2
                            cs = slice(n2 * 512, (n2 + 1) * 512)
                            _rope(e, ps_sc[:, eo * 512:eo * 512 + 512],
                                  ps_sc[:, 1024 + eo * 512:1536 + eo * 512],
                                  epr[:, cs], opr[:, cs],
                                  [t[:, cs] for t in qtab],
                                  rc[0][:, :], rc[1][:, :], rc[2][:, :],
                                  rc[3][:, :])
                            e.drain().then_inc(s_pr, 1)
                            n["pr"] += 1
                        elif kind == "qasm":
                            n["qa"] += 32
                        elif kind == "proj":
                            idx = step[1]
                            if vpend[0] is not None:
                                vflush(e)
                            ch = 0
                            for part in range(3):
                                nchunks = 4 if part < 2 else 16
                                for j in range(nchunks):
                                    ps = (ps_pa, ps_pb)[ch % 2]
                                    e.wait_ge(s_pj, n["pj"] + ch + 1)
                                    if ch == 0 and idx > 0:
                                        e.wait_ge(s_sc, 32 * idx)
                                        e.wait_ge(s_oe, 16 * idx)
                                    if part == 0:
                                        ins = e.tensor_scalar(
                                            qn_sb[:, j * 512:(j + 1) * 512],
                                            ps[:, 0:512], scc[:, 0:1], None,
                                            op0=OP.mult)
                                    elif part == 1:
                                        ins = e.tensor_copy(
                                            kn_sb[:, j * 512:(j + 1) * 512],
                                            ps[:, 0:512])
                                    else:
                                        ins = e.tensor_copy(
                                            v_sb[:, j * 128:(j + 1) * 128],
                                            ps[:, 0:128])
                                    ins.then_inc(s_pe, 1)
                                    ch += 1
                            n["pj"] += 24; n["pe"] += 24
                        elif kind == "gam":
                            c = step[1]
                            if c % 2 == 0:
                                m, rb, co, wd = GAM[c]
                                s_gdd.w(e, c)
                                e.tensor_reduce(gpart[:, c:c + 1],
                                                gb[c % 4][:, 0:wd],
                                                axis=AX.X, op=OP.add,
                                                apply_absolute_value=True
                                                ).then_inc(s_gcc2[0], 1)
                            n["gc"] += 1
                        else:
                            idx, qi = step[1], step[2]
                            klen = 128 * (qi + 1)
                            k_g = 16 * idx + qi
                            off = 1024 * (qi % 2) if qi < 8 else 0
                            p = k_g % 2
                            if qi >= 8:
                                n["sc"] += 1
                                e.wait_ge(s_sc, n["sc"])
                                e.reduce_max(stat[:, 36 + 2 * p:37 + 2 * p],
                                             ps_sc[:, 0:1024], axis=AX.X,
                                             negate=True).then_inc(s_mx, 1)
                                n["sc"] += 1
                                e.wait_ge(s_sc, n["sc"])
                                e.reduce_max(stat[:, 37 + 2 * p:38 + 2 * p],
                                             ps_sc[:, 1024:klen], axis=AX.X,
                                             negate=True).then_inc(s_mx, 1)
                                n["mx"] += 2
                                e.wait_ge(s_mx, n["mx"])
                                e.tensor_tensor(stat[:, 42 + p:43 + p],
                                                stat[:, 36 + 2 * p:37 + 2 * p],
                                                stat[:, 37 + 2 * p:38 + 2 * p],
                                                op=OP.min).then_inc(s_mx, 1)
                                n["mx"] += 1
                            else:
                                n["sc"] += 2
                                e.wait_ge(s_sc, n["sc"])
                                e.reduce_max(stat[:, 42 + p:43 + p],
                                             ps_sc[:, off:off + klen],
                                             axis=AX.X,
                                             negate=True).then_inc(s_mx, 3)
                                n["mx"] += 3
                            n["ex"] += 2
                            if vpend[0] is not None:
                                vflush(e)
                            vpend[0] = (k_g, qi, idx)
                    if vpend[0] is not None:
                        vflush(e)

                @blk.scalar
                def _(e):
                    n = counters()
                    cpend = [None]

                    def cflush(e):
                        kp, qp = cpend[0]
                        cpend[0] = None
                        for (b0, bl) in _tp_batches(qp):
                            n["tp"] += 1
                            e.wait_ge(s_tp, n["tp"])
                            e.copy(PT_sb[:, b0 * 128:(b0 + bl) * 128],
                                   ps_pt[:, 0:bl * 128]).then_inc(s_pte, 1)
                            n["pte"] += 1

                    for step in sched:
                        kind = step[0]
                        if kind == "wload":
                            pass
                        elif kind == "pack":
                            if cpend[0] is not None:
                                cflush(e)
                            n["pk"] += 1; n["pr"] += 1
                        elif kind == "qasm":
                            n["qa"] += 32
                        elif kind == "proj":
                            if cpend[0] is not None:
                                cflush(e)
                            n["pj"] += 24; n["pe"] += 24
                        elif kind == "gam":
                            c = step[1]
                            if c % 2 == 1:
                                m, rb, co, wd = GAM[c]
                                half = (c // 2) % 2
                                s_gdd.w(e, c)
                                e.activation(dump[:, 1024 + 512 * half:
                                                  1024 + 512 * half + wd],
                                             gb[c % 4][:, 0:wd], AF.Abs,
                                             accum_out=gpart[:, c:c + 1]
                                             ).then_inc(s_gcc2[1], 1)
                            n["gc"] += 1
                        else:
                            idx, qi = step[1], step[2]
                            klen = 128 * (qi + 1)
                            k_g = 16 * idx + qi
                            off = 1024 * (qi % 2) if qi < 8 else 0
                            if cpend[0] is not None:
                                cflush(e)
                            n["sc"] += 2; n["mx"] += 3
                            e.wait_ge(s_mx, n["mx"])
                            p = k_g % 2
                            if qi >= 8:
                                e.activation(P_sb2[p][:, 0:1024],
                                             ps_sc[:, 0:1024], AF.Exp,
                                             bias=stat[:, 42 + p:43 + p],
                                             accum_out=lcol[:, p:p + 1]
                                             ).then_inc(s_ex, 1)
                                e.activation(P_sb2[p][:, 1024:klen],
                                             ps_sc[:, 1024:klen], AF.Exp,
                                             bias=stat[:, 42 + p:43 + p],
                                             accum_out=lcol[:, 4 + p:5 + p]
                                             ).then_inc(s_ex, 1)
                            else:
                                e.activation(P_sb2[p][:, 0:klen],
                                             ps_sc[:, off:off + klen],
                                             AF.Exp,
                                             bias=stat[:, 42 + p:43 + p],
                                             accum_out=lcol[:, p:p + 1]
                                             ).then_inc(s_ex, 2)
                            n["ex"] += 2
                            n["pv"] += 1; n["oe"] += 1
                            cpend[0] = (k_g, qi)
                    if cpend[0] is not None:
                        cflush(e)

        tap_block("otm", BB[:, :])

    pairs = [(a, b) for a in range(NT) for b in range(a, NT)]
    s_dt, s_de = sem("s_dt"), sem("s_de")
    with nc.psum_tensor("ps_dp0", [128, 256], BF16) as pdp0, \
         nc.psum_tensor("ps_dp1", [128, 256], BF16) as pdp1:
        pdp = [pdp0, pdp1]
        with nc.Block("d1") as blk:
            @blk.tensor
            def _(e):
                for p, (a, b) in enumerate(pairs):
                    if p >= 2:
                        e.wait_ge(s_de, p - 1)
                    ins = e.transpose(pdp[p % 2][:, 0:128],
                                      BB[:, a * 2048 + b * 128:
                                         a * 2048 + (b + 1) * 128],
                                      idn_sb[:, :])
                    if a != b:
                        ins = e.transpose(pdp[p % 2][:, 128:256],
                                          BB[:, b * 2048 + a * 128:
                                             b * 2048 + (a + 1) * 128],
                                          idn_sb[:, :])
                    ins.then_inc(s_dt, 1)

            @blk.vector
            def _(e):
                for p, (a, b) in enumerate(pairs):
                    e.wait_ge(s_dt, p + 1)
                    ins = e.tensor_copy(
                        BB[:, b * 2048 + a * 128:b * 2048 + (a + 1) * 128],
                        pdp[p % 2][:, 0:128])
                    if a != b:
                        e.drain()
                        ins = e.tensor_copy(
                            BB[:, a * 2048 + b * 128:a * 2048 + (b + 1) * 128],
                            pdp[p % 2][:, 128:256])
                    ins.then_inc(s_de, 1)

    s_wo, s_da, s_dv = sem("s_wo"), sem("s_da"), sem("s_dv")
    s_xo, s_xw = DSem(nc, "s_xo"), DSem(nc, "s_xw")
    with ExitStack() as _dctx:
        wo_sb = _dctx.enter_context(nc.sbuf_tensor("wo_sb", [128, KC * 2048], BF16))
        dxs = [_dctx.enter_context(nc.sbuf_tensor(f"dx{i}", [128, 2048], F32)) for i in range(2)]
        dns = [_dctx.enter_context(nc.sbuf_tensor(f"dn{i}", [128, 2048], F32)) for i in range(2)]
        psd = [_dctx.enter_context(nc.psum_tensor(f"ps_d{i}", [128, 2048], F32)) for i in range(2)]
        wo3 = wo_sb[:, :].rearrange("p (k j) -> p k j", j=2048)
        with nc.Block("d2a") as blk:
            @blk.gpsimd
            def _(e):
                for kc in range(KC):
                    e.dma_start(wo3[:, kc, :],
                                wo_hbm.ap()[kc * 128:(kc + 1) * 128, :]
                                ).then_inc(s_wo, 16)

            @blk.sync
            def _(e):
                for ti in range(NT):
                    if ti >= 2:
                        e.wait_ge(s_dv, ti - 1)
                        s_xw.i(e.dma_start(
                            xnew_hbm.ap()[(ti - 2) * 128:(ti - 1) * 128, :],
                            dns[ti % 2][:, :]), ti - 2)
                    s_xo.i(e.dma_start(dxs[ti % 2][:, :],
                                       x_hbm.ap()[ti * 128:(ti + 1) * 128, :]),
                           ti)
                for tj in (NT - 2, NT - 1):
                    e.wait_ge(s_dv, tj + 1)
                    s_xw.i(e.dma_start(
                        xnew_hbm.ap()[tj * 128:(tj + 1) * 128, :],
                        dns[tj % 2][:, :]), tj)
                s_xw.w(e, NT - 2)
                s_xw.w(e, NT - 1)

            @blk.tensor
            def _(e):
                e.wait_ge(s_wo, 16 * KC)
                for ti in range(NT):
                    if ti >= 2:
                        e.wait_ge(s_dv, ti - 1)
                    for n4 in range(4):
                        for kc in range(KC):
                            ins = e.matmul(
                                psd[ti % 2][:, n4 * 512:(n4 + 1) * 512],
                                BB[:, kc * 2048 + ti * 128:
                                   kc * 2048 + (ti + 1) * 128],
                                wo3[:, kc, n4 * 512:(n4 + 1) * 512],
                                start=(kc == 0), stop=(kc == KC - 1))
                    ins.then_inc(s_da, 1)

            @blk.vector
            def _(e):
                for ti in range(NT):
                    e.wait_ge(s_da, ti + 1)
                    s_xo.w(e, ti)
                    if ti >= 2:
                        s_xw.w(e, ti - 2)
                    for n4 in range(4):
                        if n4 == 3:
                            e.drain()
                        ins = e.tensor_tensor(
                            dns[ti % 2][:, n4 * 512:(n4 + 1) * 512],
                            psd[ti % 2][:, n4 * 512:(n4 + 1) * 512],
                            dxs[ti % 2][:, n4 * 512:(n4 + 1) * 512],
                            op=OP.add)
                    ins.then_inc(s_dv, 1)

    emit_ln("ln2", lambda ti: xnew_hbm.ap()[ti * 128:(ti + 1) * 128, :])
    tap_block("ln2", BB[:, :])

    s_g1, s_g2, s_g3, s_g4 = sem("s_g1"), sem("s_g2"), sem("s_g3"), sem("s_g4")
    with nc.psum_tensor("ps_gg", [1, 4], F32) as ps_gg, \
         nc.psum_tensor("ps_rep", [128, 4], F32) as ps_rep:
        with nc.Block("e0") as blk:
            @blk.vector
            def _(e):
                for m in range(3):
                    e.reduce_sum(stat[:, 30 + m:31 + m],
                                 gpart[:, GAM_OFFS[m]:GAM_OFFS[m + 1]],
                                 axis=AX.X)
                e.memset(tsc[0:1, 8:9], float(HD * D))
                e.drain()
                e.reciprocal(tsc[0:1, 9:10], tsc[0:1, 8:9])
                e.drain().then_inc(s_g1, 1)
                e.tensor_scalar(tsc[0:1, 10:11], tsc[0:1, 9:10], 0.5, None,
                                op0=OP.mult)
                e.wait_ge(s_g2, 1)
                e.drain()
                for m in range(3):
                    e.tensor_scalar(tsc[0:1, 11 + m:12 + m],
                                    ps_gg[0:1, m:m + 1],
                                    tsc[0:1, 10:11], 5e-9, op0=OP.mult,
                                    op1=OP.add)
                e.drain().then_inc(s_g3, 1)
                e.wait_ge(s_g4, 1)
                for m in range(3):
                    e.tensor_copy(tcol[:, m:m + 1], ps_rep[:, m:m + 1])
                    e.tensor_scalar(tcol[:, 3 + m:4 + m], ps_rep[:, m:m + 1],
                                    -1.0, None, op0=OP.mult)

            @blk.tensor
            def _(e):
                e.wait_ge(s_g1, 1)
                for m in range(3):
                    ins = e.matmul(ps_gg[0:1, m:m + 1],
                                   stat[:, 30 + m:31 + m], onesc[:, :],
                                   start=True, stop=True)
                ins.then_inc(s_g2, 1)
                e.wait_ge(s_g3, 1)
                for m in range(3):
                    ins = e.matmul(ps_rep[:, m:m + 1], onesr[:, :],
                                   tsc[0:1, 11 + m:12 + m], start=True,
                                   stop=True)
                ins.then_inc(s_g4, 1)

    e1_sems = {k: SemC(sem("e1_" + k)) for k in
               ("A", "B", "q2", "wt", "wte", "wtv", "x1", "x2", "sg2", "h")}
    e1_sems["wd"] = DSem(nc, "e1_wd", nway=4)
    e1_sems["spl"] = DSem(nc, "e1_spl", nway=4)
    e2_sems = {k: SemC(sem("e2_" + k)) for k in ("u3", "ov")}
    e2_sems["3d"] = e1_sems["wd"]
    e2_sems["3dg"] = e1_sems["spl"]
    e2_sems["xn"] = DSem(nc, "e2_xn")
    e2_sems["y"] = DSem(nc, "e2_y")
    e2_sems["h1"] = e1_sems["h"]
    with nc.sbuf_tensor("h_sb", [128, HC * 512], BF16) as h_sb:
        for sg in range(NSG):
            _emit_e1(nc, e1_sems, sg, BB3, h_sb, idn_sb, tcol, w_hbm, wt_hbm)
            if sg == 0:
                tap_block("h0", h_sb[:, :])
            _emit_e2(nc, e2_sems, sg, h_sb, wt_hbm[2], xnew_hbm, y_hbm,
                     final=(sg == NSG - 1))

    return nc, taps


def _emit_e1(nc, sems, sg, BB3, h_sb, idn_sb, tcol, w_hbm, wt_hbm):
    tag = f"e1s{sg}"
    s_wd = sems["wd"]; s_sgA = sems["A"]; s_tsB = sems["B"]
    s_q2 = sems["q2"]; s_wt = sems["wt"]; s_wte = sems["wte"]
    s_wtv = sems["wtv"]; s_spl = sems["spl"]
    s_x1 = sems["x1"]; s_x2 = sems["x2"]
    s_sg2 = sems["sg2"]; s_h = sems["h"]

    items = []
    for c in range(HC):
        items.append((0, c))
        items.append((1, c))
        if sg == 0:
            items.append((2, c))
    ni = len(items)

    ESP = 768
    loadref = {}
    spillref = {}
    qctr = {"S": 0, "G": 0}

    def xdone_wait(e, i):
        m, c = items[i]
        if m == 0:
            s_x1.w(e, c + 1)
        elif m == 1:
            s_x2.w(e, c + 1)

    with ExitStack() as _e1ctx:
        def _sbE(n, sh, dt):
            return _e1ctx.enter_context(nc.sbuf_tensor(tag + n, sh, dt))
        def _psE(n, sh, dt=F32):
            return _e1ctx.enter_context(nc.psum_tensor(tag + n, sh, dt))
        wsb = [_sbE(f"_w{i}", [128, 2048], F32) for i in range(4)]
        A_sb2 = [_sbE(f"_A{i}", [128, 2048], BF16) for i in range(2)]
        B_sb2 = [_sbE(f"_B{i}", [128, 2048], BF16) for i in range(2)]
        wq_sb2 = [_sbE(f"_wq{i}", [128, 2048], BF16) for i in range(2)]
        wqT = [_sbE(f"_t{i}", [128, 2048], BF16) for i in range(2)]
        sgb = [_sbE(f"_sg{i}", [128, 512], F32) for i in range(2)]
        px1 = [_psE(f"_px1{i}", [128, 512]) for i in range(2)]
        px2 = [_psE(f"_px2{i}", [128, 512]) for i in range(2)]
        pwf = [_psE(f"_pwf{i}", [128, 1024], F32) for i in range(2)]

        def emit_load(e, ds, q, i):
            m, c = items[i]
            if sg == 0:
                if i >= 4:
                    s_sgA.w(e, 2 * (i - 3))
                if m < 2:
                    ins = e.dma_start(
                        wsb[i % 4][:, :],
                        w_hbm[m].ap()[c * 128:(c + 1) * 128, :])
                else:
                    ins = e.dma_start(
                        wsb[i % 4][:, :].rearrange("p (a j) -> p a j", j=128),
                        w_hbm[2].ap()[:, c * 128:(c + 1) * 128]
                        .rearrange("(a p) j -> p a j", p=128))
            else:
                if i >= 2:
                    xdone_wait(e, i - 2)
                ins = e.dma_start(wqT[i % 2][:, :], wt_hbm[m].ap()[c, :, :])
            k = qctr[q]; qctr[q] += 1
            loadref[i] = (ds, k)
            ds.i(ins, k)

        def emit_spill(e, ds, q, i):
            m, c = items[i]
            s_wtv.w(e, 2 * (i + 1))
            ins = e.dma_start(wt_hbm[m].ap()[c, :, :], wqT[i % 2][:, :])
            k = qctr[q]; qctr[q] += 1
            spillref[i] = (ds, k)
            ds.i(ins, k)

        def wref(e, ref):
            ds, k = ref
            ds.w(e, k)

        nS = ni if sg == 0 else (ni + 1) // 2
        nG = ni if sg == 0 else ni // 2

        with nc.Block(tag) as blk:
            @blk.sync
            def _(e):
                for i in range(ni):
                    if i % 2 == 0:
                        emit_load(e, s_wd, "S", i)
                    if sg == 0 and i - 5 >= 0 and (i - 5) % 2 == 1:
                        emit_spill(e, s_wd, "S", i - 5)
                if sg == 0:
                    for j in range(ni - 5, ni):
                        if j % 2 == 1:
                            emit_spill(e, s_wd, "S", j)
                    for k in range(nS - 4, nS):
                        s_wd.w(e, k)
                    for k in range(nG - 4, nG):
                        s_spl.w(e, k)

            @blk.gpsimd
            def _(e):
                for i in range(ni):
                    if i % 2 == 1:
                        emit_load(e, s_spl, "G", i)
                    if sg == 0 and i - 5 >= 0 and (i - 5) % 2 == 0:
                        emit_spill(e, s_spl, "G", i - 5)
                if sg == 0:
                    for j in range(ni - 5, ni):
                        if j % 2 == 0:
                            emit_spill(e, s_spl, "G", j)
                    for k in range(nG - 4, nG):
                        s_spl.w(e, k)
                    for k in range(nS - 4, nS):
                        s_wd.w(e, k)

            @blk.scalar
            def _(e):
                def sigmoid(c):
                    s_x1.w(e, c + 1)
                    if c >= 2:
                        s_h.w(e, c - 1)
                    s_sg2.i(e.activation(sgb[c % 2][:, :],
                                         px1[c % 2][:, 0:512],
                                         AF.Sigmoid, scale=0.5), 1)

                for i, (m, c) in enumerate(items):
                    if sg == 0:
                        wref(e, loadref[i])
                        if i >= 2:
                            s_q2.w(e, i - 1)
                        s_sgA.i(e.activation(A_sb2[i % 2][:, :],
                                             wsb[i % 4][:, :], AF.Sign,
                                             bias=tcol[:, 3 + m:4 + m]), 1)
                        s_sgA.i(e.activation(B_sb2[i % 2][:, :],
                                             wsb[i % 4][:, :], AF.Sign,
                                             bias=tcol[:, m:m + 1]), 1)
                        if m == 0 and c >= 1:
                            sigmoid(c - 1)
                    elif m == 1:
                        sigmoid(c)
                if sg == 0:
                    sigmoid(HC - 1)

            @blk.vector
            def _(e):
                def evacv(j):
                    for h in range(2):
                        s_wt.w(e, 2 * j + h + 1)
                        if h == 0 and j >= 2:
                            wref(e, spillref[j - 2])
                            xdone_wait(e, j - 2)
                        s_wtv.i(e.tensor_copy(
                            wqT[j % 2][:, h * 1024:(h + 1) * 1024],
                            pwf[h][:, :]), 1)

                def hmul(c):
                    s_sg2.w(e, c + 1)
                    s_x2.w(e, c + 1)
                    s_h.i(e.tensor_tensor(h_sb[:, c * 512:(c + 1) * 512],
                                          sgb[c % 2][:, :],
                                          px2[c % 2][:, 0:512],
                                          op=OP.mult), 1)

                for i, (m, c) in enumerate(items):
                    if sg == 0:
                        s_sgA.w(e, 2 * (i + 1))
                        if i >= 2:
                            s_wt.w(e, 2 * (i - 1))
                        s_q2.i(e.tensor_tensor(wq_sb2[i % 2][:, :],
                                               A_sb2[i % 2][:, :],
                                               B_sb2[i % 2][:, :],
                                               op=OP.add), 1)
                        if i >= 1:
                            evacv(i - 1)
                        if m == 2:
                            hmul(c)
                    elif m == 1:
                        hmul(c)
                if sg == 0:
                    evacv(ni - 1)

            @blk.tensor
            def _(e):
                def do_mms(i):
                    m, c = items[i]
                    if m == 2:
                        return
                    if sg == 0:
                        s_wtv.w(e, 2 * (i + 1))
                    else:
                        wref(e, loadref[i])
                    if c >= 2:
                        s_h.w(e, c - 1)
                    dst = (px1 if m == 0 else px2)[c % 2]
                    for kc in range(KC):
                        ins = e.matmul(
                            dst[:, 0:512],
                            wqT[i % 2][:, kc * 128:(kc + 1) * 128],
                            BB3[:, kc, sg * SG:(sg + 1) * SG],
                            start=(kc == 0), stop=(kc == KC - 1))
                    (s_x1 if m == 0 else s_x2).i(ins, 1)

                for i, (m, c) in enumerate(items):
                    if sg == 0:
                        s_q2.w(e, i + 1)
                        for h in range(2):
                            if i >= 1:
                                s_wtv.w(e, 2 * (i - 1) + h + 1)
                            for k in range(8):
                                ins = e.matmul(
                                    pwf[h][:, k * 128:(k + 1) * 128],
                                    wq_sb2[i % 2][:, (8 * h + k) * 128:
                                                  (8 * h + k + 1) * 128],
                                    idn_sb[:, :], start=True, stop=True)
                            s_wt.i(ins, 1)
                        if i >= 1:
                            do_mms(i - 1)
                    else:
                        do_mms(i)
                if sg == 0:
                    do_mms(ni - 1)

    s_wd.adv(qctr["S"]); s_spl.adv(qctr["G"])
    s_sgA.adv(2 * ni if sg == 0 else 0)
    s_q2.adv(ni if sg == 0 else 0)
    s_wt.adv(2 * ni if sg == 0 else 0)
    s_wtv.adv(2 * ni if sg == 0 else 0)
    for s in (s_x1, s_x2, s_sg2, s_h):
        s.adv(HC)


def _emit_e2(nc, sems, sg, h_sb, w3t_hbm, xnew_hbm, y_hbm, final):
    tag = f"e2s{sg}"
    s_3d = sems["3d"]; s_3dg = sems["3dg"]; s_u3 = sems["u3"]
    s_xn = sems["xn"]; s_ov = sems["ov"]; s_y = sems["y"]
    s_h1 = sems["h1"]
    hist_s = sems.setdefault("hist_s", [])
    hist_g = sems.setdefault("hist_g", [])

    with ExitStack() as _e2ctx:
        w3s = [_e2ctx.enter_context(nc.sbuf_tensor(f"{tag}_w3{i}",
                                                   [128, 1024], BF16))
               for i in range(8)]
        xns = [_e2ctx.enter_context(nc.sbuf_tensor(f"{tag}_xn{i}",
                                                   [128, 2048], F32))
               for i in range(4)]
        outs = [_e2ctx.enter_context(nc.sbuf_tensor(f"{tag}_o{i}",
                                                    [128, 2048], F32))
                for i in range(4)]
        pu = [_e2ctx.enter_context(nc.psum_tensor(f"{tag}_pu{i}",
                                                  [128, 512], F32))
              for i in range(8)]

        def buf(r, c):
            k = 22 * r + c // 2
            return 2 * (k % 4) + (c % 2)

        with nc.Block(tag) as blk:
            @blk.sync
            def _(e):
                s_ov.w(e, 0)
                for tl in range(4):
                    ti = sg * 4 + tl
                    if tl >= 2:
                        s_xn.w(e, tl - 2)
                    s_xn.i(e.dma_start(
                        xns[tl][:, :],
                        xnew_hbm.ap()[ti * 128:(ti + 1) * 128, :]), tl)
                for r in range(2):
                    for c in range(0, HC, 2):
                        n = len(hist_s)
                        if n >= 4:
                            e.wait_ge(s_u3.h, hist_s[n - 4] + 1)
                        hist_s.append(s_u3.base + r * HC + c)
                        s_3d.i(e.dma_start(
                            w3s[buf(r, c)][:, :],
                            w3t_hbm.ap()[c, :, r * 1024:(r + 1) * 1024]),
                            22 * r + c // 2)
                for r in range(2):
                    for tl in range(4):
                        ti = sg * 4 + tl
                        s_ov.w(e, r * 4 + tl + 1)
                        s_y.i(e.dma_start(
                            y_hbm.ap()[ti * 128:(ti + 1) * 128,
                                       r * 1024:(r + 1) * 1024],
                            outs[tl][:, r * 1024:(r + 1) * 1024]),
                            r * 4 + tl)
                s_y.w(e, 6)
                s_y.w(e, 7)

            @blk.gpsimd
            def _(e):
                for r in range(2):
                    for c in range(1, HC, 2):
                        n = len(hist_g)
                        if n >= 4:
                            e.wait_ge(s_u3.h, hist_g[n - 4] + 1)
                        hist_g.append(s_u3.base + r * HC + c)
                        s_3dg.i(e.dma_start(
                            w3s[buf(r, c)][:, :],
                            w3t_hbm.ap()[c, :, r * 1024:(r + 1) * 1024]),
                            22 * r + c // 2)

            @blk.tensor
            def _(e):
                for r in range(2):
                    if r > 0:
                        s_ov.w(e, 4)
                    for c in range(HC):
                        (s_3d if c % 2 == 0 else s_3dg).w(e, 22 * r + c // 2)
                        if r == 0:
                            s_h1.w(e, c + 1 - HC)
                        for tl in range(4):
                            for sub in range(2):
                                ins = e.matmul(
                                    pu[tl * 2 + sub][:, 0:512],
                                    h_sb[:, c * 512 + tl * 128:
                                         c * 512 + (tl + 1) * 128],
                                    w3s[buf(r, c)][:, sub * 512:
                                                   (sub + 1) * 512],
                                    start=(c == 0), stop=(c == HC - 1))
                        s_u3.i(ins, 1)

            @blk.vector
            def _(e):
                for r in range(2):
                    s_u3.w(e, (r + 1) * HC)
                    for tl in range(4):
                        if r == 0:
                            s_xn.w(e, tl)
                        for sub in range(2):
                            ins = e.scalar_tensor_tensor(
                                outs[tl][:, r * 1024 + sub * 512:
                                         r * 1024 + (sub + 1) * 512],
                                pu[tl * 2 + sub][:, 0:512], 0.25,
                                xns[tl][:, r * 1024 + sub * 512:
                                        r * 1024 + (sub + 1) * 512],
                                op0=OP.mult, op1=OP.add)
                        s_ov.i(ins, 1)

    s_3d.adv(2 * 22); s_3dg.adv(2 * 22); s_u3.adv(2 * HC)
    s_xn.adv(4); s_ov.adv(8); s_y.adv(8)


def _prep_consts():
    i32 = np.arange(32)
    f = np.power(10000.0, -i32.astype(np.float64) / 32.0)
    ang = np.outer(f, np.arange(S, dtype=np.float64))
    out = {}
    qscale = 1.0 / np.sqrt(np.float64(192.0))
    for nm, v in (("c", np.cos(ang)), ("s", np.sin(ang))):
        vq = v * qscale
        hi = vq.astype(np.float32)
        lo = (vq - hi.astype(np.float64)).astype(np.float32)
        out["q_" + nm + "h"] = np.ascontiguousarray(np.tile(hi, (4, 1)))
        out["q_" + nm + "l"] = np.ascontiguousarray(np.tile(lo, (4, 1)))
    ii, jj = np.meshgrid(np.arange(128), np.arange(128), indexing="ij")
    out["maskd"] = np.where(jj <= ii, 0.0, NEG_MASK).astype(ml_dtypes.bfloat16)
    out["idn"] = np.eye(128, dtype=ml_dtypes.bfloat16)
    return out


def _uq_perm():
    order = []
    for g in range(4):
        for hl in range(4):
            h = 4 * g + hl
            order += list(range(192 * h, 192 * h + 128))
        for hl in range(4):
            h = 4 * g + hl
            order += [192 * h + 128 + 2 * i for i in range(32)]
        for hl in range(4):
            h = 4 * g + hl
            order += [192 * h + 128 + 2 * i + 1 for i in range(32)]
    return np.array(order)


def kernel(L0_attn_x, L0_attn_w_dq, L0_attn_g_q, L0_attn_w_uq, L0_attn_w_dkv,
           L0_attn_g_kv, L0_attn_w_ukrope, L0_attn_w_uv, L0_attn_w_uknope,
           L0_attn_w_o, L0_ffn_w1, L0_ffn_w2, L0_ffn_w3):
    if "nc" not in _CACHE:
        _CACHE["nc"], _CACHE["taps"] = build(DEBUG_TAPS)
    nc = _CACHE["nc"]

    consts = _prep_consts()
    perm = _uq_perm()
    kperm = np.concatenate([np.arange(0, 64, 2), np.arange(1, 64, 2)])
    common = {
        "w_dq": np.ascontiguousarray(np.asarray(L0_attn_w_dq, np.float32)),
        "g_q": np.ascontiguousarray(np.asarray(L0_attn_g_q, np.float32)
                                    .reshape(1, DQ)),
        "w_uq_p": np.ascontiguousarray(
            np.asarray(L0_attn_w_uq, np.float32)[:, perm]),
        "w_dkv": np.ascontiguousarray(np.asarray(L0_attn_w_dkv, np.float32)),
        "g_kv": np.ascontiguousarray(np.asarray(L0_attn_g_kv, np.float32)
                                     .reshape(1, DKV)),
        "w_ukr_p": np.ascontiguousarray(
            np.asarray(L0_attn_w_ukrope, np.float32)[:, kperm]),
        "w_uv": np.ascontiguousarray(np.asarray(L0_attn_w_uv, np.float32)),
        "w_ukn": np.ascontiguousarray(np.asarray(L0_attn_w_uknope,
                                                 np.float32)),
        "w_o": np.ascontiguousarray(np.asarray(L0_attn_w_o, np.float32)),
        "w1": np.ascontiguousarray(np.asarray(L0_ffn_w1, np.float32)),
        "w2": np.ascontiguousarray(np.asarray(L0_ffn_w2, np.float32)),
        "w3": np.ascontiguousarray(np.asarray(L0_ffn_w3, np.float32)),
        "idn": consts["idn"], "maskd": consts["maskd"],
        "q_ch": consts["q_ch"], "q_cl": consts["q_cl"],
        "q_sh": consts["q_sh"], "q_sl": consts["q_sl"],
    }
    x = np.asarray(L0_attn_x, np.float32)
    in_maps = []
    for b in range(N_CORES):
        m = dict(common)
        m["x"] = np.ascontiguousarray(x[b])
        in_maps.append(m)

    res = run_bass_kernel_spmd(nc, in_maps, core_ids=list(range(N_CORES)))
    return np.stack([np.asarray(r["y"], np.float32) for r in res.results])


# version 15
# speedup vs baseline: 1.0248x; 1.0248x over previous
from contextlib import ExitStack

import numpy as np
import ml_dtypes

import concourse.bass as bass
import concourse.mybir as mybir
from concourse.bass_utils import run_bass_kernel_spmd

F32 = mybir.dt.float32
BF16 = mybir.dt.bfloat16
AF = mybir.ActivationFunctionType
OP = mybir.AluOpType
AX = mybir.AxisListType

S, D, H, DN, DR, DQ, DKV, HD = 2048, 2048, 16, 128, 64, 512, 512, 5632
NT, KC, HC = S // 128, D // 128, HD // 128
SG, NSG = 512, 4
NEG_MASK = -30000.0
N_CORES = 8

GAM = ([(0, r, h * 512, 512) for r in range(44) for h in range(4)]
       + [(1, r, h * 512, 512) for r in range(44) for h in range(4)]
       + [(2, r, h * 512, 512) for r in range(16) for h in range(11)])
NGAM = len(GAM)
GAM_OFFS = [0, 176, 352, NGAM]

DEBUG_TAPS = ()
_CACHE = {}


def _sched_c():
    gam = 0
    for g in range(4):
        yield ("wload", g)
        for nc2 in range(4):
            yield ("pack", g, nc2)
        for hl in range(4):
            idx = 4 * g + hl
            yield ("qasm", idx)
            yield ("proj", idx)
            for qi in range(NT):
                yield ("attn", idx, qi)
                for _ in range(2):
                    if gam < NGAM:
                        yield ("gam", gam)
                        gam += 1
            for _ in range(1):
                if gam < NGAM:
                    yield ("gam", gam)
                    gam += 1
    while gam < NGAM:
        yield ("gam", gam)
        gam += 1


def _tp_batches(qi):
    kts = qi + 1
    out, b = [], 0
    while b < kts:
        out.append((b, min(8, kts - b)))
        b += 8
    return out


class SemC:

    def __init__(self, h):
        self.h = h
        self.base = 0

    def w(self, e, v):
        e.wait_ge(self.h, self.base + v)

    def i(self, ins, n=1):
        return ins.then_inc(self.h, n)

    def adv(self, total):
        self.base += total


class DSem:

    def __init__(self, nc, name, nway=2):
        self.n = nway
        self.h = [nc.alloc_semaphore(name=name + str(i)) for i in range(nway)]
        self.base = [0] * nway

    def i(self, ins, k):
        return ins.then_inc(self.h[k % self.n], 16)

    def w(self, e, k):
        e.wait_ge(self.h[k % self.n],
                  self.base[k % self.n] + 16 * (k // self.n + 1))

    def adv(self, n_total):
        for p in range(self.n):
            self.base[p] += 16 * ((n_total + self.n - 1 - p) // self.n)


def _rope(e, Ein, Oin, Eout, Oout, tabs, t0, t1, t2, t3):
    ch, cl, sh, sl = tabs
    e.tensor_tensor(t0, Ein, ch, op=OP.mult)
    e.tensor_tensor(t1, Oin, sh, op=OP.mult)
    e.tensor_tensor(t2, Ein, cl, op=OP.mult)
    e.tensor_tensor(t3, Oin, sl, op=OP.mult)
    e.drain()
    e.tensor_tensor(t0, t0, t1, op=OP.subtract)
    e.tensor_tensor(t2, t2, t3, op=OP.subtract)
    e.drain()
    e.tensor_tensor(Eout, t0, t2, op=OP.add)
    e.drain()
    e.tensor_tensor(t0, Oin, ch, op=OP.mult)
    e.tensor_tensor(t1, Ein, sh, op=OP.mult)
    e.tensor_tensor(t2, Oin, cl, op=OP.mult)
    e.tensor_tensor(t3, Ein, sl, op=OP.mult)
    e.drain()
    e.tensor_tensor(t0, t0, t1, op=OP.add)
    e.tensor_tensor(t2, t2, t3, op=OP.add)
    e.drain()
    e.tensor_tensor(Oout, t0, t2, op=OP.add)


def build(debug_taps=()):
    nc = bass.Bass("TRN2", target_bir_lowering=False, debug=False)

    x_hbm = nc.dram_tensor("x", [S, D], F32, kind="ExternalInput")
    wdq_hbm = nc.dram_tensor("w_dq", [D, DQ], F32, kind="ExternalInput")
    gq_hbm = nc.dram_tensor("g_q", [1, DQ], F32, kind="ExternalInput")
    wuq_hbm = nc.dram_tensor("w_uq_p", [DQ, H * 192], F32, kind="ExternalInput")
    wdkv_hbm = nc.dram_tensor("w_dkv", [D, DKV], F32, kind="ExternalInput")
    gkv_hbm = nc.dram_tensor("g_kv", [1, DKV], F32, kind="ExternalInput")
    wukr_hbm = nc.dram_tensor("w_ukr_p", [D, DR], F32, kind="ExternalInput")
    wuv_hbm = nc.dram_tensor("w_uv", [DKV, H * DN], F32, kind="ExternalInput")
    wukn_hbm = nc.dram_tensor("w_ukn", [DKV, H * DN], F32, kind="ExternalInput")
    wo_hbm = nc.dram_tensor("w_o", [D, D], F32, kind="ExternalInput")
    w_hbm = [nc.dram_tensor("w1", [HD, D], F32, kind="ExternalInput"),
             nc.dram_tensor("w2", [HD, D], F32, kind="ExternalInput"),
             nc.dram_tensor("w3", [D, HD], F32, kind="ExternalInput")]
    idn_hbm = nc.dram_tensor("idn", [128, 128], BF16, kind="ExternalInput")
    mask_hbm = nc.dram_tensor("maskd", [128, 128], BF16, kind="ExternalInput")
    qtab_hbm = [nc.dram_tensor(n, [128, S], F32, kind="ExternalInput")
                for n in ("q_ch", "q_cl", "q_sh", "q_sl")]

    y_hbm = nc.dram_tensor("y", [S, D], F32, kind="ExternalOutput")
    xnew_hbm = nc.dram_tensor("x_new", [S, D], F32, kind="Internal")
    wt_hbm = [nc.dram_tensor(n, [HC, 128, D], BF16, kind="Internal")
              for n in ("w1qT", "w2qT", "w3qT")]

    taps = {}
    for name, shape, dt in (("ln1", [128, NT * 2048], BF16),
                            ("qb", [128, 4 * 2048], BF16),
                            ("kv", [128, 4 * 2048], BF16),
                            ("kr", [64, 2048], BF16),
                            ("otm", [128, NT * 2048], BF16),
                            ("ln2", [128, NT * 2048], BF16),
                            ("h0", [128, HC * 512], BF16)):
        if name in debug_taps:
            taps[name] = nc.dram_tensor("tap_" + name, shape, dt,
                                        kind="ExternalOutput")

    def sem(name):
        return nc.alloc_semaphore(name=name)

    def tap_block(name, src_ap):
        if name not in taps:
            return
        st = sem("s_tap_" + name)
        with nc.Block("tap_" + name) as blk:
            @blk.sync
            def _(e):
                e.dma_start(taps[name].ap(), src_ap).then_inc(st, 16)
                e.wait_ge(st, 16)

    BB = nc.alloc_sbuf_tensor("BB", [128, NT * 2048], BF16)
    BB3 = BB[:, :].rearrange("p (a b) -> p a b", b=2048)
    idn_sb = nc.alloc_sbuf_tensor("idn_sb", [128, 128], BF16)
    mask_sb = nc.alloc_sbuf_tensor("mask_sb", [128, 128], BF16)
    dump = nc.alloc_sbuf_tensor("dump", [128, 2048], F32)
    stat = nc.alloc_sbuf_tensor("stat", [128, 48], F32)
    onesr = nc.alloc_sbuf_tensor("onesr", [1, 128], F32)
    onesc = nc.alloc_sbuf_tensor("onesc", [128, 1], F32)
    scc = nc.alloc_sbuf_tensor("scc", [128, 4], F32)
    tcol = nc.alloc_sbuf_tensor("tcol", [128, 8], F32)
    tsc = nc.alloc_sbuf_tensor("tsc", [1, 16], F32)
    gpart = nc.alloc_sbuf_tensor("gpart", [128, NGAM], F32)
    lcol = nc.alloc_sbuf_tensor("lcol", [128, 8], F32)
    cst = nc.alloc_sbuf_tensor("cst", [128, 4], F32)

    s_c, s_i = sem("s_c"), sem("s_i")
    with nc.Block("init") as blk:
        @blk.sync
        def _(e):
            e.dma_start(idn_sb[:, :], idn_hbm.ap()).then_inc(s_c, 16)
            e.dma_start(mask_sb[:, :], mask_hbm.ap()).then_inc(s_c, 16)
            e.wait_ge(s_c, 32)

        @blk.vector
        def _(e):
            e.memset(onesr[:, :], 1.0)
            e.memset(onesc[:, :], 1.0)
            e.memset(cst[:, 0:1], 1e-5)
            e.memset(cst[:, 1:2], 1e-6)
            e.memset(scc[:, 1:2], 192.0).then_inc(s_i, 1)
            e.wait_ge(s_i, 2)
            e.reciprocal(scc[:, 0:1], scc[:, 2:3])

        @blk.scalar
        def _(e):
            e.wait_ge(s_i, 1)
            e.activation(scc[:, 2:3], scc[:, 1:2], AF.Sqrt).then_inc(s_i, 1)

    def emit_ln(tag, src_fn, extra_gpsimd=None):
        s_x = DSem(nc, tag + "_x")
        s_sq = sem(tag + "_sq")
        s_va = sem(tag + "_va")
        s_sa = sem(tag + "_sa")
        s_vb = sem(tag + "_vb")
        s_ap = sem(tag + "_ap")
        s_ta = sem(tag + "_ta")
        s_ea = sem(tag + "_ea")
        with ExitStack() as _lnctx:
            xs = [_lnctx.enter_context(nc.sbuf_tensor(f"{tag}_xs{i}", [128, 2048], F32)) for i in range(2)]
            lt = [_lnctx.enter_context(nc.sbuf_tensor(f"{tag}_lt{i}", [128, 2048], BF16)) for i in range(2)]
            pt = [_lnctx.enter_context(nc.psum_tensor(f"{tag}_pt{i}", [128, 1024], BF16)) for i in range(2)]
            with nc.Block(tag) as blk:
                @blk.sync
                def _(e):
                    for ti in range(NT):
                        if ti >= 2:
                            e.wait_ge(s_ap, ti - 1)
                        s_x.i(e.dma_start(xs[ti % 2][:, :], src_fn(ti)), ti)

                @blk.vector
                def _(e):
                    def evac(tj):
                        for hf in range(2):
                            e.wait_ge(s_ta, 2 * tj + hf + 1)
                            dst = BB3[:, hf * 8:(hf + 1) * 8,
                                      tj * 128:(tj + 1) * 128]
                            src = pt[hf][:, :].rearrange(
                                "p (a b) -> p a b", b=128)
                            e.tensor_copy(dst, src).then_inc(s_ea, 1)
                    for ti in range(NT):
                        xb = xs[ti % 2]
                        s_x.w(e, ti)
                        e.reduce_sum(stat[:, 1:2], xb[:, :], axis=AX.X)
                        e.wait_ge(s_sq, ti + 1)
                        e.drain()
                        e.tensor_scalar(stat[:, 2:3], stat[:, 1:2],
                                        1.0 / 2048.0, None, op0=OP.mult)
                        e.tensor_scalar(stat[:, 3:4], stat[:, 0:1],
                                        1.0 / 2048.0, None, op0=OP.mult)
                        e.drain()
                        e.scalar_tensor_tensor(stat[:, 5:6], stat[:, 2:3],
                                               stat[:, 2:3], stat[:, 3:4],
                                               op0=OP.mult,
                                               op1=OP.subtract
                                               ).then_inc(s_va, 1)
                        e.wait_ge(s_sa, ti + 1)
                        e.reciprocal(stat[:, 7:8], stat[:, 6:7])
                        e.drain()
                        e.tensor_scalar(stat[:, 8:9], stat[:, 2:3],
                                        stat[:, 7:8], -1.0, op0=OP.mult,
                                        op1=OP.mult).then_inc(s_vb, 1)
                        if ti > 0:
                            evac(ti - 1)
                    evac(NT - 1)

                @blk.scalar
                def _(e):
                    for ti in range(NT):
                        s_x.w(e, ti)
                        e.drain()
                        e.activation(dump[:, :], xs[ti % 2][:, :], AF.Square,
                                     accum_out=stat[:, 0:1]).then_inc(s_sq, 1)
                        e.wait_ge(s_va, ti + 1)
                        e.activation(stat[:, 6:7], stat[:, 5:6], AF.Sqrt,
                                     scale=-1.0,
                                     bias=cst[:, 0:1]).then_inc(s_sa, 1)
                        e.wait_ge(s_vb, ti + 1)
                        if ti >= 2:
                            e.wait_ge(s_ta, 2 * (ti - 2) + 2)
                        e.activation(lt[ti % 2][:, :], xs[ti % 2][:, :],
                                     AF.Identity, bias=stat[:, 8:9],
                                     scale=stat[:, 7:8]).then_inc(s_ap, 1)

                @blk.tensor
                def _(e):
                    for ti in range(NT):
                        e.wait_ge(s_ap, ti + 1)
                        for hf in range(2):
                            if ti > 0:
                                e.wait_ge(s_ea, 2 * (ti - 1) + hf + 1)
                            for k in range(8):
                                kc = hf * 8 + k
                                ins = e.transpose(
                                    pt[hf][:, k * 128:(k + 1) * 128],
                                    lt[ti % 2][:, kc * 128:(kc + 1) * 128],
                                    idn_sb[:, :])
                                if k == 7:
                                    ins.then_inc(s_ta, 1)

                if extra_gpsimd is not None:
                    blk.gpsimd(extra_gpsimd)

    with ExitStack() as _bcctx:
        QBt = _bcctx.enter_context(nc.sbuf_tensor("QB", [128, 4 * 2048], BF16))
        KVt = _bcctx.enter_context(nc.sbuf_tensor("KVt", [128, 4 * 2048], BF16))
        KR = _bcctx.enter_context(nc.sbuf_tensor("KR", [64, 2048], BF16))
        qtab = [_bcctx.enter_context(nc.sbuf_tensor(f"qt{i}", [128, S], F32)) for i in range(4)]
        QB3 = QBt[:, :].rearrange("p (a b) -> p a b", b=2048)
        KV3 = KVt[:, :].rearrange("p (a b) -> p a b", b=2048)

        s_wB = sem("s_wB")
        with ExitStack() as _wdctx:
            wdq_sb = _wdctx.enter_context(nc.sbuf_tensor("wdq_sb", [128, KC * 512], BF16))
            wdkv_sb = _wdctx.enter_context(nc.sbuf_tensor("wdkv_sb", [128, KC * 512], BF16))
            wukr_sb = _wdctx.enter_context(nc.sbuf_tensor("wukr_sb", [128, KC * 64], BF16))

            def pa_gpsimd(e):
                e.dma_start(
                    wdq_sb[:, :].rearrange("p (k j) -> p k j", j=512),
                    wdq_hbm.ap().rearrange("(k p) j -> p k j", p=128)
                ).then_inc(s_wB, 16)
                e.dma_start(
                    wdkv_sb[:, :].rearrange("p (k j) -> p k j", j=512),
                    wdkv_hbm.ap().rearrange("(k p) j -> p k j", p=128)
                ).then_inc(s_wB, 16)
                e.dma_start(
                    wukr_sb[:, :].rearrange("p (k j) -> p k j", j=64),
                    wukr_hbm.ap().rearrange("(k p) j -> p k j", p=128)
                ).then_inc(s_wB, 16)

            emit_ln("ln1", lambda ti: x_hbm.ap()[ti * 128:(ti + 1) * 128, :],
                    extra_gpsimd=pa_gpsimd)
            tap_block("ln1", BB[:, :])

            s_gd = sem("b_gd"); s_gr = sem("b_gr"); s_gv = sem("b_gv")
            s_mq = sem("b_mq"); s_mk = sem("b_mk")
            s_q1 = sem("b_q1"); s_k1 = sem("b_k1")
            s_vq = sem("b_vq"); s_vk = sem("b_vk")
            s_tq = sem("b_tq"); s_tk = sem("b_tk")
            s_eb = sem("b_eb")
            s_mr = sem("b_mr"); s_er = sem("b_er"); s_kr = sem("b_kr")

            with ExitStack() as _pbctx:
                def _sbA(n, sh, dt):
                    return _pbctx.enter_context(nc.sbuf_tensor(n, sh, dt))
                def _psA(n, sh, dt=F32):
                    return _pbctx.enter_context(nc.psum_tensor(n, sh, dt))
                ktab = [t[0:32, :] for t in qtab]
                grow = _sbA("grow", [1, DQ + DKV], F32)
                gq_rep = _sbA("gq_rep", [128, DQ], F32)
                gkv_rep = _sbA("gkv_rep", [128, DKV], F32)
                tmq = _sbA("tmq", [128, 512], F32)
                qtm0 = _sbA("qtm0", [128, 512], BF16)
                qtm1 = _sbA("qtm1", [128, 512], BF16)
                ktm0 = _sbA("ktm0", [128, 512], BF16)
                ktm1 = _sbA("ktm1", [128, 512], BF16)
                ekr = _sbA("ekr", [32, 2048], BF16)
                okr = _sbA("okr", [32, 2048], BF16)
                rt0 = _sbA("rt0", [32, 512], F32)
                rt1 = _sbA("rt1", [32, 512], F32)
                rt2 = _sbA("rt2", [32, 512], F32)
                rt3 = _sbA("rt3", [32, 512], F32)
                pcq0 = _psA("ps_cq0", [128, 512])
                pcq1 = _psA("ps_cq1", [128, 512])
                pck0 = _psA("ps_ck0", [128, 512])
                pck1 = _psA("ps_ck1", [128, 512])
                ptb0 = _psA("ps_tb0", [128, 512], BF16)
                ptb1 = _psA("ps_tb1", [128, 512], BF16)
                pkE = _psA("ps_kE", [32, 512])
                pkO = _psA("ps_kO", [32, 512])
                pcq, pck = [pcq0, pcq1], [pck0, pck1]
                ptb = [ptb0, ptb1]
                qtm, ktm = [qtm0, qtm1], [ktm0, ktm1]

                with nc.Block("pb") as blk:
                    @blk.sync
                    def _(e):
                        e.dma_start(grow[0:1, 0:DQ], gq_hbm.ap()).then_inc(
                            s_gd, 16)
                        e.dma_start(grow[0:1, DQ:DQ + DKV],
                                    gkv_hbm.ap()).then_inc(s_gd, 16)
                        for t, hbm in zip(qtab, qtab_hbm):
                            e.dma_start(t[:, :], hbm.ap()).then_inc(s_gd, 16)

                    @blk.tensor
                    def _(e):
                        def pb_transposes(tj):
                            for kind in range(2):
                                src = (qtm if kind == 0 else ktm)[tj % 2]
                                sv = s_vq if kind == 0 else s_vk
                                st = s_tq if kind == 0 else s_tk
                                e.wait_ge(sv, tj + 1)
                                if tj > 0:
                                    e.wait_ge(s_eb, 2 * (tj - 1) + kind + 1)
                                for c in range(4):
                                    ins = e.transpose(
                                        ptb[kind][:, c * 128:(c + 1) * 128],
                                        src[:, c * 128:(c + 1) * 128],
                                        idn_sb[:, :])
                                ins.then_inc(st, 1)

                        e.wait_ge(s_gd, 96)
                        e.matmul(pcq[0][:, 0:512], onesr[:, :],
                                 grow[0:1, 0:512], start=True, stop=True)
                        e.matmul(pck[0][:, 0:512], onesr[:, :],
                                 grow[0:1, 512:1024], start=True,
                                 stop=True).then_inc(s_gr, 1)
                        e.wait_ge(s_wB, 48)
                        e.wait_ge(s_gv, 1)
                        for ti in range(NT):
                            for kind in range(2):
                                ps = (pcq if kind == 0 else pck)[ti % 2]
                                w = wdq_sb if kind == 0 else wdkv_sb
                                sfree = s_vq if kind == 0 else s_vk
                                sdone = s_mq if kind == 0 else s_mk
                                if ti >= 2:
                                    e.wait_ge(sfree, ti - 1)
                                for kc in range(KC):
                                    ins = e.matmul(
                                        ps[:, 0:512],
                                        BB[:, kc * 2048 + ti * 128:
                                           kc * 2048 + (ti + 1) * 128],
                                        w[:, kc * 512:(kc + 1) * 512],
                                        start=(kc == 0), stop=(kc == KC - 1))
                                ins.then_inc(sdone, 1)
                            if ti > 0:
                                pb_transposes(ti - 1)
                        pb_transposes(NT - 1)
                        for n2 in range(4):
                            if n2 >= 1:
                                e.wait_ge(s_er, n2)
                            for kc in range(KC):
                                e.matmul(pkE[:, 0:512],
                                         wukr_sb[:, kc * 64:kc * 64 + 32],
                                         BB[:, kc * 2048 + n2 * 512:
                                            kc * 2048 + (n2 + 1) * 512],
                                         start=(kc == 0), stop=(kc == KC - 1))
                            for kc in range(KC):
                                ins = e.matmul(
                                    pkO[:, 0:512],
                                    wukr_sb[:, kc * 64 + 32:kc * 64 + 64],
                                    BB[:, kc * 2048 + n2 * 512:
                                       kc * 2048 + (n2 + 1) * 512],
                                    start=(kc == 0), stop=(kc == KC - 1))
                            ins.then_inc(s_mr, 1)

                    @blk.scalar
                    def _(e):
                        for ti in range(NT):
                            for kind in range(2):
                                ps = (pcq if kind == 0 else pck)[ti % 2]
                                sm = s_mq if kind == 0 else s_mk
                                so = s_q1 if kind == 0 else s_k1
                                acc = 16 + kind * 3 + ti % 2
                                sq = 22 + kind * 2 + ti % 2
                                e.wait_ge(sm, ti + 1)
                                e.activation(dump[:, 0:512], ps[:, 0:512],
                                             AF.Square,
                                             accum_out=stat[:, acc:acc + 1])
                                e.drain()
                                e.activation(stat[:, sq:sq + 1],
                                             stat[:, acc:acc + 1], AF.Sqrt,
                                             scale=1.0 / 512.0,
                                             bias=cst[:, 1:2]).then_inc(so, 1)

                    @blk.vector
                    def _(e):
                        def pb_evacs(tj):
                            for kind in range(2):
                                st = s_tq if kind == 0 else s_tk
                                dst3 = QB3 if kind == 0 else KV3
                                e.wait_ge(st, tj + 1)
                                e.tensor_copy(
                                    dst3[:, 0:4, tj * 128:(tj + 1) * 128],
                                    ptb[kind][:, :].rearrange(
                                        "p (a b) -> p a b", b=128)
                                ).then_inc(s_eb, 1)

                        e.wait_ge(s_gr, 1)
                        e.tensor_copy(gq_rep[:, :], pcq[0][:, 0:512])
                        e.tensor_copy(gkv_rep[:, :], pck[0][:, 0:512])
                        e.drain().then_inc(s_gv, 1)
                        for ti in range(NT):
                            for kind in range(2):
                                ps = (pcq if kind == 0 else pck)[ti % 2]
                                so = s_q1 if kind == 0 else s_k1
                                sv = s_vq if kind == 0 else s_vk
                                grep = gq_rep if kind == 0 else gkv_rep
                                dst = (qtm if kind == 0 else ktm)[ti % 2]
                                sq = 22 + kind * 2 + ti % 2
                                rc_ = 26 + kind * 2 + ti % 2
                                e.wait_ge(so, ti + 1)
                                e.reciprocal(stat[:, rc_:rc_ + 1],
                                             stat[:, sq:sq + 1])
                                e.drain()
                                e.tensor_scalar(tmq[:, :], ps[:, 0:512],
                                                stat[:, rc_:rc_ + 1], None,
                                                op0=OP.mult)
                                e.drain()
                                e.tensor_tensor(dst[:, :], tmq[:, :],
                                                grep[:, :],
                                                op=OP.mult).then_inc(sv, 1)
                            if ti > 0:
                                pb_evacs(ti - 1)
                        pb_evacs(NT - 1)
                        e.wait_ge(s_gd, 96)
                        for n2 in range(4):
                            e.wait_ge(s_mr, n2 + 1)
                            cs = slice(n2 * 512, (n2 + 1) * 512)
                            _rope(e, pkE[:, 0:512], pkO[:, 0:512],
                                  ekr[:, cs], okr[:, cs],
                                  [t[0:32, cs] for t in qtab],
                                  rt0[:, :], rt1[:, :], rt2[:, :], rt3[:, :])
                            e.drain()
                            e.tensor_scalar(ekr[:, cs], ekr[:, cs],
                                            scc[0:32, 2:3], None, op0=OP.mult)
                            e.tensor_scalar(okr[:, cs], okr[:, cs],
                                            scc[0:32, 2:3], None, op0=OP.mult)
                            e.drain().then_inc(s_er, 1)

                    @blk.gpsimd
                    def _(e):
                        e.wait_ge(s_er, 4)
                        e.dma_start(KR[0:32, :], ekr[:, :]).then_inc(s_kr, 16)
                        e.dma_start(KR[32:64, :], okr[:, :]).then_inc(s_kr, 16)
                        e.wait_ge(s_kr, 32)

            tap_block("qb", QBt[:, :])
            tap_block("kv", KVt[:, :])
            tap_block("kr", KR[:, :])

        s_gw = sem("c_gw"); s_qa = sem("c_qa")
        s_pk = sem("c_pk"); s_pr = sem("c_pr")
        s_pj = sem("c_pj"); s_pe = sem("c_pe")
        s_sc = sem("c_sc"); s_hd = sem("c_hd")
        s_mx = sem("c_mx"); s_ex = sem("c_ex")
        s_tp = sem("c_tp"); s_pte = sem("c_pte")
        s_pv = sem("c_pv"); s_oe = sem("c_oe")
        s_gdd = DSem(nc, "c_gdd", nway=4)
        s_gcc2 = [sem("c_gcc0"), sem("c_gcc1")]
        sched = list(_sched_c())

        def counters():
            return dict(pk=0, pr=0, pj=0, pe=0, sc=0, hd=0, mx=0, ex=0, tp=0,
                        pte=0, pv=0, oe=0, qa=0, gd=0, gc=0)

        with ExitStack() as _pcctx:
            def _sbC(n, sh, dt):
                return _pcctx.enter_context(nc.sbuf_tensor(n, sh, dt))
            def _psC(n, sh, dt=F32):
                return _pcctx.enter_context(nc.psum_tensor(n, sh, dt))
            wuq_g = _sbC("wuq_g", [128, 4 * 768], BF16)
            wuk_g = _sbC("wuk_g", [128, 4 * 512], BF16)
            wuv_g = _sbC("wuv_g", [128, 4 * 512], BF16)
            qn_sb = _sbC("qn_sb", [128, 2048], BF16)
            qr_sb = _sbC("qr_sb", [64, 2048], BF16)
            kn_sb = _sbC("kn_sb", [128, 2048], BF16)
            v_sb = _sbC("v_sb", [128, 2048], BF16)
            epr = _sbC("epr", [128, 2048], BF16)
            opr = _sbC("opr", [128, 2048], BF16)
            P_sb2 = [_sbC(f"P_sb{i}", [128, 2048], BF16)
                     for i in range(2)]
            PT_sb = _sbC("PT_sb", [128, 2048], BF16)
            rc = [_sbC(f"rc{i}", [128, 512], F32) for i in range(4)]
            gb = [_sbC(f"gb{i}", [128, 512], F32) for i in range(2)]
            gb = gb + [dump[:, 0:512], dump[:, 512:1024]]
            ps_pa = _psC("ps_pa", [128, 512])
            ps_pb = _psC("ps_pb", [128, 512])
            ps_sc = _psC("ps_sc", [128, 2048])
            ps_pt = _psC("ps_pt", [128, 1024], BF16)
            ps_o = _psC("ps_o", [128, 512])

            with nc.Block("pc") as blk:
                @blk.gpsimd
                def _(e):
                    n = counters()
                    for step in sched:
                        kind = step[0]
                        if kind == "wload":
                            g = step[1]
                            if g > 0:
                                e.wait_ge(s_sc, 32 * 4 * g)
                            e.dma_start(
                                wuq_g[:, :].rearrange("p (k j) -> p k j",
                                                      j=768),
                                wuq_hbm.ap()[:, g * 768:(g + 1) * 768]
                                .rearrange("(k p) j -> p k j", p=128)
                            ).then_inc(s_gw, 16)
                            e.dma_start(
                                wuk_g[:, :].rearrange("p (k j) -> p k j",
                                                      j=512),
                                wukn_hbm.ap()[:, g * 512:(g + 1) * 512]
                                .rearrange("(k p) j -> p k j", p=128)
                            ).then_inc(s_gw, 16)
                            e.dma_start(
                                wuv_g[:, :].rearrange("p (k j) -> p k j",
                                                      j=512),
                                wuv_hbm.ap()[:, g * 512:(g + 1) * 512]
                                .rearrange("(k p) j -> p k j", p=128)
                            ).then_inc(s_gw, 16)
                        elif kind == "pack":
                            n["pk"] += 1; n["pr"] += 1
                        elif kind == "qasm":
                            idx = step[1]
                            g, hl = divmod(idx, 4)
                            e.wait_ge(s_pr, 4 * g + 4)
                            if idx > 0:
                                e.wait_ge(s_sc, 32 * idx)
                            e.dma_start(qr_sb[0:32, :],
                                        epr[32 * hl:32 * hl + 32, :]
                                        ).then_inc(s_qa, 16)
                            e.dma_start(qr_sb[32:64, :],
                                        opr[32 * hl:32 * hl + 32, :]
                                        ).then_inc(s_qa, 16)
                            n["qa"] += 32
                        elif kind == "proj":
                            n["pj"] += 24; n["pe"] += 24
                        elif kind == "gam":
                            n["gc"] += 1
                        else:
                            idx, qi = step[1], step[2]
                            n["sc"] += 2; n["mx"] += 3; n["ex"] += 2
                            nb = len(_tp_batches(qi))
                            n["tp"] += nb; n["pte"] += nb
                            n["pv"] += 1; n["oe"] += 1

                @blk.sync
                def _(e):
                    n = counters()
                    for step in sched:
                        if step[0] == "gam":
                            c = step[1]
                            if c >= 4:
                                e.wait_ge(s_gcc2[c % 2], c // 2 - 1)
                            m, rb, co, wd = GAM[c]
                            s_gdd.i(e.dma_start(
                                gb[c % 4][:, 0:wd],
                                w_hbm[m].ap()[rb * 128:(rb + 1) * 128,
                                              co:co + wd]), c)
                            n["gc"] += 1

                @blk.tensor
                def _(e):
                    n = counters()
                    tpend = [None]

                    def flush_pend(e):
                        kp, qp = tpend[0]
                        tpend[0] = None
                        e.wait_ge(s_ex, 2 * (kp + 1))
                        for (b0, bl) in _tp_batches(qp):
                            if n["tp"] >= 1:
                                e.wait_ge(s_pte, n["tp"])
                            for k in range(bl):
                                ins = e.transpose(
                                    ps_pt[:, k * 128:(k + 1) * 128],
                                    P_sb2[kp % 2][:, (b0 + k) * 128:
                                                  (b0 + k + 1) * 128],
                                    idn_sb[:, :])
                            ins.then_inc(s_tp, 1)
                            n["tp"] += 1; n["pte"] += 1
                        e.wait_ge(s_pte, n["pte"])
                        if n["pv"] >= 1:
                            e.wait_ge(s_oe, n["pv"])
                        for kt in range(qp + 1):
                            ins = e.matmul(
                                ps_o[:, 0:128],
                                PT_sb[:, kt * 128:(kt + 1) * 128],
                                v_sb[:, kt * 128:(kt + 1) * 128],
                                start=(kt == 0), stop=(kt == qp))
                        ins.then_inc(s_pv, 1)
                        n["pv"] += 1; n["oe"] += 1

                    for step in sched:
                        kind = step[0]
                        if kind == "wload":
                            pass
                        elif kind == "pack":
                            g, n2 = step[1], step[2]
                            if tpend[0] is not None:
                                flush_pend(e)
                            if n2 == 0:
                                e.wait_ge(s_gw, 48 * (g + 1))
                                if g > 0:
                                    e.wait_ge(s_ex, 128 * g)
                            if n["pk"] >= 2:
                                e.wait_ge(s_pr, n["pk"] - 1)
                            eo = n2 % 2
                            for which in range(2):
                                dst = ps_sc[:, which * 1024 + eo * 512:
                                            which * 1024 + eo * 512 + 512]
                                for kc in range(4):
                                    ins = e.matmul(
                                        dst,
                                        wuq_g[:, kc * 768 + 512 + which * 128:
                                              kc * 768 + 640 + which * 128],
                                        QB3[:, kc, n2 * 512:(n2 + 1) * 512],
                                        start=(kc == 0), stop=(kc == 3))
                            ins.then_inc(s_pk, 1)
                            n["pk"] += 1; n["pr"] += 1
                        elif kind == "qasm":
                            n["qa"] += 32
                        elif kind == "proj":
                            idx = step[1]
                            g, hl = divmod(idx, 4)
                            if tpend[0] is not None:
                                flush_pend(e)
                            ch = 0
                            for part in range(3):
                                nchunks = 4 if part < 2 else 16
                                for j in range(nchunks):
                                    ps = (ps_pa, ps_pb)[ch % 2]
                                    if n["pj"] + ch >= 2:
                                        e.wait_ge(s_pe, n["pj"] + ch - 1)
                                    if part == 0:
                                        for kc in range(4):
                                            ins = e.matmul(
                                                ps[:, 0:512],
                                                wuq_g[:, kc * 768 + hl * 128:
                                                      kc * 768 + (hl + 1) * 128],
                                                QB3[:, kc,
                                                    j * 512:(j + 1) * 512],
                                                start=(kc == 0),
                                                stop=(kc == 3))
                                    elif part == 1:
                                        for kc in range(4):
                                            ins = e.matmul(
                                                ps[:, 0:512],
                                                wuk_g[:, kc * 512 + hl * 128:
                                                      kc * 512 + (hl + 1) * 128],
                                                KV3[:, kc,
                                                    j * 512:(j + 1) * 512],
                                                start=(kc == 0),
                                                stop=(kc == 3))
                                    else:
                                        for kc in range(4):
                                            ins = e.matmul(
                                                ps[:, 0:128],
                                                KV3[:, kc,
                                                    j * 128:(j + 1) * 128],
                                                wuv_g[:, kc * 512 + hl * 128:
                                                      kc * 512 + (hl + 1) * 128],
                                                start=(kc == 0),
                                                stop=(kc == 3))
                                    ins.then_inc(s_pj, 1)
                                    ch += 1
                            n["pj"] += 24; n["pe"] += 24
                        elif kind == "gam":
                            n["gc"] += 1
                        else:
                            idx, qi = step[1], step[2]
                            g = idx // 4
                            klen = 128 * (qi + 1)
                            k_g = 16 * idx + qi
                            off = 1024 * (qi % 2) if qi < 8 else 0
                            if qi == 0 and tpend[0] is not None:
                                flush_pend(e)
                            if qi == 0:
                                e.wait_ge(s_pe, 24 * (idx + 1))
                                e.wait_ge(s_qa, 32 * (idx + 1))
                                e.wait_ge(s_pr, 4 * (g + 1))
                            nck = (klen + 511) // 512
                            if qi >= 8:
                                e.wait_ge(s_ex, 2 * k_g - 1)
                                parts = [(0, 2), (2, nck)]
                            else:
                                if 1 <= qi <= 7:
                                    if k_g >= 2:
                                        e.wait_ge(s_ex, 2 * (k_g - 1))
                                elif k_g >= 1:
                                    e.wait_ge(s_ex, 2 * k_g)
                                parts = [(0, nck)]
                            for pi, (c0, c1) in enumerate(parts):
                                if pi == 1:
                                    e.wait_ge(s_ex, 2 * k_g)
                                for ck in range(c0, c1):
                                    nn = min(512, klen - ck * 512)
                                    e.matmul(ps_sc[:, off + ck * 512:
                                                   off + ck * 512 + nn],
                                             qn_sb[:, qi * 128:
                                                   (qi + 1) * 128],
                                             kn_sb[:, ck * 512:
                                                   ck * 512 + nn],
                                             start=True, stop=False)
                                for ck in range(c0, c1):
                                    nn = min(512, klen - ck * 512)
                                    diag = (ck == nck - 1)
                                    ins = e.matmul(
                                        ps_sc[:, off + ck * 512:
                                              off + ck * 512 + nn],
                                        qr_sb[:, qi * 128:(qi + 1) * 128],
                                        KR[:, ck * 512:ck * 512 + nn],
                                        start=False, stop=not diag)
                                if pi == 0 and len(parts) == 2:
                                    ins.then_inc(s_sc, 1)
                            ins = e.matmul(
                                ps_sc[:, off + klen - 128:off + klen],
                                idn_sb[:, :], mask_sb[:, :],
                                start=False, stop=True)
                            ins.then_inc(s_sc, 1 if qi >= 8 else 2)
                            n["sc"] += 2; n["mx"] += 3; n["ex"] += 2
                            if tpend[0] is not None:
                                flush_pend(e)
                            tpend[0] = (k_g, qi)
                    if tpend[0] is not None:
                        flush_pend(e)

                @blk.vector
                def _(e):
                    n = counters()
                    vpend = [None]

                    def vflush(e):
                        kp, qp, ip = vpend[0]
                        vpend[0] = None
                        for (b0, bl) in _tp_batches(qp):
                            n["tp"] += 1
                            e.wait_ge(s_tp, n["tp"])
                            e.tensor_copy(PT_sb[:, b0 * 128:(b0 + bl) * 128],
                                          ps_pt[:, 0:bl * 128]
                                          ).then_inc(s_pte, 1)
                            n["pte"] += 1
                        e.wait_ge(s_ex, 2 * (kp + 1))
                        pp = kp % 2
                        if qp >= 8:
                            e.tensor_tensor(lcol[:, 6 + pp:7 + pp],
                                            lcol[:, pp:pp + 1],
                                            lcol[:, 4 + pp:5 + pp],
                                            op=OP.add)
                            e.drain()
                            e.reciprocal(lcol[:, 2 + pp:3 + pp],
                                         lcol[:, 6 + pp:7 + pp])
                        else:
                            e.reciprocal(lcol[:, 2 + pp:3 + pp],
                                         lcol[:, pp:pp + 1])
                        e.drain()
                        n["pv"] += 1
                        e.wait_ge(s_pv, n["pv"])
                        e.tensor_scalar(
                            BB[:, qp * 2048 + ip * 128:
                               qp * 2048 + (ip + 1) * 128],
                            ps_o[:, 0:128],
                            lcol[:, 2 + kp % 2:3 + kp % 2], None,
                            op0=OP.mult).then_inc(s_oe, 1)
                        n["oe"] += 1

                    for step in sched:
                        kind = step[0]
                        if kind == "wload":
                            pass
                        elif kind == "pack":
                            g, n2 = step[1], step[2]
                            if vpend[0] is not None:
                                vflush(e)
                            n["pk"] += 1
                            e.wait_ge(s_pk, n["pk"])
                            eo = n2 % 2
                            cs = slice(n2 * 512, (n2 + 1) * 512)
                            _rope(e, ps_sc[:, eo * 512:eo * 512 + 512],
                                  ps_sc[:, 1024 + eo * 512:1536 + eo * 512],
                                  epr[:, cs], opr[:, cs],
                                  [t[:, cs] for t in qtab],
                                  rc[0][:, :], rc[1][:, :], rc[2][:, :],
                                  rc[3][:, :])
                            e.drain().then_inc(s_pr, 1)
                            n["pr"] += 1
                        elif kind == "qasm":
                            n["qa"] += 32
                        elif kind == "proj":
                            idx = step[1]
                            if vpend[0] is not None:
                                vflush(e)
                            ch = 0
                            for part in range(3):
                                nchunks = 4 if part < 2 else 16
                                for j in range(nchunks):
                                    ps = (ps_pa, ps_pb)[ch % 2]
                                    e.wait_ge(s_pj, n["pj"] + ch + 1)
                                    if ch == 0 and idx > 0:
                                        e.wait_ge(s_sc, 32 * idx)
                                        e.wait_ge(s_oe, 16 * idx)
                                    if part == 0:
                                        ins = e.tensor_scalar(
                                            qn_sb[:, j * 512:(j + 1) * 512],
                                            ps[:, 0:512], scc[:, 0:1], None,
                                            op0=OP.mult)
                                    elif part == 1:
                                        ins = e.tensor_copy(
                                            kn_sb[:, j * 512:(j + 1) * 512],
                                            ps[:, 0:512])
                                    else:
                                        ins = e.tensor_copy(
                                            v_sb[:, j * 128:(j + 1) * 128],
                                            ps[:, 0:128])
                                    ins.then_inc(s_pe, 1)
                                    ch += 1
                            n["pj"] += 24; n["pe"] += 24
                        elif kind == "gam":
                            c = step[1]
                            if c % 2 == 0:
                                m, rb, co, wd = GAM[c]
                                s_gdd.w(e, c)
                                e.tensor_reduce(gpart[:, c:c + 1],
                                                gb[c % 4][:, 0:wd],
                                                axis=AX.X, op=OP.add,
                                                apply_absolute_value=True
                                                ).then_inc(s_gcc2[0], 1)
                            n["gc"] += 1
                        else:
                            idx, qi = step[1], step[2]
                            klen = 128 * (qi + 1)
                            k_g = 16 * idx + qi
                            off = 1024 * (qi % 2) if qi < 8 else 0
                            p = k_g % 2
                            if qi >= 8:
                                n["sc"] += 1
                                e.wait_ge(s_sc, n["sc"])
                                e.reduce_max(stat[:, 36 + 2 * p:37 + 2 * p],
                                             ps_sc[:, 0:1024], axis=AX.X,
                                             negate=True).then_inc(s_mx, 1)
                                n["sc"] += 1
                                e.wait_ge(s_sc, n["sc"])
                                e.reduce_max(stat[:, 37 + 2 * p:38 + 2 * p],
                                             ps_sc[:, 1024:klen], axis=AX.X,
                                             negate=True).then_inc(s_mx, 1)
                                n["mx"] += 2
                                e.wait_ge(s_mx, n["mx"])
                                e.tensor_tensor(stat[:, 42 + p:43 + p],
                                                stat[:, 36 + 2 * p:37 + 2 * p],
                                                stat[:, 37 + 2 * p:38 + 2 * p],
                                                op=OP.min).then_inc(s_mx, 1)
                                n["mx"] += 1
                            else:
                                n["sc"] += 2
                                e.wait_ge(s_sc, n["sc"])
                                e.reduce_max(stat[:, 42 + p:43 + p],
                                             ps_sc[:, off:off + klen],
                                             axis=AX.X,
                                             negate=True).then_inc(s_mx, 3)
                                n["mx"] += 3
                            n["ex"] += 2
                            if vpend[0] is not None:
                                vflush(e)
                            vpend[0] = (k_g, qi, idx)
                    if vpend[0] is not None:
                        vflush(e)

                @blk.scalar
                def _(e):
                    n = counters()
                    for step in sched:
                        kind = step[0]
                        if kind == "wload":
                            pass
                        elif kind == "pack":
                            n["pk"] += 1; n["pr"] += 1
                        elif kind == "qasm":
                            n["qa"] += 32
                        elif kind == "proj":
                            n["pj"] += 24; n["pe"] += 24
                        elif kind == "gam":
                            c = step[1]
                            if c % 2 == 1:
                                m, rb, co, wd = GAM[c]
                                half = (c // 2) % 2
                                s_gdd.w(e, c)
                                e.activation(dump[:, 1024 + 512 * half:
                                                  1024 + 512 * half + wd],
                                             gb[c % 4][:, 0:wd], AF.Abs,
                                             accum_out=gpart[:, c:c + 1]
                                             ).then_inc(s_gcc2[1], 1)
                            n["gc"] += 1
                        else:
                            idx, qi = step[1], step[2]
                            klen = 128 * (qi + 1)
                            k_g = 16 * idx + qi
                            off = 1024 * (qi % 2) if qi < 8 else 0
                            n["sc"] += 2; n["mx"] += 3
                            e.wait_ge(s_mx, n["mx"])
                            p = k_g % 2
                            if qi >= 8:
                                e.activation(P_sb2[p][:, 0:1024],
                                             ps_sc[:, 0:1024], AF.Exp,
                                             bias=stat[:, 42 + p:43 + p],
                                             accum_out=lcol[:, p:p + 1]
                                             ).then_inc(s_ex, 1)
                                e.activation(P_sb2[p][:, 1024:klen],
                                             ps_sc[:, 1024:klen], AF.Exp,
                                             bias=stat[:, 42 + p:43 + p],
                                             accum_out=lcol[:, 4 + p:5 + p]
                                             ).then_inc(s_ex, 1)
                            else:
                                e.activation(P_sb2[p][:, 0:klen],
                                             ps_sc[:, off:off + klen],
                                             AF.Exp,
                                             bias=stat[:, 42 + p:43 + p],
                                             accum_out=lcol[:, p:p + 1]
                                             ).then_inc(s_ex, 2)
                            n["ex"] += 2
                            n["pv"] += 1; n["oe"] += 1

        tap_block("otm", BB[:, :])

    pairs = [(a, b) for a in range(NT) for b in range(a, NT)]
    s_dt, s_de = sem("s_dt"), sem("s_de")
    with nc.psum_tensor("ps_dp0", [128, 256], BF16) as pdp0, \
         nc.psum_tensor("ps_dp1", [128, 256], BF16) as pdp1:
        pdp = [pdp0, pdp1]
        with nc.Block("d1") as blk:
            @blk.tensor
            def _(e):
                for p, (a, b) in enumerate(pairs):
                    if p >= 2:
                        e.wait_ge(s_de, p - 1)
                    ins = e.transpose(pdp[p % 2][:, 0:128],
                                      BB[:, a * 2048 + b * 128:
                                         a * 2048 + (b + 1) * 128],
                                      idn_sb[:, :])
                    if a != b:
                        ins = e.transpose(pdp[p % 2][:, 128:256],
                                          BB[:, b * 2048 + a * 128:
                                             b * 2048 + (a + 1) * 128],
                                          idn_sb[:, :])
                    ins.then_inc(s_dt, 1)

            @blk.vector
            def _(e):
                for p, (a, b) in enumerate(pairs):
                    e.wait_ge(s_dt, p + 1)
                    ins = e.tensor_copy(
                        BB[:, b * 2048 + a * 128:b * 2048 + (a + 1) * 128],
                        pdp[p % 2][:, 0:128])
                    if a != b:
                        e.drain()
                        ins = e.tensor_copy(
                            BB[:, a * 2048 + b * 128:a * 2048 + (b + 1) * 128],
                            pdp[p % 2][:, 128:256])
                    ins.then_inc(s_de, 1)

    s_wo, s_da, s_dv = sem("s_wo"), sem("s_da"), sem("s_dv")
    s_xo, s_xw = DSem(nc, "s_xo"), DSem(nc, "s_xw")
    with ExitStack() as _dctx:
        wo_sb = _dctx.enter_context(nc.sbuf_tensor("wo_sb", [128, KC * 2048], BF16))
        dxs = [_dctx.enter_context(nc.sbuf_tensor(f"dx{i}", [128, 2048], F32)) for i in range(2)]
        dns = [_dctx.enter_context(nc.sbuf_tensor(f"dn{i}", [128, 2048], F32)) for i in range(2)]
        psd = [_dctx.enter_context(nc.psum_tensor(f"ps_d{i}", [128, 2048], F32)) for i in range(2)]
        wo3 = wo_sb[:, :].rearrange("p (k j) -> p k j", j=2048)
        with nc.Block("d2a") as blk:
            @blk.gpsimd
            def _(e):
                for kc in range(KC):
                    e.dma_start(wo3[:, kc, :],
                                wo_hbm.ap()[kc * 128:(kc + 1) * 128, :]
                                ).then_inc(s_wo, 16)

            @blk.sync
            def _(e):
                for ti in range(NT):
                    if ti >= 2:
                        e.wait_ge(s_dv, ti - 1)
                        s_xw.i(e.dma_start(
                            xnew_hbm.ap()[(ti - 2) * 128:(ti - 1) * 128, :],
                            dns[ti % 2][:, :]), ti - 2)
                    s_xo.i(e.dma_start(dxs[ti % 2][:, :],
                                       x_hbm.ap()[ti * 128:(ti + 1) * 128, :]),
                           ti)
                for tj in (NT - 2, NT - 1):
                    e.wait_ge(s_dv, tj + 1)
                    s_xw.i(e.dma_start(
                        xnew_hbm.ap()[tj * 128:(tj + 1) * 128, :],
                        dns[tj % 2][:, :]), tj)
                s_xw.w(e, NT - 2)
                s_xw.w(e, NT - 1)

            @blk.tensor
            def _(e):
                e.wait_ge(s_wo, 16 * KC)
                for ti in range(NT):
                    if ti >= 2:
                        e.wait_ge(s_dv, ti - 1)
                    for n4 in range(4):
                        for kc in range(KC):
                            ins = e.matmul(
                                psd[ti % 2][:, n4 * 512:(n4 + 1) * 512],
                                BB[:, kc * 2048 + ti * 128:
                                   kc * 2048 + (ti + 1) * 128],
                                wo3[:, kc, n4 * 512:(n4 + 1) * 512],
                                start=(kc == 0), stop=(kc == KC - 1))
                    ins.then_inc(s_da, 1)

            @blk.vector
            def _(e):
                for ti in range(NT):
                    e.wait_ge(s_da, ti + 1)
                    s_xo.w(e, ti)
                    if ti >= 2:
                        s_xw.w(e, ti - 2)
                    for n4 in range(4):
                        if n4 == 3:
                            e.drain()
                        ins = e.tensor_tensor(
                            dns[ti % 2][:, n4 * 512:(n4 + 1) * 512],
                            psd[ti % 2][:, n4 * 512:(n4 + 1) * 512],
                            dxs[ti % 2][:, n4 * 512:(n4 + 1) * 512],
                            op=OP.add)
                    ins.then_inc(s_dv, 1)

    emit_ln("ln2", lambda ti: xnew_hbm.ap()[ti * 128:(ti + 1) * 128, :])
    tap_block("ln2", BB[:, :])

    s_g1, s_g2, s_g3, s_g4 = sem("s_g1"), sem("s_g2"), sem("s_g3"), sem("s_g4")
    with nc.psum_tensor("ps_gg", [1, 4], F32) as ps_gg, \
         nc.psum_tensor("ps_rep", [128, 4], F32) as ps_rep:
        with nc.Block("e0") as blk:
            @blk.vector
            def _(e):
                for m in range(3):
                    e.reduce_sum(stat[:, 30 + m:31 + m],
                                 gpart[:, GAM_OFFS[m]:GAM_OFFS[m + 1]],
                                 axis=AX.X)
                e.memset(tsc[0:1, 8:9], float(HD * D))
                e.drain()
                e.reciprocal(tsc[0:1, 9:10], tsc[0:1, 8:9])
                e.drain().then_inc(s_g1, 1)
                e.tensor_scalar(tsc[0:1, 10:11], tsc[0:1, 9:10], 0.5, None,
                                op0=OP.mult)
                e.wait_ge(s_g2, 1)
                e.drain()
                for m in range(3):
                    e.tensor_scalar(tsc[0:1, 11 + m:12 + m],
                                    ps_gg[0:1, m:m + 1],
                                    tsc[0:1, 10:11], 5e-9, op0=OP.mult,
                                    op1=OP.add)
                e.drain().then_inc(s_g3, 1)
                e.wait_ge(s_g4, 1)
                for m in range(3):
                    e.tensor_copy(tcol[:, m:m + 1], ps_rep[:, m:m + 1])
                    e.tensor_scalar(tcol[:, 3 + m:4 + m], ps_rep[:, m:m + 1],
                                    -1.0, None, op0=OP.mult)

            @blk.tensor
            def _(e):
                e.wait_ge(s_g1, 1)
                for m in range(3):
                    ins = e.matmul(ps_gg[0:1, m:m + 1],
                                   stat[:, 30 + m:31 + m], onesc[:, :],
                                   start=True, stop=True)
                ins.then_inc(s_g2, 1)
                e.wait_ge(s_g3, 1)
                for m in range(3):
                    ins = e.matmul(ps_rep[:, m:m + 1], onesr[:, :],
                                   tsc[0:1, 11 + m:12 + m], start=True,
                                   stop=True)
                ins.then_inc(s_g4, 1)

    e1_sems = {k: SemC(sem("e1_" + k)) for k in
               ("A", "B", "q2", "wt", "wte", "wtv", "x1", "x2", "sg2", "h")}
    e1_sems["wd"] = DSem(nc, "e1_wd", nway=4)
    e1_sems["spl"] = DSem(nc, "e1_spl", nway=4)
    e2_sems = {k: SemC(sem("e2_" + k)) for k in ("u3", "ov")}
    e2_sems["3d"] = e1_sems["wd"]
    e2_sems["3dg"] = e1_sems["spl"]
    e2_sems["xn"] = DSem(nc, "e2_xn")
    e2_sems["y"] = DSem(nc, "e2_y")
    e2_sems["h1"] = e1_sems["h"]
    with nc.sbuf_tensor("h_sb", [128, HC * 512], BF16) as h_sb:
        for sg in range(NSG):
            _emit_e1(nc, e1_sems, sg, BB3, h_sb, idn_sb, tcol, w_hbm, wt_hbm)
            if sg == 0:
                tap_block("h0", h_sb[:, :])
            _emit_e2(nc, e2_sems, sg, h_sb, wt_hbm[2], xnew_hbm, y_hbm,
                     final=(sg == NSG - 1))

    return nc, taps


def _emit_e1(nc, sems, sg, BB3, h_sb, idn_sb, tcol, w_hbm, wt_hbm):
    tag = f"e1s{sg}"
    s_wd = sems["wd"]; s_sgA = sems["A"]; s_tsB = sems["B"]
    s_q2 = sems["q2"]; s_wt = sems["wt"]; s_wte = sems["wte"]
    s_wtv = sems["wtv"]; s_spl = sems["spl"]
    s_x1 = sems["x1"]; s_x2 = sems["x2"]
    s_sg2 = sems["sg2"]; s_h = sems["h"]

    items = []
    for c in range(HC):
        items.append((0, c))
        items.append((1, c))
        if sg == 0:
            items.append((2, c))
    ni = len(items)

    ESP = 768
    loadref = {}
    spillref = {}
    qctr = {"S": 0, "G": 0}

    def xdone_wait(e, i):
        m, c = items[i]
        if m == 0:
            s_x1.w(e, c + 1)
        elif m == 1:
            s_x2.w(e, c + 1)

    with ExitStack() as _e1ctx:
        def _sbE(n, sh, dt):
            return _e1ctx.enter_context(nc.sbuf_tensor(tag + n, sh, dt))
        def _psE(n, sh, dt=F32):
            return _e1ctx.enter_context(nc.psum_tensor(tag + n, sh, dt))
        wsb = [_sbE(f"_w{i}", [128, 2048], F32) for i in range(4)]
        A_sb2 = [_sbE(f"_A{i}", [128, 2048], BF16) for i in range(2)]
        B_sb2 = [_sbE(f"_B{i}", [128, 2048], BF16) for i in range(2)]
        wq_sb2 = [_sbE(f"_wq{i}", [128, 2048], BF16) for i in range(2)]
        wqT = [_sbE(f"_t{i}", [128, 2048], BF16) for i in range(2)]
        sgb = [_sbE(f"_sg{i}", [128, 512], F32) for i in range(2)]
        px1 = [_psE(f"_px1{i}", [128, 512]) for i in range(2)]
        px2 = [_psE(f"_px2{i}", [128, 512]) for i in range(2)]
        pwf = [_psE(f"_pwf{i}", [128, 1024], F32) for i in range(2)]

        def emit_load(e, ds, q, i):
            m, c = items[i]
            if sg == 0:
                if i >= 4:
                    s_sgA.w(e, 2 * (i - 3))
                if m < 2:
                    ins = e.dma_start(
                        wsb[i % 4][:, :],
                        w_hbm[m].ap()[c * 128:(c + 1) * 128, :])
                else:
                    ins = e.dma_start(
                        wsb[i % 4][:, :].rearrange("p (a j) -> p a j", j=128),
                        w_hbm[2].ap()[:, c * 128:(c + 1) * 128]
                        .rearrange("(a p) j -> p a j", p=128))
            else:
                if i >= 2:
                    xdone_wait(e, i - 2)
                ins = e.dma_start(wqT[i % 2][:, :], wt_hbm[m].ap()[c, :, :])
            k = qctr[q]; qctr[q] += 1
            loadref[i] = (ds, k)
            ds.i(ins, k)

        def emit_spill(e, ds, q, i):
            m, c = items[i]
            s_wtv.w(e, 2 * (i + 1))
            ins = e.dma_start(wt_hbm[m].ap()[c, :, :], wqT[i % 2][:, :])
            k = qctr[q]; qctr[q] += 1
            spillref[i] = (ds, k)
            ds.i(ins, k)

        def wref(e, ref):
            ds, k = ref
            ds.w(e, k)

        nS = ni if sg == 0 else (ni + 1) // 2
        nG = ni if sg == 0 else ni // 2

        with nc.Block(tag) as blk:
            @blk.sync
            def _(e):
                for i in range(ni):
                    if i % 2 == 0:
                        emit_load(e, s_wd, "S", i)
                    if sg == 0 and i - 5 >= 0 and (i - 5) % 2 == 1:
                        emit_spill(e, s_wd, "S", i - 5)
                if sg == 0:
                    for j in range(ni - 5, ni):
                        if j % 2 == 1:
                            emit_spill(e, s_wd, "S", j)
                    for k in range(nS - 4, nS):
                        s_wd.w(e, k)
                    for k in range(nG - 4, nG):
                        s_spl.w(e, k)

            @blk.gpsimd
            def _(e):
                for i in range(ni):
                    if i % 2 == 1:
                        emit_load(e, s_spl, "G", i)
                    if sg == 0 and i - 5 >= 0 and (i - 5) % 2 == 0:
                        emit_spill(e, s_spl, "G", i - 5)
                if sg == 0:
                    for j in range(ni - 5, ni):
                        if j % 2 == 0:
                            emit_spill(e, s_spl, "G", j)
                    for k in range(nG - 4, nG):
                        s_spl.w(e, k)
                    for k in range(nS - 4, nS):
                        s_wd.w(e, k)

            @blk.scalar
            def _(e):
                def sigmoid(c):
                    s_x1.w(e, c + 1)
                    if c >= 2:
                        s_h.w(e, c - 1)
                    s_sg2.i(e.activation(sgb[c % 2][:, :],
                                         px1[c % 2][:, 0:512],
                                         AF.Sigmoid, scale=0.5), 1)

                for i, (m, c) in enumerate(items):
                    if sg == 0:
                        wref(e, loadref[i])
                        if i >= 2:
                            s_q2.w(e, i - 1)
                        s_sgA.i(e.activation(A_sb2[i % 2][:, :],
                                             wsb[i % 4][:, :], AF.Sign,
                                             bias=tcol[:, 3 + m:4 + m]), 1)
                        s_sgA.i(e.activation(B_sb2[i % 2][:, :],
                                             wsb[i % 4][:, :], AF.Sign,
                                             bias=tcol[:, m:m + 1]), 1)
                        if m == 0 and c >= 1:
                            sigmoid(c - 1)
                    elif m == 1:
                        sigmoid(c)
                if sg == 0:
                    sigmoid(HC - 1)

            @blk.vector
            def _(e):
                def evacv(j):
                    for h in range(2):
                        s_wt.w(e, 2 * j + h + 1)
                        if h == 0 and j >= 2:
                            wref(e, spillref[j - 2])
                            xdone_wait(e, j - 2)
                        s_wtv.i(e.tensor_copy(
                            wqT[j % 2][:, h * 1024:(h + 1) * 1024],
                            pwf[h][:, :]), 1)

                def hmul(c):
                    s_sg2.w(e, c + 1)
                    s_x2.w(e, c + 1)
                    s_h.i(e.tensor_tensor(h_sb[:, c * 512:(c + 1) * 512],
                                          sgb[c % 2][:, :],
                                          px2[c % 2][:, 0:512],
                                          op=OP.mult), 1)

                for i, (m, c) in enumerate(items):
                    if sg == 0:
                        s_sgA.w(e, 2 * (i + 1))
                        if i >= 2:
                            s_wt.w(e, 2 * (i - 1))
                        s_q2.i(e.tensor_tensor(wq_sb2[i % 2][:, :],
                                               A_sb2[i % 2][:, :],
                                               B_sb2[i % 2][:, :],
                                               op=OP.add), 1)
                        if i >= 1:
                            evacv(i - 1)
                        if m == 2:
                            hmul(c)
                    elif m == 1:
                        hmul(c)
                if sg == 0:
                    evacv(ni - 1)

            @blk.tensor
            def _(e):
                def do_mms(i):
                    m, c = items[i]
                    if m == 2:
                        return
                    if sg == 0:
                        s_wtv.w(e, 2 * (i + 1))
                    else:
                        wref(e, loadref[i])
                    if c >= 2:
                        s_h.w(e, c - 1)
                    dst = (px1 if m == 0 else px2)[c % 2]
                    for kc in range(KC):
                        ins = e.matmul(
                            dst[:, 0:512],
                            wqT[i % 2][:, kc * 128:(kc + 1) * 128],
                            BB3[:, kc, sg * SG:(sg + 1) * SG],
                            start=(kc == 0), stop=(kc == KC - 1))
                    (s_x1 if m == 0 else s_x2).i(ins, 1)

                for i, (m, c) in enumerate(items):
                    if sg == 0:
                        s_q2.w(e, i + 1)
                        for h in range(2):
                            if i >= 1:
                                s_wtv.w(e, 2 * (i - 1) + h + 1)
                            for k in range(8):
                                ins = e.matmul(
                                    pwf[h][:, k * 128:(k + 1) * 128],
                                    wq_sb2[i % 2][:, (8 * h + k) * 128:
                                                  (8 * h + k + 1) * 128],
                                    idn_sb[:, :], start=True, stop=True)
                            s_wt.i(ins, 1)
                        if i >= 1:
                            do_mms(i - 1)
                    else:
                        do_mms(i)
                if sg == 0:
                    do_mms(ni - 1)

    s_wd.adv(qctr["S"]); s_spl.adv(qctr["G"])
    s_sgA.adv(2 * ni if sg == 0 else 0)
    s_q2.adv(ni if sg == 0 else 0)
    s_wt.adv(2 * ni if sg == 0 else 0)
    s_wtv.adv(2 * ni if sg == 0 else 0)
    for s in (s_x1, s_x2, s_sg2, s_h):
        s.adv(HC)


def _emit_e2(nc, sems, sg, h_sb, w3t_hbm, xnew_hbm, y_hbm, final):
    tag = f"e2s{sg}"
    s_3d = sems["3d"]; s_3dg = sems["3dg"]; s_u3 = sems["u3"]
    s_xn = sems["xn"]; s_ov = sems["ov"]; s_y = sems["y"]
    s_h1 = sems["h1"]
    hist_s = sems.setdefault("hist_s", [])
    hist_g = sems.setdefault("hist_g", [])

    with ExitStack() as _e2ctx:
        w3s = [_e2ctx.enter_context(nc.sbuf_tensor(f"{tag}_w3{i}",
                                                   [128, 1024], BF16))
               for i in range(8)]
        xns = [_e2ctx.enter_context(nc.sbuf_tensor(f"{tag}_xn{i}",
                                                   [128, 2048], F32))
               for i in range(4)]
        outs = [_e2ctx.enter_context(nc.sbuf_tensor(f"{tag}_o{i}",
                                                    [128, 2048], F32))
                for i in range(4)]
        pu = [_e2ctx.enter_context(nc.psum_tensor(f"{tag}_pu{i}",
                                                  [128, 512], F32))
              for i in range(8)]

        def buf(r, c):
            k = 22 * r + c // 2
            return 2 * (k % 4) + (c % 2)

        with nc.Block(tag) as blk:
            @blk.sync
            def _(e):
                s_ov.w(e, 0)
                for tl in range(4):
                    ti = sg * 4 + tl
                    if tl >= 2:
                        s_xn.w(e, tl - 2)
                    s_xn.i(e.dma_start(
                        xns[tl][:, :],
                        xnew_hbm.ap()[ti * 128:(ti + 1) * 128, :]), tl)
                for r in range(2):
                    for c in range(0, HC, 2):
                        n = len(hist_s)
                        if n >= 4:
                            e.wait_ge(s_u3.h, hist_s[n - 4] + 1)
                        hist_s.append(s_u3.base + r * HC + c)
                        s_3d.i(e.dma_start(
                            w3s[buf(r, c)][:, :],
                            w3t_hbm.ap()[c, :, r * 1024:(r + 1) * 1024]),
                            22 * r + c // 2)
                for r in range(2):
                    for tl in range(4):
                        ti = sg * 4 + tl
                        s_ov.w(e, r * 4 + tl + 1)
                        s_y.i(e.dma_start(
                            y_hbm.ap()[ti * 128:(ti + 1) * 128,
                                       r * 1024:(r + 1) * 1024],
                            outs[tl][:, r * 1024:(r + 1) * 1024]),
                            r * 4 + tl)
                s_y.w(e, 6)
                s_y.w(e, 7)

            @blk.gpsimd
            def _(e):
                for r in range(2):
                    for c in range(1, HC, 2):
                        n = len(hist_g)
                        if n >= 4:
                            e.wait_ge(s_u3.h, hist_g[n - 4] + 1)
                        hist_g.append(s_u3.base + r * HC + c)
                        s_3dg.i(e.dma_start(
                            w3s[buf(r, c)][:, :],
                            w3t_hbm.ap()[c, :, r * 1024:(r + 1) * 1024]),
                            22 * r + c // 2)

            @blk.tensor
            def _(e):
                for r in range(2):
                    if r > 0:
                        s_ov.w(e, 4)
                    for c in range(HC):
                        (s_3d if c % 2 == 0 else s_3dg).w(e, 22 * r + c // 2)
                        if r == 0:
                            s_h1.w(e, c + 1 - HC)
                        for tl in range(4):
                            for sub in range(2):
                                ins = e.matmul(
                                    pu[tl * 2 + sub][:, 0:512],
                                    h_sb[:, c * 512 + tl * 128:
                                         c * 512 + (tl + 1) * 128],
                                    w3s[buf(r, c)][:, sub * 512:
                                                   (sub + 1) * 512],
                                    start=(c == 0), stop=(c == HC - 1))
                        s_u3.i(ins, 1)

            @blk.vector
            def _(e):
                for r in range(2):
                    s_u3.w(e, (r + 1) * HC)
                    for tl in range(4):
                        if r == 0:
                            s_xn.w(e, tl)
                        for sub in range(2):
                            ins = e.scalar_tensor_tensor(
                                outs[tl][:, r * 1024 + sub * 512:
                                         r * 1024 + (sub + 1) * 512],
                                pu[tl * 2 + sub][:, 0:512], 0.25,
                                xns[tl][:, r * 1024 + sub * 512:
                                        r * 1024 + (sub + 1) * 512],
                                op0=OP.mult, op1=OP.add)
                        s_ov.i(ins, 1)

    s_3d.adv(2 * 22); s_3dg.adv(2 * 22); s_u3.adv(2 * HC)
    s_xn.adv(4); s_ov.adv(8); s_y.adv(8)


def _prep_consts():
    i32 = np.arange(32)
    f = np.power(10000.0, -i32.astype(np.float64) / 32.0)
    ang = np.outer(f, np.arange(S, dtype=np.float64))
    out = {}
    qscale = 1.0 / np.sqrt(np.float64(192.0))
    for nm, v in (("c", np.cos(ang)), ("s", np.sin(ang))):
        vq = v * qscale
        hi = vq.astype(np.float32)
        lo = (vq - hi.astype(np.float64)).astype(np.float32)
        out["q_" + nm + "h"] = np.ascontiguousarray(np.tile(hi, (4, 1)))
        out["q_" + nm + "l"] = np.ascontiguousarray(np.tile(lo, (4, 1)))
    ii, jj = np.meshgrid(np.arange(128), np.arange(128), indexing="ij")
    out["maskd"] = np.where(jj <= ii, 0.0, NEG_MASK).astype(ml_dtypes.bfloat16)
    out["idn"] = np.eye(128, dtype=ml_dtypes.bfloat16)
    return out


def _uq_perm():
    order = []
    for g in range(4):
        for hl in range(4):
            h = 4 * g + hl
            order += list(range(192 * h, 192 * h + 128))
        for hl in range(4):
            h = 4 * g + hl
            order += [192 * h + 128 + 2 * i for i in range(32)]
        for hl in range(4):
            h = 4 * g + hl
            order += [192 * h + 128 + 2 * i + 1 for i in range(32)]
    return np.array(order)


def kernel(L0_attn_x, L0_attn_w_dq, L0_attn_g_q, L0_attn_w_uq, L0_attn_w_dkv,
           L0_attn_g_kv, L0_attn_w_ukrope, L0_attn_w_uv, L0_attn_w_uknope,
           L0_attn_w_o, L0_ffn_w1, L0_ffn_w2, L0_ffn_w3):
    if "nc" not in _CACHE:
        _CACHE["nc"], _CACHE["taps"] = build(DEBUG_TAPS)
    nc = _CACHE["nc"]

    consts = _prep_consts()
    perm = _uq_perm()
    kperm = np.concatenate([np.arange(0, 64, 2), np.arange(1, 64, 2)])
    common = {
        "w_dq": np.ascontiguousarray(np.asarray(L0_attn_w_dq, np.float32)),
        "g_q": np.ascontiguousarray(np.asarray(L0_attn_g_q, np.float32)
                                    .reshape(1, DQ)),
        "w_uq_p": np.ascontiguousarray(
            np.asarray(L0_attn_w_uq, np.float32)[:, perm]),
        "w_dkv": np.ascontiguousarray(np.asarray(L0_attn_w_dkv, np.float32)),
        "g_kv": np.ascontiguousarray(np.asarray(L0_attn_g_kv, np.float32)
                                     .reshape(1, DKV)),
        "w_ukr_p": np.ascontiguousarray(
            np.asarray(L0_attn_w_ukrope, np.float32)[:, kperm]),
        "w_uv": np.ascontiguousarray(np.asarray(L0_attn_w_uv, np.float32)),
        "w_ukn": np.ascontiguousarray(np.asarray(L0_attn_w_uknope,
                                                 np.float32)),
        "w_o": np.ascontiguousarray(np.asarray(L0_attn_w_o, np.float32)),
        "w1": np.ascontiguousarray(np.asarray(L0_ffn_w1, np.float32)),
        "w2": np.ascontiguousarray(np.asarray(L0_ffn_w2, np.float32)),
        "w3": np.ascontiguousarray(np.asarray(L0_ffn_w3, np.float32)),
        "idn": consts["idn"], "maskd": consts["maskd"],
        "q_ch": consts["q_ch"], "q_cl": consts["q_cl"],
        "q_sh": consts["q_sh"], "q_sl": consts["q_sl"],
    }
    x = np.asarray(L0_attn_x, np.float32)
    in_maps = []
    for b in range(N_CORES):
        m = dict(common)
        m["x"] = np.ascontiguousarray(x[b])
        in_maps.append(m)

    res = run_bass_kernel_spmd(nc, in_maps, core_ids=list(range(N_CORES)))
    return np.stack([np.asarray(r["y"], np.float32) for r in res.results])


# version 16 (final)
# speedup vs baseline: 1.0488x; 1.0234x over previous
from contextlib import ExitStack

import numpy as np
import ml_dtypes

import concourse.bass as bass
import concourse.mybir as mybir
from concourse.bass_utils import run_bass_kernel_spmd

F32 = mybir.dt.float32
BF16 = mybir.dt.bfloat16
AF = mybir.ActivationFunctionType
OP = mybir.AluOpType
AX = mybir.AxisListType

S, D, H, DN, DR, DQ, DKV, HD = 2048, 2048, 16, 128, 64, 512, 512, 5632
NT, KC, HC = S // 128, D // 128, HD // 128
SG, NSG = 512, 4
NEG_MASK = -30000.0
N_CORES = 8

GAM = ([(0, r, h * 512, 512) for r in range(44) for h in range(4)]
       + [(1, r, h * 512, 512) for r in range(44) for h in range(4)]
       + [(2, r, h * 512, 512) for r in range(16) for h in range(11)])
NGAM = len(GAM)
GAM_OFFS = [0, 176, 352, NGAM]

DEBUG_TAPS = ()
_CACHE = {}


def _sched_c():
    gam = 0
    for g in range(4):
        yield ("wload", g)
        for nc2 in range(4):
            yield ("pack", g, nc2)
        for hl in range(4):
            idx = 4 * g + hl
            yield ("qasm", idx)
            yield ("proj", idx)
            for qi in range(NT):
                yield ("attn", idx, qi)
                for _ in range(2):
                    if gam < NGAM:
                        yield ("gam", gam)
                        gam += 1
            for _ in range(1):
                if gam < NGAM:
                    yield ("gam", gam)
                    gam += 1
    while gam < NGAM:
        yield ("gam", gam)
        gam += 1


def _tp_batches(qi):
    kts = qi + 1
    out, b = [], 0
    while b < kts:
        out.append((b, min(8, kts - b)))
        b += 8
    return out


class SemC:

    def __init__(self, h):
        self.h = h
        self.base = 0

    def w(self, e, v):
        e.wait_ge(self.h, self.base + v)

    def i(self, ins, n=1):
        return ins.then_inc(self.h, n)

    def adv(self, total):
        self.base += total


class DSem:

    def __init__(self, nc, name, nway=2):
        self.n = nway
        self.h = [nc.alloc_semaphore(name=name + str(i)) for i in range(nway)]
        self.base = [0] * nway

    def i(self, ins, k):
        return ins.then_inc(self.h[k % self.n], 16)

    def w(self, e, k):
        e.wait_ge(self.h[k % self.n],
                  self.base[k % self.n] + 16 * (k // self.n + 1))

    def adv(self, n_total):
        for p in range(self.n):
            self.base[p] += 16 * ((n_total + self.n - 1 - p) // self.n)


def _rope(e, Ein, Oin, Eout, Oout, tabs, t0, t1, t2, t3):
    ch, cl, sh, sl = tabs
    e.tensor_tensor(t0, Ein, ch, op=OP.mult)
    e.tensor_tensor(t1, Oin, sh, op=OP.mult)
    e.tensor_tensor(t2, Ein, cl, op=OP.mult)
    e.tensor_tensor(t3, Oin, sl, op=OP.mult)
    e.drain()
    e.tensor_tensor(t0, t0, t1, op=OP.subtract)
    e.tensor_tensor(t2, t2, t3, op=OP.subtract)
    e.drain()
    e.tensor_tensor(Eout, t0, t2, op=OP.add)
    e.drain()
    e.tensor_tensor(t0, Oin, ch, op=OP.mult)
    e.tensor_tensor(t1, Ein, sh, op=OP.mult)
    e.tensor_tensor(t2, Oin, cl, op=OP.mult)
    e.tensor_tensor(t3, Ein, sl, op=OP.mult)
    e.drain()
    e.tensor_tensor(t0, t0, t1, op=OP.add)
    e.tensor_tensor(t2, t2, t3, op=OP.add)
    e.drain()
    e.tensor_tensor(Oout, t0, t2, op=OP.add)


def build(debug_taps=()):
    nc = bass.Bass("TRN2", target_bir_lowering=False, debug=False)

    x_hbm = nc.dram_tensor("x", [S, D], F32, kind="ExternalInput")
    wdq_hbm = nc.dram_tensor("w_dq", [D, DQ], F32, kind="ExternalInput")
    gq_hbm = nc.dram_tensor("g_q", [1, DQ], F32, kind="ExternalInput")
    wuq_hbm = nc.dram_tensor("w_uq_p", [DQ, H * 192], F32, kind="ExternalInput")
    wdkv_hbm = nc.dram_tensor("w_dkv", [D, DKV], F32, kind="ExternalInput")
    gkv_hbm = nc.dram_tensor("g_kv", [1, DKV], F32, kind="ExternalInput")
    wukr_hbm = nc.dram_tensor("w_ukr_p", [D, DR], F32, kind="ExternalInput")
    wuv_hbm = nc.dram_tensor("w_uv", [DKV, H * DN], F32, kind="ExternalInput")
    wukn_hbm = nc.dram_tensor("w_ukn", [DKV, H * DN], F32, kind="ExternalInput")
    wo_hbm = nc.dram_tensor("w_o", [D, D], F32, kind="ExternalInput")
    w_hbm = [nc.dram_tensor("w1", [HD, D], F32, kind="ExternalInput"),
             nc.dram_tensor("w2", [HD, D], F32, kind="ExternalInput"),
             nc.dram_tensor("w3", [D, HD], F32, kind="ExternalInput")]
    idn_hbm = nc.dram_tensor("idn", [128, 128], BF16, kind="ExternalInput")
    mask_hbm = nc.dram_tensor("maskd", [128, 128], BF16, kind="ExternalInput")
    qtab_hbm = [nc.dram_tensor(n, [128, S], F32, kind="ExternalInput")
                for n in ("q_ch", "q_cl", "q_sh", "q_sl")]

    y_hbm = nc.dram_tensor("y", [S, D], F32, kind="ExternalOutput")
    xnew_hbm = nc.dram_tensor("x_new", [S, D], F32, kind="Internal")
    wt_hbm = [nc.dram_tensor(n, [HC, 128, D], BF16, kind="Internal")
              for n in ("w1qT", "w2qT", "w3qT")]

    taps = {}
    for name, shape, dt in (("ln1", [128, NT * 2048], BF16),
                            ("qb", [128, 4 * 2048], BF16),
                            ("kv", [128, 4 * 2048], BF16),
                            ("kr", [64, 2048], BF16),
                            ("otm", [128, NT * 2048], BF16),
                            ("ln2", [128, NT * 2048], BF16),
                            ("h0", [128, HC * 512], BF16)):
        if name in debug_taps:
            taps[name] = nc.dram_tensor("tap_" + name, shape, dt,
                                        kind="ExternalOutput")

    def sem(name):
        return nc.alloc_semaphore(name=name)

    def tap_block(name, src_ap):
        if name not in taps:
            return
        st = sem("s_tap_" + name)
        with nc.Block("tap_" + name) as blk:
            @blk.sync
            def _(e):
                e.dma_start(taps[name].ap(), src_ap).then_inc(st, 16)
                e.wait_ge(st, 16)

    BB = nc.alloc_sbuf_tensor("BB", [128, NT * 2048], BF16)
    BB3 = BB[:, :].rearrange("p (a b) -> p a b", b=2048)
    idn_sb = nc.alloc_sbuf_tensor("idn_sb", [128, 128], BF16)
    mask_sb = nc.alloc_sbuf_tensor("mask_sb", [128, 128], BF16)
    dump = nc.alloc_sbuf_tensor("dump", [128, 2048], F32)
    stat = nc.alloc_sbuf_tensor("stat", [128, 48], F32)
    onesr = nc.alloc_sbuf_tensor("onesr", [1, 128], F32)
    onesc = nc.alloc_sbuf_tensor("onesc", [128, 1], F32)
    scc = nc.alloc_sbuf_tensor("scc", [128, 4], F32)
    tcol = nc.alloc_sbuf_tensor("tcol", [128, 8], F32)
    tsc = nc.alloc_sbuf_tensor("tsc", [1, 16], F32)
    gpart = nc.alloc_sbuf_tensor("gpart", [128, NGAM], F32)
    lcol = nc.alloc_sbuf_tensor("lcol", [128, 8], F32)
    cst = nc.alloc_sbuf_tensor("cst", [128, 4], F32)

    s_c, s_i = sem("s_c"), sem("s_i")
    with nc.Block("init") as blk:
        @blk.sync
        def _(e):
            e.dma_start(idn_sb[:, :], idn_hbm.ap()).then_inc(s_c, 16)
            e.dma_start(mask_sb[:, :], mask_hbm.ap()).then_inc(s_c, 16)
            e.wait_ge(s_c, 32)

        @blk.vector
        def _(e):
            e.memset(onesr[:, :], 1.0)
            e.memset(onesc[:, :], 1.0)
            e.memset(cst[:, 0:1], 1e-5)
            e.memset(cst[:, 1:2], 1e-6)
            e.memset(scc[:, 1:2], 192.0).then_inc(s_i, 1)
            e.wait_ge(s_i, 2)
            e.reciprocal(scc[:, 0:1], scc[:, 2:3])

        @blk.scalar
        def _(e):
            e.wait_ge(s_i, 1)
            e.activation(scc[:, 2:3], scc[:, 1:2], AF.Sqrt).then_inc(s_i, 1)

    def emit_ln(tag, src_fn, extra_gpsimd=None):
        s_x = DSem(nc, tag + "_x")
        s_sq = sem(tag + "_sq")
        s_va = sem(tag + "_va")
        s_sa = sem(tag + "_sa")
        s_vb = sem(tag + "_vb")
        s_ap = sem(tag + "_ap")
        s_ta = sem(tag + "_ta")
        s_ea = sem(tag + "_ea")
        with ExitStack() as _lnctx:
            xs = [_lnctx.enter_context(nc.sbuf_tensor(f"{tag}_xs{i}", [128, 2048], F32)) for i in range(2)]
            lt = [_lnctx.enter_context(nc.sbuf_tensor(f"{tag}_lt{i}", [128, 2048], BF16)) for i in range(2)]
            pt = [_lnctx.enter_context(nc.psum_tensor(f"{tag}_pt{i}", [128, 1024], BF16)) for i in range(2)]
            with nc.Block(tag) as blk:
                @blk.sync
                def _(e):
                    for ti in range(NT):
                        if ti >= 2:
                            e.wait_ge(s_ap, ti - 1)
                        s_x.i(e.dma_start(xs[ti % 2][:, :], src_fn(ti)), ti)

                @blk.vector
                def _(e):
                    def evac(tj):
                        for hf in range(2):
                            e.wait_ge(s_ta, 2 * tj + hf + 1)
                            dst = BB3[:, hf * 8:(hf + 1) * 8,
                                      tj * 128:(tj + 1) * 128]
                            src = pt[hf][:, :].rearrange(
                                "p (a b) -> p a b", b=128)
                            e.tensor_copy(dst, src).then_inc(s_ea, 1)
                    for ti in range(NT):
                        xb = xs[ti % 2]
                        s_x.w(e, ti)
                        e.reduce_sum(stat[:, 1:2], xb[:, :], axis=AX.X)
                        e.wait_ge(s_sq, ti + 1)
                        e.drain()
                        e.tensor_scalar(stat[:, 2:3], stat[:, 1:2],
                                        1.0 / 2048.0, None, op0=OP.mult)
                        e.tensor_scalar(stat[:, 3:4], stat[:, 0:1],
                                        1.0 / 2048.0, None, op0=OP.mult)
                        e.drain()
                        e.scalar_tensor_tensor(stat[:, 5:6], stat[:, 2:3],
                                               stat[:, 2:3], stat[:, 3:4],
                                               op0=OP.mult,
                                               op1=OP.subtract
                                               ).then_inc(s_va, 1)
                        e.wait_ge(s_sa, ti + 1)
                        e.reciprocal(stat[:, 7:8], stat[:, 6:7])
                        e.drain()
                        e.tensor_scalar(stat[:, 8:9], stat[:, 2:3],
                                        stat[:, 7:8], -1.0, op0=OP.mult,
                                        op1=OP.mult).then_inc(s_vb, 1)
                        if ti > 0:
                            evac(ti - 1)
                    evac(NT - 1)

                @blk.scalar
                def _(e):
                    for ti in range(NT):
                        s_x.w(e, ti)
                        e.drain()
                        e.activation(dump[:, :], xs[ti % 2][:, :], AF.Square,
                                     accum_out=stat[:, 0:1]).then_inc(s_sq, 1)
                        e.wait_ge(s_va, ti + 1)
                        e.activation(stat[:, 6:7], stat[:, 5:6], AF.Sqrt,
                                     scale=-1.0,
                                     bias=cst[:, 0:1]).then_inc(s_sa, 1)
                        e.wait_ge(s_vb, ti + 1)
                        if ti >= 2:
                            e.wait_ge(s_ta, 2 * (ti - 2) + 2)
                        e.activation(lt[ti % 2][:, :], xs[ti % 2][:, :],
                                     AF.Identity, bias=stat[:, 8:9],
                                     scale=stat[:, 7:8]).then_inc(s_ap, 1)

                @blk.tensor
                def _(e):
                    for ti in range(NT):
                        e.wait_ge(s_ap, ti + 1)
                        for hf in range(2):
                            if ti > 0:
                                e.wait_ge(s_ea, 2 * (ti - 1) + hf + 1)
                            for k in range(8):
                                kc = hf * 8 + k
                                ins = e.transpose(
                                    pt[hf][:, k * 128:(k + 1) * 128],
                                    lt[ti % 2][:, kc * 128:(kc + 1) * 128],
                                    idn_sb[:, :])
                                if k == 7:
                                    ins.then_inc(s_ta, 1)

                if extra_gpsimd is not None:
                    blk.gpsimd(extra_gpsimd)

    with ExitStack() as _bcctx:
        QBt = _bcctx.enter_context(nc.sbuf_tensor("QB", [128, 4 * 2048], BF16))
        KVt = _bcctx.enter_context(nc.sbuf_tensor("KVt", [128, 4 * 2048], BF16))
        KR = _bcctx.enter_context(nc.sbuf_tensor("KR", [64, 2048], BF16))
        qtab = [_bcctx.enter_context(nc.sbuf_tensor(f"qt{i}", [128, S], F32)) for i in range(4)]
        QB3 = QBt[:, :].rearrange("p (a b) -> p a b", b=2048)
        KV3 = KVt[:, :].rearrange("p (a b) -> p a b", b=2048)

        s_wB = sem("s_wB")
        with ExitStack() as _wdctx:
            wdq_sb = _wdctx.enter_context(nc.sbuf_tensor("wdq_sb", [128, KC * 512], BF16))
            wdkv_sb = _wdctx.enter_context(nc.sbuf_tensor("wdkv_sb", [128, KC * 512], BF16))
            wukr_sb = _wdctx.enter_context(nc.sbuf_tensor("wukr_sb", [128, KC * 64], BF16))

            def pa_gpsimd(e):
                e.dma_start(
                    wdq_sb[:, :].rearrange("p (k j) -> p k j", j=512),
                    wdq_hbm.ap().rearrange("(k p) j -> p k j", p=128)
                ).then_inc(s_wB, 16)
                e.dma_start(
                    wdkv_sb[:, :].rearrange("p (k j) -> p k j", j=512),
                    wdkv_hbm.ap().rearrange("(k p) j -> p k j", p=128)
                ).then_inc(s_wB, 16)
                e.dma_start(
                    wukr_sb[:, :].rearrange("p (k j) -> p k j", j=64),
                    wukr_hbm.ap().rearrange("(k p) j -> p k j", p=128)
                ).then_inc(s_wB, 16)

            emit_ln("ln1", lambda ti: x_hbm.ap()[ti * 128:(ti + 1) * 128, :],
                    extra_gpsimd=pa_gpsimd)
            tap_block("ln1", BB[:, :])

            s_gd = sem("b_gd"); s_gr = sem("b_gr"); s_gv = sem("b_gv")
            s_mq = sem("b_mq"); s_mk = sem("b_mk")
            s_q1 = sem("b_q1"); s_k1 = sem("b_k1")
            s_vq = sem("b_vq"); s_vk = sem("b_vk")
            s_tq = sem("b_tq"); s_tk = sem("b_tk")
            s_eb = sem("b_eb")
            s_mr = sem("b_mr"); s_er = sem("b_er"); s_kr = sem("b_kr")

            with ExitStack() as _pbctx:
                def _sbA(n, sh, dt):
                    return _pbctx.enter_context(nc.sbuf_tensor(n, sh, dt))
                def _psA(n, sh, dt=F32):
                    return _pbctx.enter_context(nc.psum_tensor(n, sh, dt))
                ktab = [t[0:32, :] for t in qtab]
                grow = _sbA("grow", [1, DQ + DKV], F32)
                gq_rep = _sbA("gq_rep", [128, DQ], F32)
                gkv_rep = _sbA("gkv_rep", [128, DKV], F32)
                tmq = _sbA("tmq", [128, 512], F32)
                qtm0 = _sbA("qtm0", [128, 512], BF16)
                qtm1 = _sbA("qtm1", [128, 512], BF16)
                ktm0 = _sbA("ktm0", [128, 512], BF16)
                ktm1 = _sbA("ktm1", [128, 512], BF16)
                ekr = _sbA("ekr", [32, 2048], BF16)
                okr = _sbA("okr", [32, 2048], BF16)
                rt0 = _sbA("rt0", [32, 512], F32)
                rt1 = _sbA("rt1", [32, 512], F32)
                rt2 = _sbA("rt2", [32, 512], F32)
                rt3 = _sbA("rt3", [32, 512], F32)
                pcq0 = _psA("ps_cq0", [128, 512])
                pcq1 = _psA("ps_cq1", [128, 512])
                pck0 = _psA("ps_ck0", [128, 512])
                pck1 = _psA("ps_ck1", [128, 512])
                ptb0 = _psA("ps_tb0", [128, 512], BF16)
                ptb1 = _psA("ps_tb1", [128, 512], BF16)
                pkE = _psA("ps_kE", [32, 512])
                pkO = _psA("ps_kO", [32, 512])
                pcq, pck = [pcq0, pcq1], [pck0, pck1]
                ptb = [ptb0, ptb1]
                qtm, ktm = [qtm0, qtm1], [ktm0, ktm1]

                with nc.Block("pb") as blk:
                    @blk.sync
                    def _(e):
                        e.dma_start(grow[0:1, 0:DQ], gq_hbm.ap()).then_inc(
                            s_gd, 16)
                        e.dma_start(grow[0:1, DQ:DQ + DKV],
                                    gkv_hbm.ap()).then_inc(s_gd, 16)
                        for t, hbm in zip(qtab, qtab_hbm):
                            e.dma_start(t[:, :], hbm.ap()).then_inc(s_gd, 16)

                    @blk.tensor
                    def _(e):
                        def pb_transposes(tj):
                            for kind in range(2):
                                src = (qtm if kind == 0 else ktm)[tj % 2]
                                sv = s_vq if kind == 0 else s_vk
                                st = s_tq if kind == 0 else s_tk
                                e.wait_ge(sv, tj + 1)
                                if tj > 0:
                                    e.wait_ge(s_eb, 2 * (tj - 1) + kind + 1)
                                for c in range(4):
                                    ins = e.transpose(
                                        ptb[kind][:, c * 128:(c + 1) * 128],
                                        src[:, c * 128:(c + 1) * 128],
                                        idn_sb[:, :])
                                ins.then_inc(st, 1)

                        e.wait_ge(s_gd, 96)
                        e.matmul(pcq[0][:, 0:512], onesr[:, :],
                                 grow[0:1, 0:512], start=True, stop=True)
                        e.matmul(pck[0][:, 0:512], onesr[:, :],
                                 grow[0:1, 512:1024], start=True,
                                 stop=True).then_inc(s_gr, 1)
                        e.wait_ge(s_wB, 48)
                        e.wait_ge(s_gv, 1)
                        for ti in range(NT):
                            for kind in range(2):
                                ps = (pcq if kind == 0 else pck)[ti % 2]
                                w = wdq_sb if kind == 0 else wdkv_sb
                                sfree = s_vq if kind == 0 else s_vk
                                sdone = s_mq if kind == 0 else s_mk
                                if ti >= 2:
                                    e.wait_ge(sfree, ti - 1)
                                for kc in range(KC):
                                    ins = e.matmul(
                                        ps[:, 0:512],
                                        BB[:, kc * 2048 + ti * 128:
                                           kc * 2048 + (ti + 1) * 128],
                                        w[:, kc * 512:(kc + 1) * 512],
                                        start=(kc == 0), stop=(kc == KC - 1))
                                ins.then_inc(sdone, 1)
                            if ti > 0:
                                pb_transposes(ti - 1)
                        pb_transposes(NT - 1)
                        for n2 in range(4):
                            if n2 >= 1:
                                e.wait_ge(s_er, n2)
                            for kc in range(KC):
                                e.matmul(pkE[:, 0:512],
                                         wukr_sb[:, kc * 64:kc * 64 + 32],
                                         BB[:, kc * 2048 + n2 * 512:
                                            kc * 2048 + (n2 + 1) * 512],
                                         start=(kc == 0), stop=(kc == KC - 1))
                            for kc in range(KC):
                                ins = e.matmul(
                                    pkO[:, 0:512],
                                    wukr_sb[:, kc * 64 + 32:kc * 64 + 64],
                                    BB[:, kc * 2048 + n2 * 512:
                                       kc * 2048 + (n2 + 1) * 512],
                                    start=(kc == 0), stop=(kc == KC - 1))
                            ins.then_inc(s_mr, 1)

                    @blk.scalar
                    def _(e):
                        for ti in range(NT):
                            for kind in range(2):
                                ps = (pcq if kind == 0 else pck)[ti % 2]
                                sm = s_mq if kind == 0 else s_mk
                                so = s_q1 if kind == 0 else s_k1
                                acc = 16 + kind * 3 + ti % 2
                                sq = 22 + kind * 2 + ti % 2
                                e.wait_ge(sm, ti + 1)
                                e.activation(dump[:, 0:512], ps[:, 0:512],
                                             AF.Square,
                                             accum_out=stat[:, acc:acc + 1])
                                e.drain()
                                e.activation(stat[:, sq:sq + 1],
                                             stat[:, acc:acc + 1], AF.Sqrt,
                                             scale=1.0 / 512.0,
                                             bias=cst[:, 1:2]).then_inc(so, 1)

                    @blk.vector
                    def _(e):
                        def pb_evacs(tj):
                            for kind in range(2):
                                st = s_tq if kind == 0 else s_tk
                                dst3 = QB3 if kind == 0 else KV3
                                e.wait_ge(st, tj + 1)
                                e.tensor_copy(
                                    dst3[:, 0:4, tj * 128:(tj + 1) * 128],
                                    ptb[kind][:, :].rearrange(
                                        "p (a b) -> p a b", b=128)
                                ).then_inc(s_eb, 1)

                        e.wait_ge(s_gr, 1)
                        e.tensor_copy(gq_rep[:, :], pcq[0][:, 0:512])
                        e.tensor_copy(gkv_rep[:, :], pck[0][:, 0:512])
                        e.drain().then_inc(s_gv, 1)
                        for ti in range(NT):
                            for kind in range(2):
                                ps = (pcq if kind == 0 else pck)[ti % 2]
                                so = s_q1 if kind == 0 else s_k1
                                sv = s_vq if kind == 0 else s_vk
                                grep = gq_rep if kind == 0 else gkv_rep
                                dst = (qtm if kind == 0 else ktm)[ti % 2]
                                sq = 22 + kind * 2 + ti % 2
                                rc_ = 26 + kind * 2 + ti % 2
                                e.wait_ge(so, ti + 1)
                                e.reciprocal(stat[:, rc_:rc_ + 1],
                                             stat[:, sq:sq + 1])
                                e.drain()
                                e.tensor_scalar(tmq[:, :], ps[:, 0:512],
                                                stat[:, rc_:rc_ + 1], None,
                                                op0=OP.mult)
                                e.drain()
                                e.tensor_tensor(dst[:, :], tmq[:, :],
                                                grep[:, :],
                                                op=OP.mult).then_inc(sv, 1)
                            if ti > 0:
                                pb_evacs(ti - 1)
                        pb_evacs(NT - 1)
                        e.wait_ge(s_gd, 96)
                        for n2 in range(4):
                            e.wait_ge(s_mr, n2 + 1)
                            cs = slice(n2 * 512, (n2 + 1) * 512)
                            _rope(e, pkE[:, 0:512], pkO[:, 0:512],
                                  ekr[:, cs], okr[:, cs],
                                  [t[0:32, cs] for t in qtab],
                                  rt0[:, :], rt1[:, :], rt2[:, :], rt3[:, :])
                            e.drain()
                            e.tensor_scalar(ekr[:, cs], ekr[:, cs],
                                            scc[0:32, 2:3], None, op0=OP.mult)
                            e.tensor_scalar(okr[:, cs], okr[:, cs],
                                            scc[0:32, 2:3], None, op0=OP.mult)
                            e.drain().then_inc(s_er, 1)

                    @blk.gpsimd
                    def _(e):
                        e.wait_ge(s_er, 4)
                        e.dma_start(KR[0:32, :], ekr[:, :]).then_inc(s_kr, 16)
                        e.dma_start(KR[32:64, :], okr[:, :]).then_inc(s_kr, 16)
                        e.wait_ge(s_kr, 32)

            tap_block("qb", QBt[:, :])
            tap_block("kv", KVt[:, :])
            tap_block("kr", KR[:, :])

        s_gw = sem("c_gw"); s_qa = sem("c_qa")
        s_pk = sem("c_pk"); s_pr = sem("c_pr")
        s_pj = sem("c_pj"); s_pe = sem("c_pe")
        s_sc = sem("c_sc"); s_hd = sem("c_hd")
        s_mx = sem("c_mx"); s_ex = sem("c_ex")
        s_tp = sem("c_tp"); s_pte = sem("c_pte")
        s_pv = sem("c_pv"); s_oe = sem("c_oe")
        s_gdd = DSem(nc, "c_gdd", nway=4)
        s_gcc2 = [sem("c_gcc0"), sem("c_gcc1")]
        sched = list(_sched_c())

        def counters():
            return dict(pk=0, pr=0, pj=0, pe=0, sc=0, hd=0, mx=0, ex=0, tp=0,
                        pte=0, pv=0, oe=0, qa=0, gd=0, gc=0)

        with ExitStack() as _pcctx:
            def _sbC(n, sh, dt):
                return _pcctx.enter_context(nc.sbuf_tensor(n, sh, dt))
            def _psC(n, sh, dt=F32):
                return _pcctx.enter_context(nc.psum_tensor(n, sh, dt))
            wuq_g = _sbC("wuq_g", [128, 4 * 768], BF16)
            wuk_g = _sbC("wuk_g", [128, 4 * 512], BF16)
            wuv_g = _sbC("wuv_g", [128, 4 * 512], BF16)
            qn_sb = _sbC("qn_sb", [128, 2048], BF16)
            qr_sb = _sbC("qr_sb", [64, 2048], BF16)
            kn_sb = _sbC("kn_sb", [128, 2048], BF16)
            v_sb = _sbC("v_sb", [128, 2048], BF16)
            epr = _sbC("epr", [128, 2048], BF16)
            opr = _sbC("opr", [128, 2048], BF16)
            P_sb2 = [_sbC(f"P_sb{i}", [128, 2048], BF16)
                     for i in range(2)]
            PT_sb = _sbC("PT_sb", [128, 2048], BF16)
            rc = [_sbC(f"rc{i}", [128, 512], F32) for i in range(4)]
            gb = [_sbC(f"gb{i}", [128, 512], F32) for i in range(2)]
            gb = gb + [dump[:, 0:512], dump[:, 512:1024]]
            ps_pa = _psC("ps_pa", [128, 512])
            ps_pb = _psC("ps_pb", [128, 512])
            ps_sc = _psC("ps_sc", [128, 2048])
            ps_pt = _psC("ps_pt", [128, 1024], BF16)
            ps_o = _psC("ps_o", [128, 512])

            with nc.Block("pc") as blk:
                @blk.gpsimd
                def _(e):
                    n = counters()
                    for step in sched:
                        kind = step[0]
                        if kind == "wload":
                            g = step[1]
                            if g > 0:
                                e.wait_ge(s_sc, 32 * 4 * g)
                            e.dma_start(
                                wuq_g[:, :].rearrange("p (k j) -> p k j",
                                                      j=768),
                                wuq_hbm.ap()[:, g * 768:(g + 1) * 768]
                                .rearrange("(k p) j -> p k j", p=128)
                            ).then_inc(s_gw, 16)
                            e.dma_start(
                                wuk_g[:, :].rearrange("p (k j) -> p k j",
                                                      j=512),
                                wukn_hbm.ap()[:, g * 512:(g + 1) * 512]
                                .rearrange("(k p) j -> p k j", p=128)
                            ).then_inc(s_gw, 16)
                            e.dma_start(
                                wuv_g[:, :].rearrange("p (k j) -> p k j",
                                                      j=512),
                                wuv_hbm.ap()[:, g * 512:(g + 1) * 512]
                                .rearrange("(k p) j -> p k j", p=128)
                            ).then_inc(s_gw, 16)
                        elif kind == "pack":
                            n["pk"] += 1; n["pr"] += 1
                        elif kind == "qasm":
                            idx = step[1]
                            g, hl = divmod(idx, 4)
                            e.wait_ge(s_pr, 4 * g + 4)
                            if idx > 0:
                                e.wait_ge(s_sc, 32 * idx)
                            e.dma_start(qr_sb[0:32, :],
                                        epr[32 * hl:32 * hl + 32, :]
                                        ).then_inc(s_qa, 16)
                            e.dma_start(qr_sb[32:64, :],
                                        opr[32 * hl:32 * hl + 32, :]
                                        ).then_inc(s_qa, 16)
                            n["qa"] += 32
                        elif kind == "proj":
                            n["pj"] += 24; n["pe"] += 24
                        elif kind == "gam":
                            n["gc"] += 1
                        else:
                            idx, qi = step[1], step[2]
                            n["sc"] += 2; n["mx"] += 3; n["ex"] += 2
                            nb = len(_tp_batches(qi))
                            n["tp"] += nb; n["pte"] += nb
                            n["pv"] += 1; n["oe"] += 1

                @blk.sync
                def _(e):
                    n = counters()
                    for step in sched:
                        if step[0] == "gam":
                            c = step[1]
                            if c >= 4:
                                e.wait_ge(s_gcc2[c % 2], c // 2 - 1)
                            m, rb, co, wd = GAM[c]
                            s_gdd.i(e.dma_start(
                                gb[c % 4][:, 0:wd],
                                w_hbm[m].ap()[rb * 128:(rb + 1) * 128,
                                              co:co + wd]), c)
                            n["gc"] += 1

                @blk.tensor
                def _(e):
                    n = counters()
                    tpend = [None]

                    def flush_pend(e):
                        kp, qp = tpend[0]
                        tpend[0] = None
                        e.wait_ge(s_ex, 2 * (kp + 1))
                        for (b0, bl) in _tp_batches(qp):
                            if n["tp"] >= 1:
                                e.wait_ge(s_pte, n["tp"])
                            for k in range(bl):
                                ins = e.transpose(
                                    ps_pt[:, k * 128:(k + 1) * 128],
                                    P_sb2[kp % 2][:, (b0 + k) * 128:
                                                  (b0 + k + 1) * 128],
                                    idn_sb[:, :])
                            ins.then_inc(s_tp, 1)
                            n["tp"] += 1; n["pte"] += 1
                        e.wait_ge(s_pte, n["pte"])
                        if n["pv"] >= 1:
                            e.wait_ge(s_oe, n["pv"])
                        for kt in range(qp + 1):
                            ins = e.matmul(
                                ps_o[:, 0:128],
                                PT_sb[:, kt * 128:(kt + 1) * 128],
                                v_sb[:, kt * 128:(kt + 1) * 128],
                                start=(kt == 0), stop=(kt == qp))
                        ins.then_inc(s_pv, 1)
                        n["pv"] += 1; n["oe"] += 1

                    for step in sched:
                        kind = step[0]
                        if kind == "wload":
                            pass
                        elif kind == "pack":
                            g, n2 = step[1], step[2]
                            if tpend[0] is not None:
                                flush_pend(e)
                            if n2 == 0:
                                e.wait_ge(s_gw, 48 * (g + 1))
                                if g > 0:
                                    e.wait_ge(s_ex, 128 * g)
                            if n["pk"] >= 2:
                                e.wait_ge(s_pr, n["pk"] - 1)
                            eo = n2 % 2
                            for which in range(2):
                                dst = ps_sc[:, which * 1024 + eo * 512:
                                            which * 1024 + eo * 512 + 512]
                                for kc in range(4):
                                    ins = e.matmul(
                                        dst,
                                        wuq_g[:, kc * 768 + 512 + which * 128:
                                              kc * 768 + 640 + which * 128],
                                        QB3[:, kc, n2 * 512:(n2 + 1) * 512],
                                        start=(kc == 0), stop=(kc == 3))
                            ins.then_inc(s_pk, 1)
                            n["pk"] += 1; n["pr"] += 1
                        elif kind == "qasm":
                            n["qa"] += 32
                        elif kind == "proj":
                            idx = step[1]
                            g, hl = divmod(idx, 4)
                            if tpend[0] is not None:
                                flush_pend(e)
                            ch = 0
                            for part in range(3):
                                nchunks = 4 if part < 2 else 16
                                for j in range(nchunks):
                                    ps = (ps_pa, ps_pb)[ch % 2]
                                    if n["pj"] + ch >= 2:
                                        e.wait_ge(s_pe, n["pj"] + ch - 1)
                                    if part == 0:
                                        for kc in range(4):
                                            ins = e.matmul(
                                                ps[:, 0:512],
                                                wuq_g[:, kc * 768 + hl * 128:
                                                      kc * 768 + (hl + 1) * 128],
                                                QB3[:, kc,
                                                    j * 512:(j + 1) * 512],
                                                start=(kc == 0),
                                                stop=(kc == 3))
                                    elif part == 1:
                                        for kc in range(4):
                                            ins = e.matmul(
                                                ps[:, 0:512],
                                                wuk_g[:, kc * 512 + hl * 128:
                                                      kc * 512 + (hl + 1) * 128],
                                                KV3[:, kc,
                                                    j * 512:(j + 1) * 512],
                                                start=(kc == 0),
                                                stop=(kc == 3))
                                    else:
                                        for kc in range(4):
                                            ins = e.matmul(
                                                ps[:, 0:128],
                                                KV3[:, kc,
                                                    j * 128:(j + 1) * 128],
                                                wuv_g[:, kc * 512 + hl * 128:
                                                      kc * 512 + (hl + 1) * 128],
                                                start=(kc == 0),
                                                stop=(kc == 3))
                                    ins.then_inc(s_pj, 1)
                                    ch += 1
                            n["pj"] += 24; n["pe"] += 24
                        elif kind == "gam":
                            n["gc"] += 1
                        else:
                            idx, qi = step[1], step[2]
                            g = idx // 4
                            klen = 128 * (qi + 1)
                            k_g = 16 * idx + qi
                            off = 1024 * (qi % 2) if qi < 8 else 0
                            if qi == 0 and tpend[0] is not None:
                                flush_pend(e)
                            if qi == 0:
                                e.wait_ge(s_pe, 24 * (idx + 1))
                                e.wait_ge(s_qa, 32 * (idx + 1))
                                e.wait_ge(s_pr, 4 * (g + 1))
                            nck = (klen + 511) // 512
                            for ck in range(nck):
                                nn = min(512, klen - ck * 512)
                                if ck == 0:
                                    if qi >= 8:
                                        e.wait_ge(s_ex, 2 * k_g - 1)
                                    elif 1 <= qi <= 7:
                                        if k_g >= 2:
                                            e.wait_ge(s_ex, 2 * (k_g - 1))
                                    elif k_g >= 1:
                                        e.wait_ge(s_ex, 2 * k_g)
                                if ck == 2 and qi >= 8:
                                    e.wait_ge(s_ex, 2 * k_g)
                                e.matmul(ps_sc[:, off + ck * 512:
                                               off + ck * 512 + nn],
                                         qn_sb[:, qi * 128:(qi + 1) * 128],
                                         kn_sb[:, ck * 512:ck * 512 + nn],
                                         start=True, stop=False)
                            for ck in range(nck):
                                nn = min(512, klen - ck * 512)
                                diag = (ck == nck - 1)
                                ins = e.matmul(
                                    ps_sc[:, off + ck * 512:
                                          off + ck * 512 + nn],
                                    qr_sb[:, qi * 128:(qi + 1) * 128],
                                    KR[:, ck * 512:ck * 512 + nn],
                                    start=False, stop=not diag)
                                if ck == 1 and qi >= 8:
                                    ins.then_inc(s_sc, 1)
                            ins = e.matmul(
                                ps_sc[:, off + klen - 128:off + klen],
                                idn_sb[:, :], mask_sb[:, :],
                                start=False, stop=True)
                            ins.then_inc(s_sc, 1 if qi >= 8 else 2)
                            n["sc"] += 2; n["mx"] += 3; n["ex"] += 2
                            if tpend[0] is not None:
                                flush_pend(e)
                            tpend[0] = (k_g, qi)
                    if tpend[0] is not None:
                        flush_pend(e)

                @blk.vector
                def _(e):
                    n = counters()
                    vpend = [None]

                    def vflush(e):
                        kp, qp, ip = vpend[0]
                        vpend[0] = None
                        for (b0, bl) in _tp_batches(qp):
                            n["tp"] += 1
                            e.wait_ge(s_tp, n["tp"])
                            e.tensor_copy(PT_sb[:, b0 * 128:(b0 + bl) * 128],
                                          ps_pt[:, 0:bl * 128]
                                          ).then_inc(s_pte, 1)
                            n["pte"] += 1
                        e.wait_ge(s_ex, 2 * (kp + 1))
                        pp = kp % 2
                        if qp >= 8:
                            e.tensor_tensor(lcol[:, 6 + pp:7 + pp],
                                            lcol[:, pp:pp + 1],
                                            lcol[:, 4 + pp:5 + pp],
                                            op=OP.add)
                            e.drain()
                            e.reciprocal(lcol[:, 2 + pp:3 + pp],
                                         lcol[:, 6 + pp:7 + pp])
                        else:
                            e.reciprocal(lcol[:, 2 + pp:3 + pp],
                                         lcol[:, pp:pp + 1])
                        e.drain()
                        n["pv"] += 1
                        e.wait_ge(s_pv, n["pv"])
                        e.tensor_scalar(
                            BB[:, qp * 2048 + ip * 128:
                               qp * 2048 + (ip + 1) * 128],
                            ps_o[:, 0:128],
                            lcol[:, 2 + kp % 2:3 + kp % 2], None,
                            op0=OP.mult).then_inc(s_oe, 1)
                        n["oe"] += 1

                    for step in sched:
                        kind = step[0]
                        if kind == "wload":
                            pass
                        elif kind == "pack":
                            g, n2 = step[1], step[2]
                            if vpend[0] is not None:
                                vflush(e)
                            n["pk"] += 1
                            e.wait_ge(s_pk, n["pk"])
                            eo = n2 % 2
                            cs = slice(n2 * 512, (n2 + 1) * 512)
                            _rope(e, ps_sc[:, eo * 512:eo * 512 + 512],
                                  ps_sc[:, 1024 + eo * 512:1536 + eo * 512],
                                  epr[:, cs], opr[:, cs],
                                  [t[:, cs] for t in qtab],
                                  rc[0][:, :], rc[1][:, :], rc[2][:, :],
                                  rc[3][:, :])
                            e.drain().then_inc(s_pr, 1)
                            n["pr"] += 1
                        elif kind == "qasm":
                            n["qa"] += 32
                        elif kind == "proj":
                            idx = step[1]
                            if vpend[0] is not None:
                                vflush(e)
                            ch = 0
                            for part in range(3):
                                nchunks = 4 if part < 2 else 16
                                for j in range(nchunks):
                                    ps = (ps_pa, ps_pb)[ch % 2]
                                    e.wait_ge(s_pj, n["pj"] + ch + 1)
                                    if ch == 0 and idx > 0:
                                        e.wait_ge(s_sc, 32 * idx)
                                        e.wait_ge(s_oe, 16 * idx)
                                    if part == 0:
                                        ins = e.tensor_scalar(
                                            qn_sb[:, j * 512:(j + 1) * 512],
                                            ps[:, 0:512], scc[:, 0:1], None,
                                            op0=OP.mult)
                                    elif part == 1:
                                        ins = e.tensor_copy(
                                            kn_sb[:, j * 512:(j + 1) * 512],
                                            ps[:, 0:512])
                                    else:
                                        ins = e.tensor_copy(
                                            v_sb[:, j * 128:(j + 1) * 128],
                                            ps[:, 0:128])
                                    ins.then_inc(s_pe, 1)
                                    ch += 1
                            n["pj"] += 24; n["pe"] += 24
                        elif kind == "gam":
                            c = step[1]
                            if c % 2 == 0:
                                m, rb, co, wd = GAM[c]
                                s_gdd.w(e, c)
                                e.tensor_reduce(gpart[:, c:c + 1],
                                                gb[c % 4][:, 0:wd],
                                                axis=AX.X, op=OP.add,
                                                apply_absolute_value=True
                                                ).then_inc(s_gcc2[0], 1)
                            n["gc"] += 1
                        else:
                            idx, qi = step[1], step[2]
                            klen = 128 * (qi + 1)
                            k_g = 16 * idx + qi
                            off = 1024 * (qi % 2) if qi < 8 else 0
                            p = k_g % 2
                            if qi >= 8:
                                n["sc"] += 1
                                e.wait_ge(s_sc, n["sc"])
                                e.reduce_max(stat[:, 36 + 2 * p:37 + 2 * p],
                                             ps_sc[:, 0:1024], axis=AX.X,
                                             negate=True).then_inc(s_mx, 1)
                                n["sc"] += 1
                                e.wait_ge(s_sc, n["sc"])
                                e.reduce_max(stat[:, 37 + 2 * p:38 + 2 * p],
                                             ps_sc[:, 1024:klen], axis=AX.X,
                                             negate=True).then_inc(s_mx, 1)
                                n["mx"] += 2
                                e.wait_ge(s_mx, n["mx"])
                                e.tensor_tensor(stat[:, 42 + p:43 + p],
                                                stat[:, 36 + 2 * p:37 + 2 * p],
                                                stat[:, 37 + 2 * p:38 + 2 * p],
                                                op=OP.min).then_inc(s_mx, 1)
                                n["mx"] += 1
                            else:
                                n["sc"] += 2
                                e.wait_ge(s_sc, n["sc"])
                                e.reduce_max(stat[:, 42 + p:43 + p],
                                             ps_sc[:, off:off + klen],
                                             axis=AX.X,
                                             negate=True).then_inc(s_mx, 3)
                                n["mx"] += 3
                            n["ex"] += 2
                            if vpend[0] is not None:
                                vflush(e)
                            vpend[0] = (k_g, qi, idx)
                    if vpend[0] is not None:
                        vflush(e)

                @blk.scalar
                def _(e):
                    n = counters()
                    for step in sched:
                        kind = step[0]
                        if kind == "wload":
                            pass
                        elif kind == "pack":
                            n["pk"] += 1; n["pr"] += 1
                        elif kind == "qasm":
                            n["qa"] += 32
                        elif kind == "proj":
                            n["pj"] += 24; n["pe"] += 24
                        elif kind == "gam":
                            c = step[1]
                            if c % 2 == 1:
                                m, rb, co, wd = GAM[c]
                                half = (c // 2) % 2
                                s_gdd.w(e, c)
                                e.activation(dump[:, 1024 + 512 * half:
                                                  1024 + 512 * half + wd],
                                             gb[c % 4][:, 0:wd], AF.Abs,
                                             accum_out=gpart[:, c:c + 1]
                                             ).then_inc(s_gcc2[1], 1)
                            n["gc"] += 1
                        else:
                            idx, qi = step[1], step[2]
                            klen = 128 * (qi + 1)
                            k_g = 16 * idx + qi
                            off = 1024 * (qi % 2) if qi < 8 else 0
                            n["sc"] += 2; n["mx"] += 3
                            e.wait_ge(s_mx, n["mx"])
                            p = k_g % 2
                            if qi >= 8:
                                e.activation(P_sb2[p][:, 0:1024],
                                             ps_sc[:, 0:1024], AF.Exp,
                                             bias=stat[:, 42 + p:43 + p],
                                             accum_out=lcol[:, p:p + 1]
                                             ).then_inc(s_ex, 1)
                                e.activation(P_sb2[p][:, 1024:klen],
                                             ps_sc[:, 1024:klen], AF.Exp,
                                             bias=stat[:, 42 + p:43 + p],
                                             accum_out=lcol[:, 4 + p:5 + p]
                                             ).then_inc(s_ex, 1)
                            else:
                                e.activation(P_sb2[p][:, 0:klen],
                                             ps_sc[:, off:off + klen],
                                             AF.Exp,
                                             bias=stat[:, 42 + p:43 + p],
                                             accum_out=lcol[:, p:p + 1]
                                             ).then_inc(s_ex, 2)
                            n["ex"] += 2
                            n["pv"] += 1; n["oe"] += 1

        tap_block("otm", BB[:, :])

    pairs = [(a, b) for a in range(NT) for b in range(a, NT)]
    s_dt, s_de = sem("s_dt"), sem("s_de")
    with nc.psum_tensor("ps_dp0", [128, 256], BF16) as pdp0, \
         nc.psum_tensor("ps_dp1", [128, 256], BF16) as pdp1:
        pdp = [pdp0, pdp1]
        with nc.Block("d1") as blk:
            @blk.tensor
            def _(e):
                for p, (a, b) in enumerate(pairs):
                    if p >= 2:
                        e.wait_ge(s_de, p - 1)
                    ins = e.transpose(pdp[p % 2][:, 0:128],
                                      BB[:, a * 2048 + b * 128:
                                         a * 2048 + (b + 1) * 128],
                                      idn_sb[:, :])
                    if a != b:
                        ins = e.transpose(pdp[p % 2][:, 128:256],
                                          BB[:, b * 2048 + a * 128:
                                             b * 2048 + (a + 1) * 128],
                                          idn_sb[:, :])
                    ins.then_inc(s_dt, 1)

            @blk.vector
            def _(e):
                for p, (a, b) in enumerate(pairs):
                    e.wait_ge(s_dt, p + 1)
                    ins = e.tensor_copy(
                        BB[:, b * 2048 + a * 128:b * 2048 + (a + 1) * 128],
                        pdp[p % 2][:, 0:128])
                    if a != b:
                        e.drain()
                        ins = e.tensor_copy(
                            BB[:, a * 2048 + b * 128:a * 2048 + (b + 1) * 128],
                            pdp[p % 2][:, 128:256])
                    ins.then_inc(s_de, 1)

    s_wo, s_da, s_dv = sem("s_wo"), sem("s_da"), sem("s_dv")
    s_xo, s_xw = DSem(nc, "s_xo"), DSem(nc, "s_xw")
    with ExitStack() as _dctx:
        wo_sb = _dctx.enter_context(nc.sbuf_tensor("wo_sb", [128, KC * 2048], BF16))
        dxs = [_dctx.enter_context(nc.sbuf_tensor(f"dx{i}", [128, 2048], F32)) for i in range(2)]
        dns = [_dctx.enter_context(nc.sbuf_tensor(f"dn{i}", [128, 2048], F32)) for i in range(2)]
        psd = [_dctx.enter_context(nc.psum_tensor(f"ps_d{i}", [128, 2048], F32)) for i in range(2)]
        wo3 = wo_sb[:, :].rearrange("p (k j) -> p k j", j=2048)
        with nc.Block("d2a") as blk:
            @blk.gpsimd
            def _(e):
                for kc in range(KC):
                    e.dma_start(wo3[:, kc, :],
                                wo_hbm.ap()[kc * 128:(kc + 1) * 128, :]
                                ).then_inc(s_wo, 16)

            @blk.sync
            def _(e):
                for ti in range(NT):
                    if ti >= 2:
                        e.wait_ge(s_dv, ti - 1)
                        s_xw.i(e.dma_start(
                            xnew_hbm.ap()[(ti - 2) * 128:(ti - 1) * 128, :],
                            dns[ti % 2][:, :]), ti - 2)
                    s_xo.i(e.dma_start(dxs[ti % 2][:, :],
                                       x_hbm.ap()[ti * 128:(ti + 1) * 128, :]),
                           ti)
                for tj in (NT - 2, NT - 1):
                    e.wait_ge(s_dv, tj + 1)
                    s_xw.i(e.dma_start(
                        xnew_hbm.ap()[tj * 128:(tj + 1) * 128, :],
                        dns[tj % 2][:, :]), tj)
                s_xw.w(e, NT - 2)
                s_xw.w(e, NT - 1)

            @blk.tensor
            def _(e):
                e.wait_ge(s_wo, 16 * KC)
                for ti in range(NT):
                    if ti >= 2:
                        e.wait_ge(s_dv, ti - 1)
                    for n4 in range(4):
                        for kc in range(KC):
                            ins = e.matmul(
                                psd[ti % 2][:, n4 * 512:(n4 + 1) * 512],
                                BB[:, kc * 2048 + ti * 128:
                                   kc * 2048 + (ti + 1) * 128],
                                wo3[:, kc, n4 * 512:(n4 + 1) * 512],
                                start=(kc == 0), stop=(kc == KC - 1))
                    ins.then_inc(s_da, 1)

            @blk.vector
            def _(e):
                for ti in range(NT):
                    e.wait_ge(s_da, ti + 1)
                    s_xo.w(e, ti)
                    if ti >= 2:
                        s_xw.w(e, ti - 2)
                    for n4 in range(4):
                        if n4 == 3:
                            e.drain()
                        ins = e.tensor_tensor(
                            dns[ti % 2][:, n4 * 512:(n4 + 1) * 512],
                            psd[ti % 2][:, n4 * 512:(n4 + 1) * 512],
                            dxs[ti % 2][:, n4 * 512:(n4 + 1) * 512],
                            op=OP.add)
                    ins.then_inc(s_dv, 1)

    emit_ln("ln2", lambda ti: xnew_hbm.ap()[ti * 128:(ti + 1) * 128, :])
    tap_block("ln2", BB[:, :])

    s_g1, s_g2, s_g3, s_g4 = sem("s_g1"), sem("s_g2"), sem("s_g3"), sem("s_g4")
    with nc.psum_tensor("ps_gg", [1, 4], F32) as ps_gg, \
         nc.psum_tensor("ps_rep", [128, 4], F32) as ps_rep:
        with nc.Block("e0") as blk:
            @blk.vector
            def _(e):
                for m in range(3):
                    e.reduce_sum(stat[:, 30 + m:31 + m],
                                 gpart[:, GAM_OFFS[m]:GAM_OFFS[m + 1]],
                                 axis=AX.X)
                e.memset(tsc[0:1, 8:9], float(HD * D))
                e.drain()
                e.reciprocal(tsc[0:1, 9:10], tsc[0:1, 8:9])
                e.drain().then_inc(s_g1, 1)
                e.tensor_scalar(tsc[0:1, 10:11], tsc[0:1, 9:10], 0.5, None,
                                op0=OP.mult)
                e.wait_ge(s_g2, 1)
                e.drain()
                for m in range(3):
                    e.tensor_scalar(tsc[0:1, 11 + m:12 + m],
                                    ps_gg[0:1, m:m + 1],
                                    tsc[0:1, 10:11], 5e-9, op0=OP.mult,
                                    op1=OP.add)
                e.drain().then_inc(s_g3, 1)
                e.wait_ge(s_g4, 1)
                for m in range(3):
                    e.tensor_copy(tcol[:, m:m + 1], ps_rep[:, m:m + 1])
                    e.tensor_scalar(tcol[:, 3 + m:4 + m], ps_rep[:, m:m + 1],
                                    -1.0, None, op0=OP.mult)

            @blk.tensor
            def _(e):
                e.wait_ge(s_g1, 1)
                for m in range(3):
                    ins = e.matmul(ps_gg[0:1, m:m + 1],
                                   stat[:, 30 + m:31 + m], onesc[:, :],
                                   start=True, stop=True)
                ins.then_inc(s_g2, 1)
                e.wait_ge(s_g3, 1)
                for m in range(3):
                    ins = e.matmul(ps_rep[:, m:m + 1], onesr[:, :],
                                   tsc[0:1, 11 + m:12 + m], start=True,
                                   stop=True)
                ins.then_inc(s_g4, 1)

    e1_sems = {k: SemC(sem("e1_" + k)) for k in
               ("A", "B", "q2", "wt", "wte", "wtv", "x1", "x2", "sg2", "h")}
    e1_sems["wd"] = DSem(nc, "e1_wd", nway=4)
    e1_sems["spl"] = DSem(nc, "e1_spl", nway=4)
    e2_sems = {k: SemC(sem("e2_" + k)) for k in ("u3", "ov")}
    e2_sems["3d"] = e1_sems["wd"]
    e2_sems["3dg"] = e1_sems["spl"]
    e2_sems["xn"] = DSem(nc, "e2_xn")
    e2_sems["y"] = DSem(nc, "e2_y")
    e2_sems["h1"] = e1_sems["h"]
    with nc.sbuf_tensor("h_sb", [128, HC * 512], BF16) as h_sb:
        for sg in range(NSG):
            _emit_e1(nc, e1_sems, sg, BB3, h_sb, idn_sb, tcol, w_hbm, wt_hbm)
            if sg == 0:
                tap_block("h0", h_sb[:, :])
            _emit_e2(nc, e2_sems, sg, h_sb, wt_hbm[2], xnew_hbm, y_hbm,
                     final=(sg == NSG - 1))

    return nc, taps


def _emit_e1(nc, sems, sg, BB3, h_sb, idn_sb, tcol, w_hbm, wt_hbm):
    tag = f"e1s{sg}"
    s_wd = sems["wd"]; s_sgA = sems["A"]; s_tsB = sems["B"]
    s_q2 = sems["q2"]; s_wt = sems["wt"]; s_wte = sems["wte"]
    s_wtv = sems["wtv"]; s_spl = sems["spl"]
    s_x1 = sems["x1"]; s_x2 = sems["x2"]
    s_sg2 = sems["sg2"]; s_h = sems["h"]

    items = []
    for c in range(HC):
        items.append((0, c))
        items.append((1, c))
        if sg == 0:
            items.append((2, c))
    ni = len(items)

    ESP = 768
    loadref = {}
    spillref = {}
    qctr = {"S": 0, "G": 0}

    def xdone_wait(e, i):
        m, c = items[i]
        if m == 0:
            s_x1.w(e, c + 1)
        elif m == 1:
            s_x2.w(e, c + 1)

    with ExitStack() as _e1ctx:
        def _sbE(n, sh, dt):
            return _e1ctx.enter_context(nc.sbuf_tensor(tag + n, sh, dt))
        def _psE(n, sh, dt=F32):
            return _e1ctx.enter_context(nc.psum_tensor(tag + n, sh, dt))
        wsb = [_sbE(f"_w{i}", [128, 2048], F32) for i in range(4)]
        A_sb2 = [_sbE(f"_A{i}", [128, 2048], BF16) for i in range(2)]
        B_sb2 = [_sbE(f"_B{i}", [128, 2048], BF16) for i in range(2)]
        wq_sb2 = [_sbE(f"_wq{i}", [128, 2048], BF16) for i in range(2)]
        wqT = [_sbE(f"_t{i}", [128, 2048], BF16) for i in range(2)]
        sgb = [_sbE(f"_sg{i}", [128, 512], F32) for i in range(2)]
        px1 = [_psE(f"_px1{i}", [128, 512]) for i in range(2)]
        px2 = [_psE(f"_px2{i}", [128, 512]) for i in range(2)]
        pwf = [_psE(f"_pwf{i}", [128, 1024], F32) for i in range(2)]

        def emit_load(e, ds, q, i):
            m, c = items[i]
            if sg == 0:
                if i >= 4:
                    s_sgA.w(e, 2 * (i - 3))
                if m < 2:
                    ins = e.dma_start(
                        wsb[i % 4][:, :],
                        w_hbm[m].ap()[c * 128:(c + 1) * 128, :])
                else:
                    ins = e.dma_start(
                        wsb[i % 4][:, :].rearrange("p (a j) -> p a j", j=128),
                        w_hbm[2].ap()[:, c * 128:(c + 1) * 128]
                        .rearrange("(a p) j -> p a j", p=128))
            else:
                if i >= 4:
                    xdone_wait(e, i - 4)
                dstb = (wqT + A_sb2)[i % 4]
                ins = e.dma_start(dstb[:, :], wt_hbm[m].ap()[c, :, :])
            k = qctr[q]; qctr[q] += 1
            loadref[i] = (ds, k)
            ds.i(ins, k)

        def emit_spill(e, ds, q, i):
            m, c = items[i]
            s_wtv.w(e, 2 * (i + 1))
            ins = e.dma_start(wt_hbm[m].ap()[c, :, :], wqT[i % 2][:, :])
            k = qctr[q]; qctr[q] += 1
            spillref[i] = (ds, k)
            ds.i(ins, k)

        def wref(e, ref):
            ds, k = ref
            ds.w(e, k)

        nS = ni if sg == 0 else (ni + 1) // 2
        nG = ni if sg == 0 else ni // 2

        with nc.Block(tag) as blk:
            @blk.sync
            def _(e):
                for i in range(ni):
                    if i % 2 == 0:
                        emit_load(e, s_wd, "S", i)
                    if sg == 0 and i - 5 >= 0 and (i - 5) % 2 == 1:
                        emit_spill(e, s_wd, "S", i - 5)
                if sg == 0:
                    for j in range(ni - 5, ni):
                        if j % 2 == 1:
                            emit_spill(e, s_wd, "S", j)
                    for k in range(nS - 4, nS):
                        s_wd.w(e, k)
                    for k in range(nG - 4, nG):
                        s_spl.w(e, k)

            @blk.gpsimd
            def _(e):
                for i in range(ni):
                    if i % 2 == 1:
                        emit_load(e, s_spl, "G", i)
                    if sg == 0 and i - 5 >= 0 and (i - 5) % 2 == 0:
                        emit_spill(e, s_spl, "G", i - 5)
                if sg == 0:
                    for j in range(ni - 5, ni):
                        if j % 2 == 0:
                            emit_spill(e, s_spl, "G", j)
                    for k in range(nG - 4, nG):
                        s_spl.w(e, k)
                    for k in range(nS - 4, nS):
                        s_wd.w(e, k)

            @blk.scalar
            def _(e):
                def sigmoid(c):
                    s_x1.w(e, c + 1)
                    if c >= 2:
                        s_h.w(e, c - 1)
                    s_sg2.i(e.activation(sgb[c % 2][:, :],
                                         px1[c % 2][:, 0:512],
                                         AF.Sigmoid, scale=0.5), 1)

                for i, (m, c) in enumerate(items):
                    if sg == 0:
                        wref(e, loadref[i])
                        if i >= 2:
                            s_q2.w(e, i - 1)
                        s_sgA.i(e.activation(A_sb2[i % 2][:, :],
                                             wsb[i % 4][:, :], AF.Sign,
                                             bias=tcol[:, 3 + m:4 + m]), 1)
                        s_sgA.i(e.activation(B_sb2[i % 2][:, :],
                                             wsb[i % 4][:, :], AF.Sign,
                                             bias=tcol[:, m:m + 1]), 1)
                        if m == 0 and c >= 1:
                            sigmoid(c - 1)
                    elif m == 1:
                        sigmoid(c)
                if sg == 0:
                    sigmoid(HC - 1)

            @blk.vector
            def _(e):
                def evacv(j):
                    for h in range(2):
                        s_wt.w(e, 2 * j + h + 1)
                        if h == 0 and j >= 2:
                            wref(e, spillref[j - 2])
                            xdone_wait(e, j - 2)
                        s_wtv.i(e.tensor_copy(
                            wqT[j % 2][:, h * 1024:(h + 1) * 1024],
                            pwf[h][:, :]), 1)

                def hmul(c):
                    s_sg2.w(e, c + 1)
                    s_x2.w(e, c + 1)
                    s_h.i(e.tensor_tensor(h_sb[:, c * 512:(c + 1) * 512],
                                          sgb[c % 2][:, :],
                                          px2[c % 2][:, 0:512],
                                          op=OP.mult), 1)

                for i, (m, c) in enumerate(items):
                    if sg == 0:
                        s_sgA.w(e, 2 * (i + 1))
                        if i >= 2:
                            s_wt.w(e, 2 * (i - 1))
                        s_q2.i(e.tensor_tensor(wq_sb2[i % 2][:, :],
                                               A_sb2[i % 2][:, :],
                                               B_sb2[i % 2][:, :],
                                               op=OP.add), 1)
                        if i >= 1:
                            evacv(i - 1)
                        if m == 2:
                            hmul(c)
                    elif m == 1:
                        hmul(c)
                if sg == 0:
                    evacv(ni - 1)

            @blk.tensor
            def _(e):
                def do_mms(i):
                    m, c = items[i]
                    if m == 2:
                        return
                    if sg == 0:
                        s_wtv.w(e, 2 * (i + 1))
                        lhs = wqT[i % 2]
                    else:
                        wref(e, loadref[i])
                        lhs = (wqT + A_sb2)[i % 4]
                    if c >= 2:
                        s_h.w(e, c - 1)
                    dst = (px1 if m == 0 else px2)[c % 2]
                    for kc in range(KC):
                        ins = e.matmul(
                            dst[:, 0:512],
                            lhs[:, kc * 128:(kc + 1) * 128],
                            BB3[:, kc, sg * SG:(sg + 1) * SG],
                            start=(kc == 0), stop=(kc == KC - 1))
                    (s_x1 if m == 0 else s_x2).i(ins, 1)

                for i, (m, c) in enumerate(items):
                    if sg == 0:
                        s_q2.w(e, i + 1)
                        for h in range(2):
                            if i >= 1:
                                s_wtv.w(e, 2 * (i - 1) + h + 1)
                            for k in range(8):
                                ins = e.matmul(
                                    pwf[h][:, k * 128:(k + 1) * 128],
                                    wq_sb2[i % 2][:, (8 * h + k) * 128:
                                                  (8 * h + k + 1) * 128],
                                    idn_sb[:, :], start=True, stop=True)
                            s_wt.i(ins, 1)
                        if i >= 1:
                            do_mms(i - 1)
                    else:
                        do_mms(i)
                if sg == 0:
                    do_mms(ni - 1)

    s_wd.adv(qctr["S"]); s_spl.adv(qctr["G"])
    s_sgA.adv(2 * ni if sg == 0 else 0)
    s_q2.adv(ni if sg == 0 else 0)
    s_wt.adv(2 * ni if sg == 0 else 0)
    s_wtv.adv(2 * ni if sg == 0 else 0)
    for s in (s_x1, s_x2, s_sg2, s_h):
        s.adv(HC)


def _emit_e2(nc, sems, sg, h_sb, w3t_hbm, xnew_hbm, y_hbm, final):
    tag = f"e2s{sg}"
    s_3d = sems["3d"]; s_3dg = sems["3dg"]; s_u3 = sems["u3"]
    s_xn = sems["xn"]; s_ov = sems["ov"]; s_y = sems["y"]
    s_h1 = sems["h1"]
    hist_s = sems.setdefault("hist_s", [])
    hist_g = sems.setdefault("hist_g", [])

    with ExitStack() as _e2ctx:
        w3s = [_e2ctx.enter_context(nc.sbuf_tensor(f"{tag}_w3{i}",
                                                   [128, 1024], BF16))
               for i in range(8)]
        xns = [_e2ctx.enter_context(nc.sbuf_tensor(f"{tag}_xn{i}",
                                                   [128, 2048], F32))
               for i in range(4)]
        outs = [_e2ctx.enter_context(nc.sbuf_tensor(f"{tag}_o{i}",
                                                    [128, 2048], F32))
                for i in range(4)]
        pu = [_e2ctx.enter_context(nc.psum_tensor(f"{tag}_pu{i}",
                                                  [128, 512], F32))
              for i in range(8)]

        def buf(r, c):
            k = 22 * r + c // 2
            return 2 * (k % 4) + (c % 2)

        with nc.Block(tag) as blk:
            @blk.sync
            def _(e):
                s_ov.w(e, 0)
                for tl in range(4):
                    ti = sg * 4 + tl
                    if tl >= 2:
                        s_xn.w(e, tl - 2)
                    s_xn.i(e.dma_start(
                        xns[tl][:, :],
                        xnew_hbm.ap()[ti * 128:(ti + 1) * 128, :]), tl)
                for r in range(2):
                    for c in range(0, HC, 2):
                        n = len(hist_s)
                        if n >= 4:
                            e.wait_ge(s_u3.h, hist_s[n - 4] + 1)
                        hist_s.append(s_u3.base + r * HC + c)
                        s_3d.i(e.dma_start(
                            w3s[buf(r, c)][:, :],
                            w3t_hbm.ap()[c, :, r * 1024:(r + 1) * 1024]),
                            22 * r + c // 2)
                for r in range(2):
                    for tl in range(4):
                        ti = sg * 4 + tl
                        s_ov.w(e, r * 4 + tl + 1)
                        s_y.i(e.dma_start(
                            y_hbm.ap()[ti * 128:(ti + 1) * 128,
                                       r * 1024:(r + 1) * 1024],
                            outs[tl][:, r * 1024:(r + 1) * 1024]),
                            r * 4 + tl)
                s_y.w(e, 6)
                s_y.w(e, 7)

            @blk.gpsimd
            def _(e):
                for r in range(2):
                    for c in range(1, HC, 2):
                        n = len(hist_g)
                        if n >= 4:
                            e.wait_ge(s_u3.h, hist_g[n - 4] + 1)
                        hist_g.append(s_u3.base + r * HC + c)
                        s_3dg.i(e.dma_start(
                            w3s[buf(r, c)][:, :],
                            w3t_hbm.ap()[c, :, r * 1024:(r + 1) * 1024]),
                            22 * r + c // 2)

            @blk.tensor
            def _(e):
                for r in range(2):
                    if r > 0:
                        s_ov.w(e, 4)
                    for c in range(HC):
                        (s_3d if c % 2 == 0 else s_3dg).w(e, 22 * r + c // 2)
                        if r == 0:
                            s_h1.w(e, c + 1 - HC)
                        for tl in range(4):
                            for sub in range(2):
                                ins = e.matmul(
                                    pu[tl * 2 + sub][:, 0:512],
                                    h_sb[:, c * 512 + tl * 128:
                                         c * 512 + (tl + 1) * 128],
                                    w3s[buf(r, c)][:, sub * 512:
                                                   (sub + 1) * 512],
                                    start=(c == 0), stop=(c == HC - 1))
                        s_u3.i(ins, 1)

            @blk.vector
            def _(e):
                for r in range(2):
                    s_u3.w(e, (r + 1) * HC)
                    for tl in range(4):
                        if r == 0:
                            s_xn.w(e, tl)
                        for sub in range(2):
                            ins = e.scalar_tensor_tensor(
                                outs[tl][:, r * 1024 + sub * 512:
                                         r * 1024 + (sub + 1) * 512],
                                pu[tl * 2 + sub][:, 0:512], 0.25,
                                xns[tl][:, r * 1024 + sub * 512:
                                        r * 1024 + (sub + 1) * 512],
                                op0=OP.mult, op1=OP.add)
                        s_ov.i(ins, 1)

    s_3d.adv(2 * 22); s_3dg.adv(2 * 22); s_u3.adv(2 * HC)
    s_xn.adv(4); s_ov.adv(8); s_y.adv(8)


def _prep_consts():
    i32 = np.arange(32)
    f = np.power(10000.0, -i32.astype(np.float64) / 32.0)
    ang = np.outer(f, np.arange(S, dtype=np.float64))
    out = {}
    qscale = 1.0 / np.sqrt(np.float64(192.0))
    for nm, v in (("c", np.cos(ang)), ("s", np.sin(ang))):
        vq = v * qscale
        hi = vq.astype(np.float32)
        lo = (vq - hi.astype(np.float64)).astype(np.float32)
        out["q_" + nm + "h"] = np.ascontiguousarray(np.tile(hi, (4, 1)))
        out["q_" + nm + "l"] = np.ascontiguousarray(np.tile(lo, (4, 1)))
    ii, jj = np.meshgrid(np.arange(128), np.arange(128), indexing="ij")
    out["maskd"] = np.where(jj <= ii, 0.0, NEG_MASK).astype(ml_dtypes.bfloat16)
    out["idn"] = np.eye(128, dtype=ml_dtypes.bfloat16)
    return out


def _uq_perm():
    order = []
    for g in range(4):
        for hl in range(4):
            h = 4 * g + hl
            order += list(range(192 * h, 192 * h + 128))
        for hl in range(4):
            h = 4 * g + hl
            order += [192 * h + 128 + 2 * i for i in range(32)]
        for hl in range(4):
            h = 4 * g + hl
            order += [192 * h + 128 + 2 * i + 1 for i in range(32)]
    return np.array(order)


def kernel(L0_attn_x, L0_attn_w_dq, L0_attn_g_q, L0_attn_w_uq, L0_attn_w_dkv,
           L0_attn_g_kv, L0_attn_w_ukrope, L0_attn_w_uv, L0_attn_w_uknope,
           L0_attn_w_o, L0_ffn_w1, L0_ffn_w2, L0_ffn_w3):
    if "nc" not in _CACHE:
        _CACHE["nc"], _CACHE["taps"] = build(DEBUG_TAPS)
    nc = _CACHE["nc"]

    consts = _prep_consts()
    perm = _uq_perm()
    kperm = np.concatenate([np.arange(0, 64, 2), np.arange(1, 64, 2)])
    common = {
        "w_dq": np.ascontiguousarray(np.asarray(L0_attn_w_dq, np.float32)),
        "g_q": np.ascontiguousarray(np.asarray(L0_attn_g_q, np.float32)
                                    .reshape(1, DQ)),
        "w_uq_p": np.ascontiguousarray(
            np.asarray(L0_attn_w_uq, np.float32)[:, perm]),
        "w_dkv": np.ascontiguousarray(np.asarray(L0_attn_w_dkv, np.float32)),
        "g_kv": np.ascontiguousarray(np.asarray(L0_attn_g_kv, np.float32)
                                     .reshape(1, DKV)),
        "w_ukr_p": np.ascontiguousarray(
            np.asarray(L0_attn_w_ukrope, np.float32)[:, kperm]),
        "w_uv": np.ascontiguousarray(np.asarray(L0_attn_w_uv, np.float32)),
        "w_ukn": np.ascontiguousarray(np.asarray(L0_attn_w_uknope,
                                                 np.float32)),
        "w_o": np.ascontiguousarray(np.asarray(L0_attn_w_o, np.float32)),
        "w1": np.ascontiguousarray(np.asarray(L0_ffn_w1, np.float32)),
        "w2": np.ascontiguousarray(np.asarray(L0_ffn_w2, np.float32)),
        "w3": np.ascontiguousarray(np.asarray(L0_ffn_w3, np.float32)),
        "idn": consts["idn"], "maskd": consts["maskd"],
        "q_ch": consts["q_ch"], "q_cl": consts["q_cl"],
        "q_sh": consts["q_sh"], "q_sl": consts["q_sl"],
    }
    x = np.asarray(L0_attn_x, np.float32)
    in_maps = []
    for b in range(N_CORES):
        m = dict(common)
        m["x"] = np.ascontiguousarray(x[b])
        in_maps.append(m)

    res = run_bass_kernel_spmd(nc, in_maps, core_ids=list(range(N_CORES)))
    return np.stack([np.asarray(r["y"], np.float32) for r in res.results])
